# version 10
# baseline (speedup 1.0000x reference)
import numpy as np
import ml_dtypes
from contextlib import ExitStack

import concourse.bass as bass
import concourse.mybir as mybir
from concourse.bass_utils import run_bass_kernel_spmd

F32 = mybir.dt.float32
BF16 = mybir.dt.bfloat16
U32 = mybir.dt.uint32
AF = mybir.ActivationFunctionType
ALU = mybir.AluOpType
AX = mybir.AxisListType
bf16 = ml_dtypes.bfloat16

NCORE = 8
D = 1024
E, DFF = 8, 4096
CAP = 2560
TPC = 2048
NT = 16
KB = 1024
EPS_ATTN = 1e-6
EPS_LN = 1e-5


class S:
    def __init__(self, ctx, nc, name):
        self.h = ctx.enter_context(nc.semaphore(name))
        self.n = 0

    def inc(self, inst, k=1):
        inst.then_inc(self.h, k)
        self.n += k
        return self.n


def build(upto=6):
    nc = bass.Bass("TRN2", target_bir_lowering=False)

    x_in = nc.dram_tensor("x", [TPC, D], F32, kind="ExternalInput").ap()
    wqT_in = nc.dram_tensor("wqT", [D, D], F32, kind="ExternalInput").ap()
    wkT_in = nc.dram_tensor("wkT", [D, D], F32, kind="ExternalInput").ap()
    wvT_in = nc.dram_tensor("wvT", [D, D], F32, kind="ExternalInput").ap()
    woT_in = nc.dram_tensor("woT", [D, D], F32, kind="ExternalInput").ap()
    wo2_in = nc.dram_tensor("wo2", [D, D], F32, kind="ExternalInput").ap()
    wr_in = nc.dram_tensor("wr", [D, E], F32, kind="ExternalInput").ap()
    w1_in = nc.dram_tensor("w1c", [D, DFF], F32, kind="ExternalInput").ap()
    w2_in = nc.dram_tensor("w2c", [DFF, D], F32, kind="ExternalInput").ap()
    cmask_in = nc.dram_tensor("cmask", [8, 1], F32, kind="ExternalInput").ap()
    cvals_in = nc.dram_tensor("cvals", [128, 2], F32, kind="ExternalInput").ap()

    y_out = nc.dram_tensor("y", [TPC, D], F32, kind="ExternalOutput").ap()

    ident_bf = nc.inline_tensor(np.eye(128, dtype=bf16), name="ident_bf").ap()
    ident_f = nc.inline_tensor(np.eye(128, dtype=np.float32), name="ident_f").ap()
    jj = np.arange(128)
    Lstrict_c = nc.inline_tensor((jj[:, None] < jj[None, :]).astype(bf16), name="Lstrict").ap()
    gidx_, eidx_ = np.arange(128) // 8, np.arange(128) % 8
    Pref_c = nc.inline_tensor(((eidx_[:, None] == eidx_[None, :]) &
                               (gidx_[:, None] < gidx_[None, :])).astype(bf16), name="Pref").ap()
    SumG_c = nc.inline_tensor((eidx_[:, None] == np.arange(8)[None, :]).astype(bf16), name="SumG").ap()
    TileE_c = nc.inline_tensor((np.arange(8)[:, None] == eidx_[None, :]).astype(np.float32),
                               name="TileE").ap()
    selp = np.zeros((16, 8 * 128), dtype=bf16)
    for j in range(8):
        selp[2 * j, j * 128:j * 128 + 64] = 1
        selp[2 * j + 1, j * 128 + 64:(j + 1) * 128] = 1
    SEL_c = nc.inline_tensor(selp, name="SELpack").ap()
    iota8_c = nc.inline_tensor(np.tile(np.arange(8, dtype=np.float32), (128, 1)), name="iota8").ap()
    _g = np.arange(16)[None, :]
    _p = np.arange(128)[:, None]
    tokid1_c = nc.inline_tensor(((_g // 4) * 4096 + (_g % 4) * 128 + _p + 1
                                 ).astype(np.float32), name="tokid1").ap()
    onescol_c = nc.inline_tensor(np.ones((128, 1), dtype=bf16), name="onescol").ap()

    y1_dram = nc.dram_tensor("y1_dram", [TPC, D], F32)
    t_loc = nc.dram_tensor("t_loc", [TPC, D], BF16)
    t_full = nc.dram_tensor("t_full", [NCORE * TPC, D], BF16, addr_space="Shared")
    eout_loc = nc.dram_tensor("eout_loc", [CAP, D], BF16)
    eout_full = nc.dram_tensor("eout_full", [NCORE * CAP, D], BF16, addr_space="Shared")
    kv_red_i = nc.dram_tensor("kv_red_i", [129, D], F32)
    kv_red_o = nc.dram_tensor("kv_red_o", [129, D], F32)
    cnt_i = nc.dram_tensor("cnt_i", [8, 1], F32)
    cnt_o = nc.dram_tensor("cnt_o", [64, 1], F32)
    id_disp = nc.dram_tensor("id_disp", [E * CAP], F32)
    id_full = nc.dram_tensor("id_full", [E * CAP], F32, addr_space="Shared")
    brow_d = nc.dram_tensor("brow_d", [1, 128], F32)

    ctx = ExitStack()
    EC = ctx.enter_context

    def sb(name, shape, dtype=F32):
        return EC(nc.sbuf_tensor(name, shape, dtype)).ap()

    sb_ident = sb("c_ident", [128, 128], BF16)
    sb_identf = sb("c_identf", [128, 128], F32)
    sb_L = sb("c_L", [128, 128], BF16)
    sb_pref = sb("c_pref", [128, 128], BF16)
    sb_sumg = sb("c_sumg", [128, 8], BF16)
    sb_tileE = sb("c_tileE", [8, 128], F32)
    sb_sel = sb("c_sel", [16, 1024], BF16)
    sb_iota8 = sb("c_iota8", [128, 8], F32)
    sb_tokid1 = sb("c_tokid1", [128, 16], F32)
    sb_ones_c = sb("c_onesc", [128, 1], BF16)
    sb_cmask = sb("c_cmask", [8, 1], F32)
    sb_cvals = sb("c_cvals", [128, 2], F32)
    sb_wr = sb("wr_sb", [128, 8, 8], BF16)

    x_rot = [sb(f"x{i}", [128, D], F32) for i in range(2)]
    xn_rot = [sb(f"xn{i}", [128, D], BF16) for i in range(2)]
    y1_rot = [sb(f"y1r{i}", [128, D], F32) for i in range(2)]
    t_rot = [sb(f"trt{i}", [128, D], BF16) for i in range(2)]
    sb_elu = [sb(f"eluscr{i}", [128, 512], F32) for i in range(2)]

    def cols16(name):
        return sb(name, [128, 16], F32)
    ln1 = {k: cols16("ln1" + k) for k in ("sum", "ssq", "m", "msq", "var", "lnl", "rstd", "std")}
    ln2 = {k: cols16("ln2" + k) for k in ("sum", "ssq", "m", "msq", "var", "lnl", "rstd")}
    sb_WW = sb("WW", [128, 8, 8], BF16)
    sb_Sbc = sb("Sbc", [128, 8], F32)
    sb_c1 = cols16("lgc1")
    sb_c2 = cols16("lgc2")

    sb_KS = sb("KS", [128, 8, 16], BF16)
    sb_kspad = sb("kspad", [128, 128], F32)

    sb_lgt = sb("lgt", [128, 128], F32)
    sb_eexp = sb("eexp", [128, 8], F32)
    sb_probs = sb("probs", [128, 8], F32)
    sb_top8 = sb("top8", [128, 8], F32)
    sb_idx8 = sb("idx8", [128, 8], U32)
    sb_nmx = cols16("nmx")
    sb_esum = cols16("esum")
    sb_ers = cols16("ers")
    sb_pm = cols16("pm")
    sb_idxf = cols16("idxf")
    sb_ohb = sb("ohb", [128, 128], BF16)
    sb_ohf = sb("ohf", [128, 128], F32)
    sb_totr = sb("totr", [1, 128], F32)
    sb_totp = sb("totp", [128, 128], F32)
    sb_totc = sb("totc", [128, 1], BF16)
    sb_cnt8 = sb("cnt8", [8, 1], F32)
    sb_cnts = sb("cnts", [8, 8], F32)
    sb_bt = sb("bt", [128, 1], F32)
    sb_bc2 = sb("bc2", [128, 1], F32)
    sb_bcol = sb("bcol", [128, 1], F32)
    sb_bpad = sb("bpad", [128, 128], F32)
    sb_brow = sb("brow_s", [1, 128], F32)
    sb_bbc = sb("bbc", [128, 128], F32)
    sb_posf = sb("posf", [128, 128], F32)
    sb_prod = sb("prod", [128, 128], F32)
    sb_psel = cols16("psel")
    sb_kept = cols16("kept")
    sb_plo = cols16("plo")
    sb_phi = cols16("phi")
    sb_posc = cols16("posc")
    sb_flat = cols16("flat")
    sb_flatu = sb("flatu", [128, 16], U32)
    sb_afac = cols16("afac")
    sb_bfac = cols16("bfac")
    sb_gid1 = cols16("gid1")
    sb_scf = cols16("scf")
    sb_scf2 = cols16("scf2")
    sb_scu = sb("scu", [128, 16], U32)
    sb_eps = sb("epsc", [128, 1], F32)

    sb_ids20 = sb("ids20", [20, 128], F32)
    sb_idx20 = sb("idx20", [20, 1], F32)
    sb_idx20u = sb("idx20u", [20, 1], U32)
    sb_idm1 = sb("idm1", [128, 20], F32)
    sb_gidx = sb("gidx_sb", [128, 20], U32)


    MTOT = 166 * KB
    MBASE = nc.sbuf_top - MTOT
    rem = nc.sbuf_bytes_remaining
    assert rem >= MTOT + 512, f"left allocations too big: remaining={rem}, need {MTOT}"

    def at(name, shape, dtype, rel):
        return nc.alloc_sbuf_tensor_at(name, shape, dtype, offset=MBASE + rel).ap()

    xnT = at("xnT", [128, 8, 1024], BF16, 0)
    wq_sb = at("wq_sb", [128, 8, 1024], BF16, 16 * KB)
    wk_sb = at("wk_sb", [128, 8, 1024], BF16, 32 * KB)
    wv_sb = at("wv_sb", [128, 8, 1024], BF16, 48 * KB)
    q_sb = at("q_sb", [128, 8, 2048], BF16, 64 * KB)
    ai_sb = at("ai_sb", [128, 8, 2048], BF16, 64 * KB)
    k_sb = at("k_sb", [128, 16, 1024], BF16, 96 * KB)
    v_sb = at("v_sb", [128, 16, 1024], BF16, 128 * KB)
    kv_sb = at("kv_sb", [128, 1024], F32, 160 * KB)
    h_sb = at("h_sb", [128, 32, 256], BF16, 0)
    w1_sb = at("w1_sb", [128, 6, 4096], BF16, 16 * KB)
    w1t_sb = at("w1t_sb", [128, 2, 4096], BF16, 96 * KB)
    wo_sb = at("wo_sb", [128, 8, 1024], BF16, 112 * KB)
    w2a_sb = at("w2a_sb", [128, 16, 1024], BF16, 64 * KB)
    w2b_sb = at("w2b_sb", [128, 8, 1024], BF16, 112 * KB)
    w2c_sb = at("w2c_sb", [128, 8, 1024], BF16, 128 * KB)
    tTe_sb = at("tTe_sb", [128, 8, 256], BF16, 144 * KB)
    gath_sb = [at(f"gathA{i}", [128, 1024], BF16, 148 * KB + i * 2 * KB) for i in range(2)]
    eo_sb = [at(f"eoA{i}", [128, 1024], BF16, 152 * KB + i * 2 * KB) for i in range(2)]
    eg_sb = [at(f"egA{i}", [128, 1024], BF16, 156 * KB + i * 2 * KB) for i in range(2)]
    sb_junk = at("junkA", [128, 1024], F32, 156 * KB)
    sb_ksrow = at("ksrowA", [1, 1024], F32, 0)
    sb_den = at("denA", [16, 512], F32, 4 * KB)
    sb_recb = at("recbA", [16, 512], BF16, 6 * KB)
    sb_bcs = at("bcsA", [128, 512], F32, 7 * KB)
    sb_lgev = at("lgevA", [8, 512], F32, 148 * KB)
    sb_kv2 = at("kv2A", [128, 8, 128], BF16, 9 * KB)
    sb_tts = at("ttsA", [128, 2, 512], BF16, 11 * KB)

    def w2_ap(fj):
        if fj < 16:
            return w2a_sb[:, fj, :]
        if fj < 24:
            return w2b_sb[:, fj - 16, :]
        return w2c_sb[:, fj - 24, :]

    def w1_ap(kj, f0, f1):
        if kj < 6:
            return w1_sb[:, kj, f0:f1]
        return w1t_sb[:, kj - 6, f0:f1]

    hc = S(ctx, nc, "hc")
    dw = S(ctx, nc, "dw")
    dw1 = S(ctx, nc, "dw1")
    dwo = S(ctx, nc, "dwo")
    dw2 = S(ctx, nc, "dw2")
    dkv = S(ctx, nc, "dkv")
    dcn = S(ctx, nc, "dcn")
    dsid = S(ctx, nc, "dsid")
    did = S(ctx, nc, "did")
    dbr = S(ctx, nc, "dbr")
    deo = S(ctx, nc, "deo")
    dts2 = [S(ctx, nc, f"dts{i}") for i in range(2)]
    dww2 = [S(ctx, nc, f"dww{i}") for i in range(2)]
    dwr = S(ctx, nc, "dwr")
    dwv = S(ctx, nc, "dwv")
    pcb = S(ctx, nc, "pcb")
    dtg2 = [S(ctx, nc, f"dtg{i}") for i in range(2)]
    deg2 = [S(ctx, nc, f"deg{i}") for i in range(2)]
    dx2 = [S(ctx, nc, f"dx{i}") for i in range(2)]
    dy2 = [S(ctx, nc, f"dy{i}") for i in range(2)]
    dt2 = [S(ctx, nc, f"dt{i}") for i in range(2)]
    dhl2 = [S(ctx, nc, f"dhl{i}") for i in range(2)]
    osy2 = [S(ctx, nc, f"osy{i}") for i in range(2)]
    pe = S(ctx, nc, "pe")
    dv = S(ctx, nc, "dv")
    ac = S(ctx, nc, "ac")
    po = S(ctx, nc, "po")
    c_kv = S(ctx, nc, "c_kv")
    c_cnt = S(ctx, nc, "c_cnt")
    c_id = S(ctx, nc, "c_id")
    c_tc = [S(ctx, nc, f"c_tc{i}") for i in range(4)]
    c_ec = [S(ctx, nc, f"c_ec{i}") for i in range(5)]

    ALLSEMS = [pe, dv, ac, po, hc, dw, dw1, dwo, dw2, dkv, dcn, dsid, did, dbr,
               deo, dwr, dwv, pcb] + dww2 + dts2 + dtg2 + deg2 + dx2 + dy2 + dt2 + dhl2 + osy2
    ALLCOLL = [c_kv, c_cnt, c_id] + c_tc + c_ec

    block = EC(nc.Block())
    M = {}

    def finish_early():
        def flush(g):
            for s_ in ALLSEMS:
                if s_.n:
                    g.wait_ge(s_.h, s_.n)
            for s_ in ALLCOLL:
                if s_.n:
                    g.wait_ge(s_.h, 1)
        G(flush)
        return nc

    def T(f):
        block.tensor(f)

    def V(f):
        block.vector(f)

    def SC(f):
        block.scalar(f)

    def SY(f):
        block.sync(f)

    def G(f):
        block.gpsimd(f)

    def phA_eps(v):
        dv.inc(v.memset(sb_eps, EPS_LN))
    V(phA_eps)

    def phA_sync(s):
        for csrc, cdst in [(ident_bf, sb_ident), (ident_f, sb_identf), (Lstrict_c, sb_L),
                           (Pref_c, sb_pref), (SumG_c, sb_sumg), (TileE_c, sb_tileE),
                           (SEL_c, sb_sel), (iota8_c, sb_iota8), (tokid1_c, sb_tokid1),
                           (onescol_c, sb_ones_c), (cmask_in, sb_cmask), (cvals_in, sb_cvals)]:
            hc.inc(s.dma_start(out=cdst, in_=csrc), 16)
        M["consts"] = hc.n
    SY(phA_sync)

    def phA_g(g):
        for kj in range(8):
            dwr.inc(g.dma_start(out=sb_wr[:, kj, :], in_=wr_in[kj * 128:(kj + 1) * 128, :]), 16)
        M["wrdone"] = dwr.n
        po.inc(g.memset(sb_bpad, 0.0))
        g.wait_ge(po.h, po.n)
        dw.inc(g.dma_start(out=id_disp.ap().rearrange("(p t) -> p t", p=128)[:, 0:128],
                           in_=sb_bpad), 16)
        dw.inc(g.dma_start(out=id_disp.ap().rearrange("(p t) -> p t", p=128)[:, 128:160],
                           in_=sb_bpad[:, 0:32]), 16)
        M["idz"] = dw.n
        for kj in range(8):
            dw.inc(g.dma_start(out=wk_sb[:, kj, :], in_=wkT_in[kj * 128:(kj + 1) * 128, :]), 16)
        M["wkv"] = dw.n
        for kj in range(8):
            dwv.inc(g.dma_start(out=wv_sb[:, kj, :], in_=wvT_in[kj * 128:(kj + 1) * 128, :]), 16)
        M["wv"] = dwv.n
        for kj in range(8):
            dw.inc(g.dma_start(out=wq_sb[:, kj, :], in_=wqT_in[kj * 128:(kj + 1) * 128, :]), 16)
        M["wqkv"] = dw.n
    G(phA_g)

    wwtmp = [q_sb[:, i, 0:1024] for i in range(8)]

    def phA_wo2(g):
        for kj in range(8):
            dww2[0].inc(g.dma_start(out=wwtmp[kj],
                                    in_=wo2_in[kj * 128:(kj + 1) * 128, :]), 16)
    G(phA_wo2)

    def emit_ww():
        ctxA = ExitStack()
        psWW = ctxA.enter_context(nc.psum_tensor("psWW", [128, 128], F32))

        def phA_ww(t):
            t.wait_ge(dww2[0].h, 128)
            t.wait_ge(dv.h, M[("halfVdone", 0)])
            for mc in range(8):
                for kj in range(8):
                    t.matmul(psWW.ap()[:, mc * 8:(mc + 1) * 8],
                             wwtmp[kj][:, mc * 128:(mc + 1) * 128],
                             sb_wr[:, kj, :], start=(kj == 0), stop=(kj == 7))
            last = None
            for kj in range(8):
                last = t.matmul(psWW.ap()[0:1, 64:72], sb_ones_c, sb_wr[:, kj, :],
                                start=(kj == 0), stop=(kj == 7))
            pe.inc(last)
            M["WWmm"] = pe.n
        T(phA_ww)

        def phA_wwev(v):
            v.wait_ge(pe.h, M["WWmm"])
            dv.inc(v.tensor_copy(sb_WW.rearrange("p a b -> p (a b)"), psWW.ap()[:, 0:64]))
            dv.inc(v.tensor_copy(sb_brow[0:1, 0:8], psWW.ap()[0:1, 64:72]))
            M["WWev"] = dv.n
        V(phA_wwev)

        def phA_sbc(g):
            g.wait_ge(dv.h, M["WWev"])
            dbr.inc(g.dma_start(out=brow_d.ap()[0:1, 0:8], in_=sb_brow[0:1, 0:8]), 16)
            g.wait_ge(dbr.h, dbr.n)
            M["Sbc"] = dbr.inc(g.dma_start(
                out=sb_Sbc, in_=brow_d.ap()[0:1, 0:8].partition_broadcast(128)), 16)
        G(phA_sbc)
        ctxA.close()

    x_loaded = [0] * NT
    ln_applied = [0] * NT
    eluk = [y1_rot[0][:, 0:512], y1_rot[0][:, 512:1024]]

    for half in range(2):
        tiles = list(range(half * 8, half * 8 + 8))

        ctxB = ExitStack()
        psB = [ctxB.enter_context(nc.psum_tensor(f"psB{half}_{i}", [128, 128], BF16))
               for i in range(2)]
        psLgX = ctxB.enter_context(nc.psum_tensor(f"psLgX{half}", [128, 8], F32))
        psK2 = [ctxB.enter_context(nc.psum_tensor(f"psK{half}_{i}", [128, 512], F32))
                for i in range(2)]
        psV2 = [ctxB.enter_context(nc.psum_tensor(f"psV{half}_{i}", [128, 512], F32))
                for i in range(2)]

        def emit_ldx_stats(g_):
            def ld_x(s, g_=g_):
                if g_ >= 2:
                    s.wait_ge(dv.h, ln_applied[g_ - 2])
                x_loaded[g_] = dx2[g_ % 2].inc(s.dma_start(out=x_rot[g_ % 2],
                                                  in_=x_in[g_ * 128:(g_ + 1) * 128, :]), 16)
            SY(ld_x)

            def ln_stats(v, g_=g_):
                xa = x_rot[g_ % 2]
                v.wait_ge(dx2[g_ % 2].h, x_loaded[g_])
                dv.inc(v.reduce_sum(ln1["sum"][:, g_:g_ + 1], xa, axis=AX.X))
                n1 = dv.inc(v.scalar_tensor_tensor(sb_junk, xa, 0.0, xa, ALU.add, ALU.mult,
                                                   accum_out=ln1["ssq"][:, g_:g_ + 1]))
                v.wait_ge(dv.h, n1)
                n2 = dv.inc(v.tensor_scalar(ln1["m"][:, g_:g_ + 1], ln1["sum"][:, g_:g_ + 1],
                                            1.0 / D, None, ALU.mult))
                v.wait_ge(dv.h, n2)
                n3 = dv.inc(v.tensor_mul(ln1["msq"][:, g_:g_ + 1], ln1["m"][:, g_:g_ + 1],
                                         ln1["m"][:, g_:g_ + 1]))
                v.wait_ge(dv.h, n3)
                M[("var1", g_)] = dv.inc(v.scalar_tensor_tensor(
                    ln1["var"][:, g_:g_ + 1], ln1["ssq"][:, g_:g_ + 1], 1.0 / D,
                    ln1["msq"][:, g_:g_ + 1], ALU.mult, ALU.subtract))
            V(ln_stats)

        emit_ldx_stats(tiles[0])
        for g_ in tiles:
            gl = g_ - half * 8
            if g_ + 1 <= tiles[-1]:
                emit_ldx_stats(g_ + 1)

            def sc_tile(sc, g_=g_):
                sc.wait_ge(dv.h, M[("var1", g_)])
                a1 = ac.inc(sc.activation(ln1["lnl"][:, g_:g_ + 1], ln1["var"][:, g_:g_ + 1],
                                          AF.Ln, bias=sb_eps[:, 0:1], scale=1.0))
                sc.wait_ge(ac.h, a1)
                ac.inc(sc.activation(ln1["std"][:, g_:g_ + 1], ln1["lnl"][:, g_:g_ + 1],
                                     AF.Exp, bias=0.0, scale=0.5))
                M[("rstd1", g_)] = ac.inc(sc.activation(ln1["rstd"][:, g_:g_ + 1],
                                                        ln1["lnl"][:, g_:g_ + 1],
                                                        AF.Exp, bias=0.0, scale=-0.5))
            SC(sc_tile)

            def ln_apply(v, g_=g_):
                v.wait_ge(ac.h, M[("rstd1", g_)])
                if ("trlast", g_ - 2) in M:
                    v.wait_ge(pe.h, M[("trlast", g_ - 2)])
                ln_applied[g_] = dv.inc(v.tensor_scalar(
                    xn_rot[g_ % 2], x_rot[g_ % 2], ln1["m"][:, g_:g_ + 1],
                    ln1["rstd"][:, g_:g_ + 1], ALU.subtract, ALU.mult))
                M[("xn1", g_)] = ln_applied[g_]
            V(ln_apply)

            for j in range(8):
                def tr_xn(t, g_=g_, j=j):
                    if g_ == 0 and j == 0:
                        t.wait_ge(hc.h, M["consts"])
                    if g_ == 8 and j == 0:
                        t.wait_ge(dv.h, M["half0_alldv"])
                    if j == 0:
                        t.wait_ge(dv.h, M[("xn1", g_)])
                    prev = ("trev", g_, j - 2) if j >= 2 else ("trev", g_ - 1, 6 + j)
                    if prev in M:
                        t.wait_ge(dv.h, M[prev])
                    M[("tr", g_, j)] = pe.inc(t.transpose(
                        psB[j % 2].ap(), xn_rot[g_ % 2][:, j * 128:(j + 1) * 128],
                        sb_ident))
                T(tr_xn)

                def tr_ev(v, g_=g_, gl=gl, j=j):
                    v.wait_ge(pe.h, M[("tr", g_, j)])
                    M[("trev", g_, j)] = dv.inc(v.tensor_copy(
                        xnT[:, j, gl * 128:(gl + 1) * 128], psB[j % 2].ap()))
                V(tr_ev)
            M[("trlast", g_)] = M[("tr", g_, 7)]

            def b_lga(t, g_=g_, gl=gl):
                if g_ == 0:
                    t.wait_ge(dwr.h, M["wrdone"])
                t.wait_ge(dv.h, M[("trev", g_, 7)])
                if ("lgaev", g_ - 1) in M:
                    t.wait_ge(dv.h, M[("lgaev", g_ - 1)])
                last = None
                for kj in range(8):
                    last = t.matmul(psLgX.ap(), xnT[:, kj, gl * 128:(gl + 1) * 128],
                                    sb_wr[:, kj, :], start=(kj == 0), stop=(kj == 7))
                M[("lgamm", g_)] = pe.inc(last)
            T(b_lga)

            def b_lgaev(v, g_=g_):
                v.wait_ge(pe.h, M[("lgamm", g_)])
                M[("lgaev", g_)] = dv.inc(v.tensor_copy(sb_lgt[:, g_ * 8:(g_ + 1) * 8],
                                                        psLgX.ap()))
            V(b_lgaev)

            for hh in range(2):
                def kmm(t, g_=g_, hh=hh, gl=gl):
                    if g_ == 0 and hh == 0:
                        t.wait_ge(dw.h, M["wkv"])
                    if ("kstt", g_ - 1, hh) in M:
                        t.wait_ge(dv.h, M[("kstt", g_ - 1, hh)])
                    last = None
                    for kj in range(8):
                        last = t.matmul(psK2[hh].ap(), xnT[:, kj, gl * 128:(gl + 1) * 128],
                                        wk_sb[:, kj, hh * 512:(hh + 1) * 512],
                                        start=(kj == 0), stop=(kj == 7))
                    M[("kmm", g_, hh)] = pe.inc(last)
                T(kmm)
            for hh in range(2):
                def vmm(t, g_=g_, hh=hh, gl=gl):
                    if g_ == 0 and hh == 0:
                        t.wait_ge(dwv.h, M["wv"])
                    if ("vev", g_ - 1, hh) in M:
                        t.wait_ge(dv.h, M[("vev", g_ - 1, hh)])
                    last = None
                    for kj in range(8):
                        last = t.matmul(psV2[hh].ap(), xnT[:, kj, gl * 128:(gl + 1) * 128],
                                        wv_sb[:, kj, hh * 512:(hh + 1) * 512],
                                        start=(kj == 0), stop=(kj == 7))
                    M[("vmm", g_, hh)] = pe.inc(last)
                T(vmm)
            for hh in range(2):
                def kmin(v, g_=g_, hh=hh):
                    v.wait_ge(pe.h, M[("kmm", g_, hh)])
                    if g_ == 0 and hh == 0:
                        v.wait_ge(dw.h, M["idz"])
                    if ("kstt", g_ - 1, hh) in M:
                        v.wait_ge(dv.h, M[("kstt", g_ - 1, hh)])
                    M[("kmin", g_, hh)] = dv.inc(v.tensor_single_scalar(
                        eluk[hh], psK2[hh].ap(), 0.0, ALU.min))
                V(kmin)
            for hh in range(2):
                def vev(v, g_=g_, hh=hh):
                    v.wait_ge(pe.h, M[("vmm", g_, hh)])
                    M[("vev", g_, hh)] = dv.inc(v.tensor_copy(
                        v_sb[:, g_, hh * 512:(hh + 1) * 512], psV2[hh].ap()))
                V(vev)
            for hh in range(2):
                def kexp(sc, g_=g_, hh=hh):
                    sc.wait_ge(dv.h, M[("kmin", g_, hh)])
                    M[("kexp", g_, hh)] = ac.inc(sc.activation(
                        eluk[hh], eluk[hh], AF.Exp, bias=0.0, scale=1.0))
                SC(kexp)
            for hh in range(2):
                def kstt(v, g_=g_, hh=hh):
                    v.wait_ge(ac.h, M[("kexp", g_, hh)])
                    M[("kstt", g_, hh)] = dv.inc(v.scalar_tensor_tensor(
                        k_sb[:, g_, hh * 512:(hh + 1) * 512], psK2[hh].ap(), 0.0,
                        eluk[hh], ALU.max, ALU.add))
                V(kstt)

        M[("halfVdone", half)] = dv.n
        ctxB.close()

        if half == 0:
            emit_ww()

        ctxQ = ExitStack()
        psQ = [ctxQ.enter_context(nc.psum_tensor(f"psQ{half}_{i}", [128, 512], F32))
               for i in range(2)]

        for s_ in range(2):
            for oj in range(8):
                def q_mm(t, s_=s_, oj=oj, half=half):
                    if s_ == 0 and oj == 0:
                        t.wait_ge(dw.h, M["wqkv"])
                        t.wait_ge(dv.h, M[("halfVdone", half)])
                        if half == 0:
                            t.wait_ge(dv.h, M["WWev"])
                    key = ("qev", half, s_, oj - 2) if oj >= 2 else \
                          (("qev", half, s_ - 1, 6 + oj) if s_ > 0 else ("qev", half - 1, 1, 6 + oj))
                    if key in M:
                        t.wait_ge(dv.h, M[key])
                    last = None
                    for kj in range(8):
                        last = t.matmul(psQ[oj % 2].ap(),
                                        wq_sb[:, kj, oj * 128:(oj + 1) * 128],
                                        xnT[:, kj, s_ * 512:(s_ + 1) * 512],
                                        start=(kj == 0), stop=(kj == 7))
                    M[("qmm", half, s_, oj)] = pe.inc(last)
                T(q_mm)

                def q_min(v, s_=s_, oj=oj, half=half):
                    i = (s_ * 8 + oj) % 2
                    v.wait_ge(pe.h, M[("qmm", half, s_, oj)])
                    key = ("qev", half, s_, oj - 2) if oj >= 2 else \
                          (("qev", half, s_ - 1, 6 + oj) if s_ > 0 else ("qev", half - 1, 1, 6 + oj))
                    if key in M:
                        v.wait_ge(dv.h, M[key])
                    M[("qmin", half, s_, oj)] = dv.inc(v.tensor_single_scalar(
                        sb_elu[i], psQ[oj % 2].ap(), 0.0, ALU.min))
                V(q_min)

                def q_exp(sc, s_=s_, oj=oj, half=half):
                    i = (s_ * 8 + oj) % 2
                    sc.wait_ge(dv.h, M[("qmin", half, s_, oj)])
                    M[("qexp", half, s_, oj)] = ac.inc(sc.activation(
                        sb_elu[i], sb_elu[i], AF.Exp, bias=0.0, scale=1.0))
                SC(q_exp)

                def q_ev(v, s_=s_, oj=oj, half=half):
                    i = (s_ * 8 + oj) % 2
                    v.wait_ge(ac.h, M[("qexp", half, s_, oj)])
                    if half == 0 and s_ == 0 and oj == 0:
                        v.wait_ge(pe.h, M["WWmm"])
                    c0 = half * 1024 + s_ * 512
                    M[("qev", half, s_, oj)] = dv.inc(v.scalar_tensor_tensor(
                        q_sb[:, oj, c0:c0 + 512], psQ[oj % 2].ap(), 0.0, sb_elu[i],
                        ALU.max, ALU.add))
                V(q_ev)
        ctxQ.close()

        if half == 0:
            M["half0_alldv"] = dv.n
            M["half0_dv"] = dv.n

    M["attn_proj_done_dv"] = dv.n
    M["attn_proj_done_pe"] = pe.n

    if upto <= 1:
        return finish_early()

    def w1_load(g):
        g.wait_ge(pe.h, M["attn_proj_done_pe"])
        for kj in range(6):
            dw1.inc(g.dma_start(out=w1_sb[:, kj, :], in_=w1_in[kj * 128:(kj + 1) * 128, :]), 16)
    G(w1_load)

    ctxD = ExitStack()
    psKVp = [ctxD.enter_context(nc.psum_tensor(f"psKVp{i}", [128, 512], F32)) for i in range(8)]

    def d_kv(t):
        t.wait_ge(dv.h, M["attn_proj_done_dv"])
        for j in range(8):
            last = None
            for g_ in range(NT):
                last = t.matmul(psKVp[j].ap()[:, 0:128],
                                k_sb[:, g_, j * 128:(j + 1) * 128],
                                v_sb[:, g_, j * 128:(j + 1) * 128],
                                start=(g_ == 0), stop=(g_ == NT - 1))
            M[("kvp", j)] = pe.inc(last)
    T(d_kv)

    def d_kv_ev(v):
        for j in range(8):
            v.wait_ge(pe.h, M[("kvp", j)])
            M[("kvpev", j)] = dv.inc(v.tensor_copy(kv_sb[:, j * 128:(j + 1) * 128],
                                                   psKVp[j].ap()[:, 0:128]))
        M["kvall_ev"] = dv.n
    V(d_kv_ev)
    ctxD.close()

    ctxD2 = ExitStack()
    psKS = ctxD2.enter_context(nc.psum_tensor("psKS", [1, 1024], F32))

    def d_ks(t):
        t.wait_ge(dv.h, M["kvall_ev"])
        for hh in range(2):
            last = None
            for g_ in range(NT):
                last = t.matmul(psKS.ap()[0:1, hh * 512:(hh + 1) * 512],
                                sb_ones_c, k_sb[:, g_, hh * 512:(hh + 1) * 512],
                                start=(g_ == 0), stop=(g_ == NT - 1))
            M[("ksum", hh)] = pe.inc(last)
    T(d_ks)

    def d_ks_ev(v):
        v.wait_ge(pe.h, M[("ksum", 1)])
        M["ksumev"] = dv.inc(v.tensor_copy(sb_ksrow, psKS.ap()[0:1, :]))
    V(d_ks_ev)
    ctxD2.close()

    def d_ar(g):
        g.wait_ge(dv.h, M["kvall_ev"])
        dkv.inc(g.dma_start(out=kv_red_i.ap()[0:128, :], in_=kv_sb), 16)
        g.wait_ge(dv.h, M["ksumev"])
        dkv.inc(g.dma_start(out=kv_red_i.ap()[128:129, :], in_=sb_ksrow), 16)
        g.wait_ge(dkv.h, dkv.n)
        c_kv.inc(g.collective_compute(
            "AllReduce", ALU.add, replica_groups=[[0, 1, 2, 3], [4, 5, 6, 7]],
            ins=[kv_red_i.ap().opt()], outs=[kv_red_o.ap().opt()]))
        g.wait_ge(c_kv.h, 1)
        dkv.inc(g.dma_start(out=kv_sb, in_=kv_red_o.ap()[0:128, :]), 16)
        M["kvback"] = dkv.inc(g.dma_start(out=sb_ksrow, in_=kv_red_o.ap()[128:129, :]), 16)
        for kj in range(2):
            dw1.inc(g.dma_start(out=w1t_sb[:, kj, :],
                                in_=w1_in[(6 + kj) * 128:(7 + kj) * 128, :]), 16)
        M["w1full"] = dw1.n
        for kj in range(8):
            dwo.inc(g.dma_start(out=wo_sb[:, kj, :], in_=woT_in[kj * 128:(kj + 1) * 128, :]), 16)
        M["wofull"] = dwo.n
        for fj in range(8):
            dw2.inc(g.dma_start(out=w2c_sb[:, fj, :],
                                in_=w2_in[(24 + fj) * 128:(25 + fj) * 128, :]), 16)
    G(d_ar)

    ctxD3 = ExitStack()
    psT1 = ctxD3.enter_context(nc.psum_tensor("psT1", [128, 512], F32))

    def d_kv2(v):
        v.wait_ge(dkv.h, M["kvback"])
        n0 = dv.inc(v.memset(sb_kv2[:, :, :], 0.0))
        v.wait_ge(dv.h, n0)
        for j in range(8):
            dv.inc(v.tensor_copy(sb_kv2[0:64, j, 0:64], kv_sb[0:64, j * 128:j * 128 + 64]))
            dv.inc(v.tensor_copy(sb_kv2[64:128, j, 64:128],
                                 kv_sb[64:128, j * 128 + 64:(j + 1) * 128]))
        M["kv2done"] = dv.n
        n1 = dv.inc(v.memset(sb_KS[:, :, :], 0.0))
        n2 = dv.inc(v.memset(sb_kspad, 0.0))
        v.wait_ge(dv.h, n2)
        M["kspadz"] = dv.n
    V(d_kv2)

    for j in range(8):
        def ks_row(v, j=j):
            if j == 0:
                v.wait_ge(dv.h, M["kspadz"])
            else:
                v.wait_ge(pe.h, M[("kstr", j - 1)])
            M[("kspadw", j)] = dv.inc(v.tensor_copy(sb_kspad[0:1, :],
                                                    sb_ksrow[0:1, j * 128:(j + 1) * 128]))
        V(ks_row)

        def ks_tr(t, j=j):
            t.wait_ge(dv.h, M[("kspadw", j)])
            if j > 0:
                t.wait_ge(dv.h, M[("kscopy", j - 1)])
            M[("kstr", j)] = pe.inc(t.transpose(psT1.ap()[:, 0:128], sb_kspad, sb_identf))
        T(ks_tr)

        def ks_cp(v, j=j):
            v.wait_ge(pe.h, M[("kstr", j)])
            dv.inc(v.tensor_copy(sb_KS[0:64, j, 2 * j:2 * j + 1], psT1.ap()[0:64, 0:1]))
            M[("kscopy", j)] = dv.inc(v.tensor_copy(sb_KS[64:128, j, 2 * j + 1:2 * j + 2],
                                                    psT1.ap()[64:128, 0:1]))
        V(ks_cp)
    M["KSdone"] = dv.n
    ctxD3.close()

    if upto <= 2:
        return finish_early()

    ctxE = ExitStack()
    psDen = ctxE.enter_context(nc.psum_tensor("psDen", [128, 512], F32))
    psNum = [ctxE.enter_context(nc.psum_tensor(f"psNum{i}", [128, 512], F32)) for i in range(2)]
    psBc = [ctxE.enter_context(nc.psum_tensor(f"psBc{i}", [128, 512], F32)) for i in range(2)]
    psWo = [ctxE.enter_context(nc.psum_tensor(f"psWo{i}", [128, 512], F32)) for i in range(2)]
    psLg = ctxE.enter_context(nc.psum_tensor("psLg", [128, 512], F32))

    for s_ in range(4):
        def e_den(t, s_=s_):
            if s_ == 0:
                t.wait_ge(dv.h, M["KSdone"])
                t.wait_ge(dv.h, M["kv2done"])
            else:
                t.wait_ge(dv.h, M[("denev", s_ - 1)])
            last = None
            for j in range(8):
                last = t.matmul(psDen.ap()[0:16, :], sb_KS[:, j, :],
                                q_sb[:, j, s_ * 512:(s_ + 1) * 512],
                                start=(j == 0), stop=(j == 7))
            M[("denmm", s_)] = pe.inc(last)
        T(e_den)

        def e_denev(v, s_=s_):
            v.wait_ge(pe.h, M[("denmm", s_)])
            n1 = dv.inc(v.tensor_scalar(sb_den, psDen.ap()[0:16, :], EPS_ATTN, None, ALU.add))
            v.wait_ge(dv.h, n1)
            n2 = dv.inc(v.reciprocal(sb_den, sb_den))
            v.wait_ge(dv.h, n2)
            M[("denev", s_)] = dv.inc(v.tensor_copy(sb_recb, sb_den))
        V(e_denev)

        for j in range(8):
            def e_num(t, s_=s_, j=j):
                if j == 0:
                    t.wait_ge(dv.h, M[("denev", s_)])
                keyn = ("ainum", s_, j - 2) if j >= 2 else ("ainum", s_ - 1, 6 + j)
                if keyn in M:
                    t.wait_ge(dv.h, M[keyn])
                M[("nummm", s_, j)] = pe.inc(t.matmul(psNum[j % 2].ap(), sb_kv2[:, j, :],
                                                      q_sb[:, j, s_ * 512:(s_ + 1) * 512],
                                                      start=True, stop=True))
                keyb = ("bcev", s_, j - 2) if j >= 2 else ("bcev", s_ - 1, 6 + j)
                if keyb in M:
                    t.wait_ge(dv.h, M[keyb])
                M[("bcmm", s_, j)] = pe.inc(t.matmul(psBc[j % 2].ap(),
                                                     sb_sel[:, j * 128:(j + 1) * 128],
                                                     sb_recb, start=True, stop=True))
            T(e_num)

            def e_bcev(v, s_=s_, j=j):
                v.wait_ge(pe.h, M[("bcmm", s_, j)])
                keym = ("ainum", s_, j - 1) if j >= 1 else ("ainum", s_ - 1, 7)
                if keym in M:
                    v.wait_ge(dv.h, M[keym])
                M[("bcev", s_, j)] = dv.inc(v.tensor_copy(sb_bcs, psBc[j % 2].ap()))
            V(e_bcev)

            def e_ai(v, s_=s_, j=j):
                v.wait_ge(pe.h, M[("nummm", s_, j)])
                v.wait_ge(dv.h, M[("bcev", s_, j)])
                M[("ainum", s_, j)] = dv.inc(v.tensor_mul(
                    ai_sb[:, j, s_ * 512:(s_ + 1) * 512], psNum[j % 2].ap(), sb_bcs))
            V(e_ai)
        M[("aislice", s_)] = dv.n

        for gg in range(4):
            g_ = s_ * 4 + gg

            def e_ldx(s, g_=g_, s_=s_):
                if g_ < 2:
                    s.wait_ge(dv.h, M["attn_proj_done_dv"])
                if ("y1add", g_ - 2) in M:
                    s.wait_ge(dv.h, M[("y1add", g_ - 2)])
                x_loaded[g_] = dx2[g_ % 2].inc(s.dma_start(out=x_rot[g_ % 2],
                                                  in_=x_in[g_ * 128:(g_ + 1) * 128, :]), 16)
            SY(e_ldx)

            def e_wo(t, g_=g_, s_=s_):
                if g_ == 0:
                    t.wait_ge(dwo.h, M["wofull"])
                t.wait_ge(dv.h, M[("aislice", s_)])
                if ("y1add", g_ - 1) in M:
                    t.wait_ge(dv.h, M[("y1add", g_ - 1)])
                last = None
                for hh in range(2):
                    for j in range(8):
                        last = t.matmul(psWo[hh].ap(), ai_sb[:, j, g_ * 128:(g_ + 1) * 128],
                                        wo_sb[:, j, hh * 512:(hh + 1) * 512],
                                        start=(j == 0), stop=(j == 7))
                M[("womm", g_)] = pe.inc(last)
            T(e_wo)

            def e_res(v, g_=g_):
                v.wait_ge(pe.h, M[("womm", g_)])
                v.wait_ge(dx2[g_ % 2].h, x_loaded[g_])
                if g_ == 0:
                    v.wait_ge(dw.h, M["wqkv"])
                if ("y1store", g_ - 2) in M:
                    v.wait_ge(dy2[g_ % 2].h, M[("y1store", g_ - 2)])
                dv.inc(v.tensor_add(y1_rot[g_ % 2][:, 0:512], psWo[0].ap(),
                                    x_rot[g_ % 2][:, 0:512]))
                n0 = dv.inc(v.tensor_add(y1_rot[g_ % 2][:, 512:1024], psWo[1].ap(),
                                         x_rot[g_ % 2][:, 512:1024]))
                M[("y1add", g_)] = n0
            V(e_res)

            def e_rstd(sc, g_=g_):
                ya = y1_rot[g_ % 2]
                gcol = ln2["m"][:, g_:g_ + 1]
                sc.wait_ge(dv.h, M[("y1add", g_)])
                a0 = ac.inc(sc.activation(sb_junk, ya, AF.Copy, bias=0.0, scale=1.0,
                                          accum_out=ln2["sum"][:, g_:g_ + 1]))
                sc.wait_ge(ac.h, a0)
                a1 = ac.inc(sc.activation(ln2["msq"][:, g_:g_ + 1], ln2["sum"][:, g_:g_ + 1],
                                          AF.Copy, bias=0.0, scale=-1.0 / D))
                a1b = ac.inc(sc.activation(gcol, ln2["sum"][:, g_:g_ + 1], AF.Copy,
                                           bias=0.0, scale=1.0 / D))
                sc.wait_ge(ac.h, a1b)
                a2 = ac.inc(sc.activation(sb_junk, ya, AF.Square,
                                          bias=ln2["msq"][:, g_:g_ + 1], scale=1.0,
                                          accum_out=ln2["ssq"][:, g_:g_ + 1]))
                sc.wait_ge(ac.h, a2)
                a3 = ac.inc(sc.activation(ln2["lnl"][:, g_:g_ + 1], ln2["ssq"][:, g_:g_ + 1],
                                          AF.Ln, bias=sb_eps[:, 0:1], scale=1.0 / D))
                sc.wait_ge(ac.h, a3)
                M[("rstd2", g_)] = ac.inc(sc.activation(ln2["rstd"][:, g_:g_ + 1],
                                                        ln2["lnl"][:, g_:g_ + 1],
                                                        AF.Exp, bias=0.0, scale=-0.5))
            SC(e_rstd)

            def e_t(v, g_=g_):
                v.wait_ge(ac.h, M[("rstd2", g_)])
                if ("tstore", g_ - 2) in M:
                    v.wait_ge(dt2[g_ % 2].h, M[("tstore", g_ - 2)])
                M[("t2", g_)] = dv.inc(v.tensor_scalar(
                    t_rot[g_ % 2], y1_rot[g_ % 2], ln2["m"][:, g_:g_ + 1],
                    ln2["rstd"][:, g_:g_ + 1], ALU.subtract, ALU.mult))
            V(e_t)

            def e_store(s, g_=g_):
                s.wait_ge(dv.h, M[("y1add", g_)])
                M[("y1store", g_)] = dy2[g_ % 2].inc(s.dma_start(
                    out=y1_dram.ap()[g_ * 128:(g_ + 1) * 128, :], in_=y1_rot[g_ % 2]), 16)
                s.wait_ge(dv.h, M[("t2", g_)])
                M[("tstore", g_)] = dt2[g_ % 2].inc(s.dma_start(
                    out=t_loc.ap()[g_ * 128:(g_ + 1) * 128, :], in_=t_rot[g_ % 2]), 16)
            SY(e_store)

        for gg in range(4):
            g_ = s_ * 4 + gg

            def e_lg(t, g_=g_, s_=s_):
                if g_ == 0:
                    t.wait_ge(dbr.h, M["Sbc"])
                t.wait_ge(dv.h, M[("aislice", s_)])
                if ("lgtev", g_ - 1) in M:
                    t.wait_ge(dv.h, M[("lgtev", g_ - 1)])
                last = None
                for j in range(8):
                    last = t.matmul(psLg.ap()[:, 0:8], ai_sb[:, j, g_ * 128:(g_ + 1) * 128],
                                    sb_WW[:, j, :], start=(j == 0), stop=(j == 7))
                M[("lgmm2", g_)] = pe.inc(last)
            T(e_lg)

            def e_lgc(v, g_=g_):
                v.wait_ge(pe.h, M[("lgmm2", g_)])
                v.wait_ge(ac.h, M[("rstd2", g_)])
                n0 = dv.inc(v.tensor_mul(sb_c1[:, g_:g_ + 1], ln1["std"][:, g_:g_ + 1],
                                         ln2["rstd"][:, g_:g_ + 1]))
                n1 = dv.inc(v.tensor_sub(sb_c2[:, g_:g_ + 1], ln1["m"][:, g_:g_ + 1],
                                         ln2["m"][:, g_:g_ + 1]))
                v.wait_ge(dv.h, n1)
                n2 = dv.inc(v.tensor_mul(sb_c2[:, g_:g_ + 1], sb_c2[:, g_:g_ + 1],
                                         ln2["rstd"][:, g_:g_ + 1]))
                v.wait_ge(dv.h, max(n0, n2))
                n3 = dv.inc(v.tensor_scalar(sb_lgt[:, g_ * 8:(g_ + 1) * 8],
                                            sb_lgt[:, g_ * 8:(g_ + 1) * 8],
                                            sb_c1[:, g_:g_ + 1], None, ALU.mult))
                v.wait_ge(dv.h, n3)
                n4 = dv.inc(v.scalar_tensor_tensor(
                    sb_lgt[:, g_ * 8:(g_ + 1) * 8], psLg.ap()[:, 0:8],
                    ln2["rstd"][:, g_:g_ + 1], sb_lgt[:, g_ * 8:(g_ + 1) * 8],
                    ALU.mult, ALU.add))
                v.wait_ge(dv.h, n4)
                M[("lgtev", g_)] = dv.inc(v.scalar_tensor_tensor(
                    sb_lgt[:, g_ * 8:(g_ + 1) * 8], sb_Sbc, sb_c2[:, g_:g_ + 1],
                    sb_lgt[:, g_ * 8:(g_ + 1) * 8], ALU.mult, ALU.add))
            V(e_lgc)

            def r_nmx(v, g_=g_):
                v.wait_ge(dv.h, M[("lgtev", g_)])
                n0 = dv.inc(v.reduce_max(sb_nmx[:, g_:g_ + 1], sb_lgt[:, g_ * 8:(g_ + 1) * 8],
                                         axis=AX.X))
                v.wait_ge(dv.h, n0)
                M[("nmx", g_)] = dv.inc(v.tensor_scalar(sb_nmx[:, g_:g_ + 1],
                                                        sb_nmx[:, g_:g_ + 1],
                                                        -1.0, None, ALU.mult))
            V(r_nmx)

            def r_exp(sc, g_=g_):
                sc.wait_ge(dv.h, M[("nmx", g_)])
                if ("probs", g_ - 1) in M:
                    sc.wait_ge(dv.h, M[("probs", g_ - 1)])
                M[("eexp", g_)] = ac.inc(sc.activation(sb_eexp, sb_lgt[:, g_ * 8:(g_ + 1) * 8],
                                                       AF.Exp, bias=sb_nmx[:, g_:g_ + 1],
                                                       scale=1.0))
            SC(r_exp)

            def r_rest(v, g_=g_):
                v.wait_ge(ac.h, M[("eexp", g_)])
                n1 = dv.inc(v.reduce_sum(sb_esum[:, g_:g_ + 1], sb_eexp, axis=AX.X))
                v.wait_ge(dv.h, n1)
                n2 = dv.inc(v.reciprocal(sb_ers[:, g_:g_ + 1], sb_esum[:, g_:g_ + 1]))
                v.wait_ge(dv.h, n2)
                M[("probs", g_)] = dv.inc(v.tensor_scalar(sb_probs, sb_eexp,
                                                          sb_ers[:, g_:g_ + 1], None, ALU.mult))
                v.wait_ge(dv.h, M[("probs", g_)])
                n3 = dv.inc(v.max(sb_top8, sb_probs))
                v.wait_ge(dv.h, n3)
                n4 = dv.inc(v.max_index(sb_idx8, sb_top8, sb_probs))
                v.wait_ge(dv.h, n4)
                dv.inc(v.tensor_copy(sb_pm[:, g_:g_ + 1], sb_top8[:, 0:1]))
                n5 = dv.inc(v.tensor_copy(sb_idxf[:, g_:g_ + 1], sb_idx8[:, 0:1]))
                v.wait_ge(dv.h, n5)
                n6 = dv.inc(v.tensor_scalar(sb_ohb[:, g_ * 8:(g_ + 1) * 8], sb_iota8,
                                            sb_idxf[:, g_:g_ + 1], None, ALU.is_equal))
                v.wait_ge(dv.h, n6)
                M[("oh", g_)] = dv.inc(v.tensor_copy(sb_ohf[:, g_ * 8:(g_ + 1) * 8],
                                                     sb_ohb[:, g_ * 8:(g_ + 1) * 8]))
            V(r_rest)

        def t_ag_chunk(g, s_=s_):
            g.wait_ge(dt2[0].h, M[("tstore", s_ * 4 + 2)])
            g.wait_ge(dt2[1].h, M[("tstore", s_ * 4 + 3)])
            c_tc[s_].inc(g.collective_compute(
                "AllGather", ALU.bypass, replica_groups=[list(range(NCORE))],
                ins=[t_loc.ap()[s_ * 512:(s_ + 1) * 512, :].opt()],
                outs=[t_full.ap()[s_ * 4096:(s_ + 1) * 4096, :].opt()]))
        G(t_ag_chunk)

    M["E_done_pe"] = pe.n
    M["E_done_dv"] = dv.n
    M["t_stored0"] = dt2[0].n
    M["t_stored1"] = dt2[1].n
    ctxE.close()

    if upto <= 3:
        return finish_early()

    def w2_load(g):
        g.wait_ge(pe.h, M["E_done_pe"])
        g.wait_ge(dv.h, M["E_done_dv"])
        for fj in range(16):
            dw2.inc(g.dma_start(out=w2a_sb[:, fj, :], in_=w2_in[fj * 128:(fj + 1) * 128, :]), 16)
        for fj in range(8):
            dw2.inc(g.dma_start(out=w2b_sb[:, fj, :],
                                in_=w2_in[(16 + fj) * 128:(17 + fj) * 128, :]), 16)
        M["w2full"] = dw2.n
    G(w2_load)

    ctxF = ExitStack()
    psPos = ctxF.enter_context(nc.psum_tensor("psPos", [128, 512], F32))
    psSm = [ctxF.enter_context(nc.psum_tensor(f"psSm{i}", [128, 512], F32)) for i in range(3)]

    M["oh_all"] = M[("oh", NT - 1)]

    def f_pos(t):
        t.wait_ge(dv.h, M["oh_all"])
        M["posmm"] = pe.inc(t.matmul(psPos.ap()[:, 0:128], sb_L, sb_ohb, start=True, stop=True))
        M["totmm"] = pe.inc(t.matmul(psSm[0].ap()[0:1, 0:128], sb_ones_c, sb_ohb,
                                     start=True, stop=True))
    T(f_pos)

    def f_tot(v):
        v.wait_ge(pe.h, M["totmm"])
        n0 = dv.inc(v.tensor_copy(sb_totr, psSm[0].ap()[0:1, 0:128]))
        n1 = dv.inc(v.memset(sb_totp, 0.0))
        v.wait_ge(dv.h, max(n0, n1))
        M["totp"] = dv.inc(v.tensor_copy(sb_totp[0:1, :], sb_totr))
    V(f_tot)

    def f_tottr(t):
        t.wait_ge(dv.h, M["totp"])
        M["tottr"] = pe.inc(t.transpose(psSm[1].ap()[:, 0:128], sb_totp, sb_identf))
    T(f_tottr)

    def f_totc(v):
        v.wait_ge(pe.h, M["tottr"])
        M["totc"] = dv.inc(v.tensor_copy(sb_totc, psSm[1].ap()[:, 0:1]))
    V(f_totc)

    def f_bt(t):
        t.wait_ge(dv.h, M["totc"])
        M["btmm"] = pe.inc(t.matmul(psSm[2].ap()[:, 0:1], sb_pref, sb_totc,
                                    start=True, stop=True))
        M["cntmm"] = pe.inc(t.matmul(psSm[0].ap()[0:8, 0:1], sb_sumg, sb_totc,
                                     start=True, stop=True))
    T(f_bt)

    def f_cnt(v):
        v.wait_ge(pe.h, M["cntmm"])
        M["cnt8"] = dv.inc(v.tensor_copy(sb_cnt8, psSm[0].ap()[0:8, 0:1]))
        v.wait_ge(pe.h, M["btmm"])
        M["btev"] = dv.inc(v.tensor_copy(sb_bt, psSm[2].ap()[:, 0:1]))
    V(f_cnt)

    def f_cntx(g):
        g.wait_ge(dv.h, M["cnt8"])
        dcn.inc(g.dma_start(out=cnt_i.ap(), in_=sb_cnt8), 16)
        g.wait_ge(dcn.h, dcn.n)
        c_cnt.inc(g.collective_compute(
            "AllGather", ALU.bypass, replica_groups=[list(range(NCORE))],
            ins=[cnt_i.ap().opt()], outs=[cnt_o.ap().opt()]))
        g.wait_ge(c_cnt.h, 1)
        M["cnts"] = dcn.inc(g.dma_start(out=sb_cnts,
                                       in_=cnt_o.ap().rearrange("(c e) o -> c (e o)", e=8)), 16)
    G(f_cntx)

    def f_bcore(t):
        t.wait_ge(dcn.h, M["cnts"])
        M["bcoremm"] = pe.inc(t.matmul(psSm[0].ap()[0:8, 0:1], sb_cnts, sb_cmask,
                                       start=True, stop=True))
    T(f_bcore)

    def f_bcore8(v):
        v.wait_ge(pe.h, M["bcoremm"])
        M["bcore8"] = dv.inc(v.tensor_copy(sb_cnt8, psSm[0].ap()[0:8, 0:1]))
    V(f_bcore8)

    def f_bc2(t):
        t.wait_ge(dv.h, M["bcore8"])
        M["bc2mm"] = pe.inc(t.matmul(psSm[1].ap()[:, 0:1], sb_tileE, sb_cnt8,
                                     start=True, stop=True))
    T(f_bc2)

    def f_bcol(v):
        v.wait_ge(pe.h, M["bc2mm"])
        n0 = dv.inc(v.tensor_copy(sb_bc2, psSm[1].ap()[:, 0:1]))
        v.wait_ge(dv.h, n0)
        n1 = dv.inc(v.tensor_add(sb_bcol, sb_bt, sb_bc2))
        n2 = dv.inc(v.memset(sb_bpad, 0.0))
        v.wait_ge(dv.h, max(n1, n2))
        M["bpad"] = dv.inc(v.tensor_copy(sb_bpad[:, 0:1], sb_bcol))
    V(f_bcol)

    def f_btr(t):
        t.wait_ge(dv.h, M["bpad"])
        M["btr"] = pe.inc(t.transpose(psSm[2].ap()[:, 0:128], sb_bpad, sb_identf))
    T(f_btr)

    def f_brow(v):
        v.wait_ge(pe.h, M["btr"])
        M["brow"] = dv.inc(v.tensor_copy(sb_brow, psSm[2].ap()[0:1, 0:128]))
    V(f_brow)

    def f_bbc(g):
        g.wait_ge(dv.h, M["brow"])
        dbr.inc(g.dma_start(out=brow_d.ap(), in_=sb_brow), 16)
        g.wait_ge(dbr.h, dbr.n)
        M["bbc"] = dbr.inc(g.dma_start(out=sb_bbc, in_=brow_d.ap().partition_broadcast(128)), 16)
    G(f_bbc)

    def f_flat(v):
        v.wait_ge(dbr.h, M["bbc"])
        v.wait_ge(pe.h, M["posmm"])
        n0 = dv.inc(v.tensor_add(sb_posf, psPos.ap()[:, 0:128], sb_bbc))
        v.wait_ge(dv.h, n0)
        n1 = dv.inc(v.tensor_mul(sb_prod, sb_posf, sb_ohf))
        v.wait_ge(dv.h, n1)
        n2 = dv.inc(v.reduce_sum(sb_psel, sb_prod.rearrange("p (g e) -> p g e", e=8), axis=AX.X))
        v.wait_ge(dv.h, n2)
        dv.inc(v.tensor_scalar(sb_kept, sb_psel, float(CAP), None, ALU.is_lt))
        n3 = dv.inc(v.tensor_scalar(sb_posc, sb_psel, float(CAP - 1), None, ALU.min))
        v.wait_ge(dv.h, n3)
        n4 = dv.inc(v.scalar_tensor_tensor(sb_flat, sb_idxf, float(CAP), sb_posc,
                                           ALU.mult, ALU.add))
        na = dv.inc(v.tensor_scalar(sb_plo, sb_posc, 512.0, None, ALU.is_ge))
        for thr in (1024.0, 1536.0, 2048.0):
            nb = dv.inc(v.tensor_scalar(sb_phi, sb_posc, thr, None, ALU.is_ge))
            v.wait_ge(dv.h, nb)
            na = dv.inc(v.tensor_add(sb_plo, sb_plo, sb_phi))
            v.wait_ge(dv.h, na)
        nc_ = dv.inc(v.scalar_tensor_tensor(sb_phi, sb_plo, 3584.0, sb_posc,
                                            ALU.mult, ALU.add))
        v.wait_ge(dv.h, nc_)
        nd = dv.inc(v.scalar_tensor_tensor(sb_phi, sb_idxf, 512.0, sb_phi,
                                           ALU.mult, ALU.add))
        v.wait_ge(dv.h, max(n4, nd))
        dv.inc(v.tensor_copy(sb_flatu, sb_phi))
        dv.inc(v.tensor_mul(sb_afac, sb_kept, sb_pm))
        dv.inc(v.tensor_scalar(sb_bfac, sb_kept, -1.0, 1.0, ALU.mult, ALU.add))
        dv.inc(v.tensor_scalar(sb_gid1, sb_tokid1, sb_cvals[:, 0:1], None, ALU.add))
        n7 = dv.inc(v.tensor_scalar(sb_scf, sb_flat, 1.0e6, None, ALU.add))
        v.wait_ge(dv.h, n7)
        n8 = dv.inc(v.scalar_tensor_tensor(sb_scf2, sb_kept, -1.0e6, sb_scf,
                                           ALU.mult, ALU.add))
        v.wait_ge(dv.h, n8)
        M["scu"] = dv.inc(v.tensor_copy(sb_scu, sb_scf2))
        M["routing_done"] = dv.n
    V(f_flat)
    ctxF.close()

    def f_scatter(g):
        g.wait_ge(dv.h, M["routing_done"])
        g.wait_ge(dw.h, M["wqkv"])
        for g_ in range(NT):
            dsid.inc(g.indirect_dma_start(
                out=id_disp.ap().rearrange("(a b) -> a b", b=1),
                out_offset=bass.IndirectOffsetOnAxis(sb_scu[:, g_:g_ + 1], 0),
                in_=sb_gid1[:, g_:g_ + 1], in_offset=None,
                bounds_check=E * CAP - 1, oob_is_err=False), 16)
        g.wait_ge(dsid.h, dsid.n)
        c_id.inc(g.collective_compute(
            "AllReduce", ALU.add, replica_groups=[list(range(NCORE))],
            ins=[id_disp.ap().rearrange("(a b) -> a b", b=128).opt()],
            outs=[id_full.ap().rearrange("(a b) -> a b", b=128).opt()]))
        po.inc(g.iota(sb_idx20, pattern=[[0, 1]], base=0, channel_multiplier=128,
                      allow_small_or_imprecise_dtypes=True))
        M["po_iota"] = po.n
    G(f_scatter)

    def f_idx20(v):
        v.wait_ge(po.h, M["po_iota"])
        n0 = dv.inc(v.tensor_scalar(sb_idx20, sb_idx20, sb_cvals[0:20, 1:2], None, ALU.add))
        v.wait_ge(dv.h, n0)
        M["idx20u"] = dv.inc(v.tensor_copy(sb_idx20u, sb_idx20))
    V(f_idx20)

    def f_ids(g):
        g.wait_ge(dv.h, M["idx20u"])
        g.wait_ge(c_id.h, 1)
        M["ids20"] = did.inc(g.indirect_dma_start(
            out=sb_ids20[:, :], out_offset=None,
            in_=id_full.ap().rearrange("(a b) -> a b", b=1),
            in_offset=bass.IndirectOffsetOnAxis(sb_idx20u[:, 0:1], 0)), 16)
    G(f_ids)

    ctxG0 = ExitStack()
    psG0 = ctxG0.enter_context(nc.psum_tensor("psG0", [128, 512], F32))

    def g_idpad(v):
        v.wait_ge(did.h, M["ids20"])
        n0 = dv.inc(v.memset(sb_bpad, 0.0))
        v.wait_ge(dv.h, n0)
        M["idpad"] = dv.inc(v.tensor_copy(sb_bpad[0:20, :], sb_ids20))
    V(g_idpad)

    def g_idtr(t):
        t.wait_ge(dv.h, M["idpad"])
        M["idtr"] = pe.inc(t.transpose(psG0.ap()[:, 0:128], sb_bpad, sb_identf))
    T(g_idtr)

    def g_gidx(v):
        v.wait_ge(pe.h, M["idtr"])
        n0 = dv.inc(v.tensor_scalar(sb_idm1, psG0.ap()[:, 0:20], 1.0, None, ALU.subtract))
        v.wait_ge(dv.h, n0)
        M["gidx"] = dv.inc(v.tensor_copy(sb_gidx, sb_idm1))
    V(g_gidx)
    ctxG0.close()

    if upto <= 4:
        return finish_early()

    ctxG = ExitStack()
    psTr = [ctxG.enter_context(nc.psum_tensor(f"psTrG{i}", [128, 1024], BF16)) for i in range(2)]
    psH = [ctxG.enter_context(nc.psum_tensor(f"psHG{i}", [128, 512], F32)) for i in range(2)]
    psEo = [ctxG.enter_context(nc.psum_tensor(f"psEG{i}", [128, 512], F32)) for i in range(4)]

    NST = CAP // 256

    for st in range(NST):
        for qq in range(2):
            r0 = st * 2 + qq

            def g_gath(s, st=st, qq=qq, r0=r0):
                if st == 0 and qq == 0:
                    s.wait_ge(dv.h, M["gidx"])
                    for cs in c_tc:
                        s.wait_ge(cs.h, 1)
                if ("tgathtr", st - 1, qq) in M:
                    s.wait_ge(pe.h, M[("tgathtr", st - 1, qq)])
                M[("tgath", st, qq)] = dtg2[qq].inc(s.indirect_dma_start(
                    out=gath_sb[qq], out_offset=None,
                    in_=t_full.ap(),
                    in_offset=bass.IndirectOffsetOnAxis(sb_gidx[:, r0:r0 + 1], 0),
                    bounds_check=NCORE * TPC - 1, oob_is_err=False), 16)
            G(g_gath)

            for j in range(8):
                def g_tr(t, st=st, qq=qq, j=j):
                    if j == 0:
                        t.wait_ge(dtg2[qq].h, M[("tgath", st, qq)])
                    prev = ("tgev", st, qq, j - 2) if j >= 2 else \
                           (("tgev", st, qq - 1, 6 + j) if qq > 0 else ("tgev", st - 1, 1, 6 + j))
                    if prev in M:
                        t.wait_ge(dv.h, M[prev])
                    M[("tgtr", st, qq, j)] = pe.inc(t.transpose(
                        psTr[j % 2].ap()[:, 0:128], gath_sb[qq][:, j * 128:(j + 1) * 128],
                        sb_ident))
                T(g_tr)

                def g_trev(v, st=st, qq=qq, j=j):
                    v.wait_ge(pe.h, M[("tgtr", st, qq, j)])
                    M[("tgev", st, qq, j)] = dv.inc(v.tensor_copy(
                        tTe_sb[:, j, qq * 128:(qq + 1) * 128], psTr[j % 2].ap()[:, 0:128]))
                V(g_trev)
            M[("tgathtr", st, qq)] = pe.n
            M[("tTe", st, qq)] = dv.n

        for fj in range(32):
            def g_h(t, st=st, fj=fj):
                if fj == 0:
                    t.wait_ge(dv.h, M[("tTe", st, 1)])
                    if st == 0:
                        t.wait_ge(dw1.h, M["w1full"])
                key = ("hev", st, fj - 2) if fj >= 2 else ("hev", st - 1, 30 + fj)
                if key in M:
                    t.wait_ge(dv.h, M[key])
                last = None
                for kj in range(8):
                    last = t.matmul(psH[fj % 2].ap()[:, 0:256], w1_ap(kj, fj * 128, (fj + 1) * 128),
                                    tTe_sb[:, kj, :], start=(kj == 0), stop=(kj == 7))
                M[("hmm", st, fj)] = pe.inc(last)
            T(g_h)

            def g_hev(v, st=st, fj=fj):
                v.wait_ge(pe.h, M[("hmm", st, fj)])
                if fj == 0:
                    if st == 0:
                        v.wait_ge(pe.h, M[("lgmm2", 15)])
                    if ("w2mm", st - 1) in M:
                        v.wait_ge(pe.h, M[("w2mm", st - 1)])
                M[("hev", st, fj)] = dv.inc(v.tensor_scalar(
                    h_sb[:, fj, :], psH[fj % 2].ap()[:, 0:256], 0.0, None, ALU.max))
            V(g_hev)
        M[("hall", st)] = dv.n

        def g_w2(t, st=st):
            t.wait_ge(dv.h, M[("hall", st)])
            if st == 0:
                t.wait_ge(dw2.h, M["w2full"])
            if ("eoev", st - 1) in M:
                t.wait_ge(dv.h, M[("eoev", st - 1)])
            last = None
            for rs in range(2):
                for hh in range(2):
                    for fj in range(32):
                        last = t.matmul(psEo[rs * 2 + hh].ap(),
                                        h_sb[:, fj, rs * 128:(rs + 1) * 128],
                                        w2_ap(fj)[:, hh * 512:(hh + 1) * 512],
                                        start=(fj == 0), stop=(fj == 31))
            M[("w2mm", st)] = pe.inc(last)
        T(g_w2)

        def g_eoev(v, st=st):
            v.wait_ge(pe.h, M[("w2mm", st)])
            if ("eostore", st - 1) in M:
                v.wait_ge(deo.h, M[("eostore", st - 1)])
            for rs in range(2):
                dv.inc(v.tensor_copy(eo_sb[rs][:, 0:512], psEo[rs * 2].ap()))
                dv.inc(v.tensor_copy(eo_sb[rs][:, 512:1024], psEo[rs * 2 + 1].ap()))
            M[("eoev", st)] = dv.n
        V(g_eoev)

        def g_eost(s, st=st):
            s.wait_ge(dv.h, M[("eoev", st)])
            for rs in range(2):
                deo.inc(s.dma_start(
                    out=eout_loc.ap()[st * 256 + rs * 128: st * 256 + (rs + 1) * 128, :],
                    in_=eo_sb[rs]), 16)
            M[("eostore", st)] = deo.n
        SY(g_eost)

        if st % 2 == 1:
            def eo_ag_chunk(g, st=st):
                k = st // 2
                g.wait_ge(deo.h, M[("eostore", st)])
                c_ec[k].inc(g.collective_compute(
                    "AllGather", ALU.bypass, replica_groups=[list(range(NCORE))],
                    ins=[eout_loc.ap()[k * 512:(k + 1) * 512, :].opt()],
                    outs=[eout_full.ap()[k * 4096:(k + 1) * 4096, :].opt()]))
            G(eo_ag_chunk)
    ctxG.close()

    if upto <= 5:
        return finish_early()

    def emit_eg(g_):
        def h_eg(g, g_=g_):
            if g_ == 0:
                for cs in c_ec:
                    g.wait_ge(cs.h, 1)
            if ("comb1", g_ - 2) in M:
                g.wait_ge(dv.h, M[("comb1", g_ - 2)])
            M[("eg", g_)] = deg2[g_ % 2].inc(g.indirect_dma_start(
                out=eg_sb[g_ % 2], out_offset=None,
                in_=eout_full.ap(),
                in_offset=bass.IndirectOffsetOnAxis(sb_flatu[:, g_:g_ + 1], 0)), 16)
        G(h_eg)

    emit_eg(0)
    for g_ in range(NT):
        if g_ + 1 < NT:
            emit_eg(g_ + 1)

        def h_ld(s, g_=g_):
            if g_ == 0:
                s.wait_ge(deo.h, M[("eostore", NST - 1)])
                s.wait_ge(dv.h, M["E_done_dv"])
                s.wait_ge(dy2[0].h, dy2[0].n)
                s.wait_ge(dy2[1].h, dy2[1].n)
            if ("comb2", g_ - 2) in M:
                s.wait_ge(pcb.h, M[("comb2", g_ - 2)])
            M[("y1l", g_)] = dhl2[g_ % 2].inc(s.dma_start(out=y1_rot[g_ % 2],
                                                 in_=y1_dram.ap()[g_ * 128:(g_ + 1) * 128, :]), 16)
            M[("tl", g_)] = dhl2[g_ % 2].inc(s.dma_start(out=t_rot[g_ % 2],
                                                in_=t_loc.ap()[g_ * 128:(g_ + 1) * 128, :]), 16)
        SY(h_ld)

        def h_comb(v, g_=g_):
            v.wait_ge(deg2[g_ % 2].h, M[("eg", g_)])
            v.wait_ge(dhl2[g_ % 2].h, M[("tl", g_)])
            if g_ < 2:
                v.wait_ge(dv.h, M["E_done_dv"])
            if ("yst", g_ - 2) in M:
                v.wait_ge(osy2[g_ % 2].h, M[("yst", g_ - 2)])
            M[("comb1", g_)] = dv.inc(v.scalar_tensor_tensor(
                x_rot[g_ % 2], eg_sb[g_ % 2], sb_afac[:, g_:g_ + 1], y1_rot[g_ % 2],
                ALU.mult, ALU.add))
        V(h_comb)

        def h_comb2(v, g_=g_):
            v.wait_ge(dv.h, M[("comb1", g_)])
            M[("comb2", g_)] = pcb.inc(v.scalar_tensor_tensor(
                x_rot[g_ % 2], t_rot[g_ % 2], sb_bfac[:, g_:g_ + 1], x_rot[g_ % 2],
                ALU.mult, ALU.add))
        V(h_comb2)

        def h_st(sc, g_=g_):
            sc.wait_ge(pcb.h, M[("comb2", g_)])
            M[("yst", g_)] = osy2[g_ % 2].inc(sc.dma_start(out=y_out[g_ * 128:(g_ + 1) * 128, :],
                                                   in_=x_rot[g_ % 2]), 16)
        SC(h_st)

    def fin(g):
        g.wait_ge(osy2[0].h, osy2[0].n)
        g.wait_ge(osy2[1].h, osy2[1].n)
    G(fin)

    return nc


_NC_CACHE = None


def kernel(L0_attn_x, L0_attn_Wq, L0_attn_Wk, L0_attn_Wv, L0_attn_Wo,
           L0_ffn_Wr, L0_ffn_W1, L0_ffn_W2):
    global _NC_CACHE
    x = np.ascontiguousarray(np.asarray(L0_attn_x, dtype=np.float32).reshape(NCORE * TPC, D))
    wqT = np.ascontiguousarray(np.asarray(L0_attn_Wq, dtype=np.float32).T)
    wkT = np.ascontiguousarray(np.asarray(L0_attn_Wk, dtype=np.float32).T)
    wvT = np.ascontiguousarray(np.asarray(L0_attn_Wv, dtype=np.float32).T)
    woT = np.ascontiguousarray(np.asarray(L0_attn_Wo, dtype=np.float32).T)
    wo2 = np.ascontiguousarray(np.asarray(L0_attn_Wo, dtype=np.float32))
    wr = np.ascontiguousarray(np.asarray(L0_ffn_Wr, dtype=np.float32))
    w1 = np.ascontiguousarray(np.asarray(L0_ffn_W1, dtype=np.float32))
    w2 = np.ascontiguousarray(np.asarray(L0_ffn_W2, dtype=np.float32))

    if _NC_CACHE is None:
        _NC_CACHE = build()
    nc = _NC_CACHE

    in_maps = []
    for c in range(NCORE):
        cmask = (np.arange(8) < c).astype(np.float32).reshape(8, 1)
        cvals = np.zeros((128, 2), np.float32)
        cvals[:, 0] = c * 512
        cvals[:, 1] = c * CAP
        in_maps.append({
            "x": x[c * TPC:(c + 1) * TPC],
            "wqT": wqT, "wkT": wkT, "wvT": wvT, "woT": woT, "wo2": wo2,
            "wr": wr,
            "w1c": np.ascontiguousarray(w1[c]),
            "w2c": np.ascontiguousarray(w2[c]),
            "cmask": cmask, "cvals": cvals,
        })
    res = run_bass_kernel_spmd(nc, in_maps, core_ids=list(range(NCORE)))
    out = np.concatenate([r["y"] for r in res.results], axis=0)
    return out.reshape(2, 8192, 1024)


# version 13
# speedup vs baseline: 1.0282x; 1.0282x over previous
import numpy as np
import ml_dtypes
from contextlib import ExitStack

import concourse.bass as bass
import concourse.mybir as mybir
from concourse.bass_utils import run_bass_kernel_spmd

F32 = mybir.dt.float32
BF16 = mybir.dt.bfloat16
U32 = mybir.dt.uint32
AF = mybir.ActivationFunctionType
ALU = mybir.AluOpType
AX = mybir.AxisListType
bf16 = ml_dtypes.bfloat16

NCORE = 8
D = 1024
E, DFF = 8, 4096
CAP = 2560
TPC = 2048
NT = 16
KB = 1024
EPS_ATTN = 1e-6
EPS_LN = 1e-5


class S:
    def __init__(self, ctx, nc, name):
        self.h = ctx.enter_context(nc.semaphore(name))
        self.n = 0

    def inc(self, inst, k=1):
        inst.then_inc(self.h, k)
        self.n += k
        return self.n


def build(upto=6):
    nc = bass.Bass("TRN2", target_bir_lowering=False)

    x_in = nc.dram_tensor("x", [TPC, D], F32, kind="ExternalInput").ap()
    wqT_in = nc.dram_tensor("wqT", [D, D], F32, kind="ExternalInput").ap()
    wkT_in = nc.dram_tensor("wkT", [D, D], F32, kind="ExternalInput").ap()
    wvT_in = nc.dram_tensor("wvT", [D, D], F32, kind="ExternalInput").ap()
    woT_in = nc.dram_tensor("woT", [D, D], F32, kind="ExternalInput").ap()
    wo2_in = nc.dram_tensor("wo2", [D, D], F32, kind="ExternalInput").ap()
    wr_in = nc.dram_tensor("wr", [D, E], F32, kind="ExternalInput").ap()
    w1_in = nc.dram_tensor("w1c", [D, DFF], F32, kind="ExternalInput").ap()
    w2_in = nc.dram_tensor("w2c", [DFF, D], F32, kind="ExternalInput").ap()
    cmask_in = nc.dram_tensor("cmask", [8, 1], F32, kind="ExternalInput").ap()
    cvals_in = nc.dram_tensor("cvals", [128, 2], F32, kind="ExternalInput").ap()

    y_out = nc.dram_tensor("y", [TPC, D], F32, kind="ExternalOutput").ap()

    ident_bf = nc.inline_tensor(np.eye(128, dtype=bf16), name="ident_bf").ap()
    ident_f = nc.inline_tensor(np.eye(128, dtype=np.float32), name="ident_f").ap()
    jj = np.arange(128)
    Lstrict_c = nc.inline_tensor((jj[:, None] < jj[None, :]).astype(bf16), name="Lstrict").ap()
    gidx_, eidx_ = np.arange(128) // 8, np.arange(128) % 8
    Pref_c = nc.inline_tensor(((eidx_[:, None] == eidx_[None, :]) &
                               (gidx_[:, None] < gidx_[None, :])).astype(bf16), name="Pref").ap()
    SumG_c = nc.inline_tensor((eidx_[:, None] == np.arange(8)[None, :]).astype(bf16), name="SumG").ap()
    TileE_c = nc.inline_tensor((np.arange(8)[:, None] == eidx_[None, :]).astype(np.float32),
                               name="TileE").ap()
    selp = np.zeros((16, 8 * 128), dtype=bf16)
    for j in range(8):
        selp[2 * j, j * 128:j * 128 + 64] = 1
        selp[2 * j + 1, j * 128 + 64:(j + 1) * 128] = 1
    SEL_c = nc.inline_tensor(selp, name="SELpack").ap()
    iota8_c = nc.inline_tensor(np.tile(np.arange(8, dtype=np.float32), (128, 1)), name="iota8").ap()
    _g = np.arange(16)[None, :]
    _p = np.arange(128)[:, None]
    tokid1_c = nc.inline_tensor(((_g // 4) * 4096 + (_g % 4) * 128 + _p + 1
                                 ).astype(np.float32), name="tokid1").ap()
    onescol_c = nc.inline_tensor(np.ones((128, 1), dtype=bf16), name="onescol").ap()

    y1_dram = nc.dram_tensor("y1_dram", [TPC, D], F32)
    t_loc = nc.dram_tensor("t_loc", [TPC, D], BF16)
    t_full = nc.dram_tensor("t_full", [NCORE * TPC, D], BF16, addr_space="Shared")
    eout_loc = nc.dram_tensor("eout_loc", [CAP, D], BF16)
    eout_full = nc.dram_tensor("eout_full", [NCORE * CAP, D], BF16, addr_space="Shared")
    kv_red_i = nc.dram_tensor("kv_red_i", [129, D], F32)
    kv_red_o = nc.dram_tensor("kv_red_o", [129, D], F32)
    cnt_i = nc.dram_tensor("cnt_i", [8, 1], F32)
    cnt_o = nc.dram_tensor("cnt_o", [64, 1], F32)
    id_disp = nc.dram_tensor("id_disp", [E * CAP], F32)
    id_full = nc.dram_tensor("id_full", [E * CAP], F32, addr_space="Shared")
    brow_d = nc.dram_tensor("brow_d", [1, 128], F32)

    ctx = ExitStack()
    EC = ctx.enter_context

    def sb(name, shape, dtype=F32):
        return EC(nc.sbuf_tensor(name, shape, dtype)).ap()

    sb_ident = sb("c_ident", [128, 128], BF16)
    sb_identf = sb("c_identf", [128, 128], F32)
    sb_L = sb("c_L", [128, 128], BF16)
    sb_pref = sb("c_pref", [128, 128], BF16)
    sb_sumg = sb("c_sumg", [128, 8], BF16)
    sb_tileE = sb("c_tileE", [8, 128], F32)
    sb_sel = sb("c_sel", [16, 1024], BF16)
    sb_iota8 = sb("c_iota8", [128, 8], F32)
    sb_tokid1 = sb("c_tokid1", [128, 16], F32)
    sb_ones_c = sb("c_onesc", [128, 1], BF16)
    sb_cmask = sb("c_cmask", [8, 1], F32)
    sb_cvals = sb("c_cvals", [128, 2], F32)
    sb_wr = sb("wr_sb", [128, 8, 8], BF16)

    x_rot = [sb(f"x{i}", [128, D], F32) for i in range(2)]
    xn_rot = [sb(f"xn{i}", [128, D], BF16) for i in range(2)]
    y1_rot = [sb(f"y1r{i}", [128, D], F32) for i in range(2)]
    t_rot = [sb(f"trt{i}", [128, D], BF16) for i in range(2)]
    sb_elu = [sb(f"eluscr{i}", [128, 512], F32) for i in range(2)]

    def cols16(name):
        return sb(name, [128, 16], F32)
    ln1 = {k: cols16("ln1" + k) for k in ("sum", "ssq", "m", "msq", "var", "lnl", "rstd", "std")}
    ln2 = {k: cols16("ln2" + k) for k in ("sum", "ssq", "m", "msq", "var", "lnl", "rstd")}
    sb_WW = sb("WW", [128, 8, 8], BF16)
    sb_Sbc = sb("Sbc", [128, 8], F32)
    sb_c1 = cols16("lgc1")
    sb_c2 = cols16("lgc2")

    sb_KS = sb("KS", [128, 8, 16], BF16)
    sb_kspad = sb("kspad", [128, 128], F32)

    sb_lgt = sb("lgt", [128, 128], F32)
    sb_eexp = sb("eexp", [128, 8], F32)
    sb_probs = sb("probs", [128, 8], F32)
    sb_top8 = sb("top8", [128, 8], F32)
    sb_idx8 = sb("idx8", [128, 8], U32)
    sb_nmx = cols16("nmx")
    sb_esum = cols16("esum")
    sb_ers = cols16("ers")
    sb_pm = cols16("pm")
    sb_idxf = cols16("idxf")
    sb_ohb = sb("ohb", [128, 128], BF16)
    sb_ohf = sb("ohf", [128, 128], F32)
    sb_totr = sb("totr", [1, 128], F32)
    sb_totp = sb("totp", [128, 128], F32)
    sb_totc = sb("totc", [128, 1], BF16)
    sb_cnt8 = sb("cnt8", [8, 1], F32)
    sb_cnts = sb("cnts", [8, 8], F32)
    sb_bt = sb("bt", [128, 1], F32)
    sb_bc2 = sb("bc2", [128, 1], F32)
    sb_bcol = sb("bcol", [128, 1], F32)
    sb_bpad = sb("bpad", [128, 128], F32)
    sb_brow = sb("brow_s", [1, 128], F32)
    sb_bbc = sb("bbc", [128, 128], F32)
    sb_posf = sb("posf", [128, 128], F32)
    sb_prod = sb("prod", [128, 128], F32)
    sb_psel = cols16("psel")
    sb_kept = cols16("kept")
    sb_plo = cols16("plo")
    sb_phi = cols16("phi")
    sb_posc = cols16("posc")
    sb_flat = cols16("flat")
    sb_flatu = sb("flatu", [128, 16], U32)
    sb_afac = cols16("afac")
    sb_bfac = cols16("bfac")
    sb_gid1 = cols16("gid1")
    sb_scf = cols16("scf")
    sb_scf2 = cols16("scf2")
    sb_scu = sb("scu", [128, 16], U32)
    sb_eps = sb("epsc", [128, 1], F32)

    sb_ids20 = sb("ids20", [20, 128], F32)
    sb_idx20 = sb("idx20", [20, 1], F32)
    sb_idx20u = sb("idx20u", [20, 1], U32)
    sb_idm1 = sb("idm1", [128, 20], F32)
    sb_gidx = sb("gidx_sb", [128, 20], U32)


    MTOT = 166 * KB
    MBASE = nc.sbuf_top - MTOT
    rem = nc.sbuf_bytes_remaining
    assert rem >= MTOT + 512, f"left allocations too big: remaining={rem}, need {MTOT}"

    def at(name, shape, dtype, rel):
        return nc.alloc_sbuf_tensor_at(name, shape, dtype, offset=MBASE + rel).ap()

    xnT = at("xnT", [128, 8, 1024], BF16, 0)
    wq_sb = at("wq_sb", [128, 8, 1024], BF16, 16 * KB)
    wk_sb = at("wk_sb", [128, 8, 1024], BF16, 32 * KB)
    wv_sb = at("wv_sb", [128, 8, 1024], BF16, 48 * KB)
    q_sb = at("q_sb", [128, 8, 2048], BF16, 64 * KB)
    ai_sb = at("ai_sb", [128, 8, 2048], BF16, 64 * KB)
    k_sb = at("k_sb", [128, 16, 1024], BF16, 96 * KB)
    v_sb = at("v_sb", [128, 16, 1024], BF16, 128 * KB)
    kv_sb = at("kv_sb", [128, 1024], F32, 160 * KB)
    h_sb = at("h_sb", [128, 32, 256], BF16, 0)
    w1_sb = at("w1_sb", [128, 6, 4096], BF16, 16 * KB)
    w1t_sb = at("w1t_sb", [128, 2, 4096], BF16, 96 * KB)
    wo_sb = at("wo_sb", [128, 8, 1024], BF16, 112 * KB)
    w2a_sb = at("w2a_sb", [128, 16, 1024], BF16, 64 * KB)
    w2b_sb = at("w2b_sb", [128, 8, 1024], BF16, 112 * KB)
    w2c_sb = at("w2c_sb", [128, 8, 1024], BF16, 128 * KB)
    tTe_sb = at("tTe_sb", [128, 8, 256], BF16, 144 * KB)
    gath_sb = [at(f"gathA{i}", [128, 1024], BF16, 148 * KB + i * 2 * KB) for i in range(2)]
    eo_sb = [at(f"eoA{i}", [128, 1024], BF16, 152 * KB + i * 2 * KB) for i in range(2)]
    eg_sb = [at(f"egA{i}", [128, 1024], BF16, 156 * KB + i * 2 * KB) for i in range(2)]
    sb_junk = at("junkA", [128, 1024], F32, 156 * KB)
    sb_ksrow = at("ksrowA", [1, 1024], F32, 0)
    sb_den = at("denA", [16, 512], F32, 4 * KB)
    sb_recb = at("recbA", [16, 512], BF16, 6 * KB)
    sb_bcs = at("bcsA", [128, 512], F32, 7 * KB)
    sb_lgev = at("lgevA", [8, 512], F32, 148 * KB)
    sb_kv2 = at("kv2A", [128, 8, 128], BF16, 9 * KB)
    sb_tts = at("ttsA", [128, 2, 512], BF16, 11 * KB)

    def w2_ap(fj):
        if fj < 16:
            return w2a_sb[:, fj, :]
        if fj < 24:
            return w2b_sb[:, fj - 16, :]
        return w2c_sb[:, fj - 24, :]

    def w1_ap(kj, f0, f1):
        if kj < 6:
            return w1_sb[:, kj, f0:f1]
        return w1t_sb[:, kj - 6, f0:f1]

    hc = S(ctx, nc, "hc")
    dw = S(ctx, nc, "dw")
    dw1 = S(ctx, nc, "dw1")
    dwo = S(ctx, nc, "dwo")
    dw2 = S(ctx, nc, "dw2")
    dkv = S(ctx, nc, "dkv")
    dcn = S(ctx, nc, "dcn")
    dsid = S(ctx, nc, "dsid")
    did = S(ctx, nc, "did")
    dbr = S(ctx, nc, "dbr")
    deo = S(ctx, nc, "deo")
    dts2 = [S(ctx, nc, f"dts{i}") for i in range(2)]
    dww2 = [S(ctx, nc, f"dww{i}") for i in range(2)]
    dwr = S(ctx, nc, "dwr")
    dwv = S(ctx, nc, "dwv")
    pcb = S(ctx, nc, "pcb")
    pst = S(ctx, nc, "pst")
    dtg2 = [S(ctx, nc, f"dtg{i}") for i in range(2)]
    deg2 = [S(ctx, nc, f"deg{i}") for i in range(2)]
    dx2 = [S(ctx, nc, f"dx{i}") for i in range(2)]
    dy2 = [S(ctx, nc, f"dy{i}") for i in range(2)]
    dt2 = [S(ctx, nc, f"dt{i}") for i in range(2)]
    dhl2 = [S(ctx, nc, f"dhl{i}") for i in range(2)]
    osy2 = [S(ctx, nc, f"osy{i}") for i in range(2)]
    pe = S(ctx, nc, "pe")
    dv = S(ctx, nc, "dv")
    ac = S(ctx, nc, "ac")
    po = S(ctx, nc, "po")
    c_kv = S(ctx, nc, "c_kv")
    c_cnt = S(ctx, nc, "c_cnt")
    c_id = S(ctx, nc, "c_id")
    c_tc = [S(ctx, nc, f"c_tc{i}") for i in range(4)]
    c_ec = [S(ctx, nc, f"c_ec{i}") for i in range(5)]

    ALLSEMS = [pe, dv, ac, po, hc, dw, dw1, dwo, dw2, dkv, dcn, dsid, did, dbr,
               deo, dwr, dwv, pcb, pst] + dww2 + dts2 + dtg2 + deg2 + dx2 + dy2 + dt2 + dhl2 + osy2
    ALLCOLL = [c_kv, c_cnt, c_id] + c_tc + c_ec

    block = EC(nc.Block())
    M = {}

    def finish_early():
        def flush(g):
            for s_ in ALLSEMS:
                if s_.n:
                    g.wait_ge(s_.h, s_.n)
            for s_ in ALLCOLL:
                if s_.n:
                    g.wait_ge(s_.h, 1)
        G(flush)
        return nc

    def T(f):
        block.tensor(f)

    def V(f):
        block.vector(f)

    def SC(f):
        block.scalar(f)

    def SY(f):
        block.sync(f)

    def G(f):
        block.gpsimd(f)

    def phA_eps(v):
        dv.inc(v.memset(sb_eps, EPS_LN))
    V(phA_eps)

    def phA_sync(s):
        for csrc, cdst in [(ident_bf, sb_ident), (ident_f, sb_identf), (Lstrict_c, sb_L),
                           (Pref_c, sb_pref), (SumG_c, sb_sumg), (TileE_c, sb_tileE),
                           (SEL_c, sb_sel), (iota8_c, sb_iota8), (tokid1_c, sb_tokid1),
                           (onescol_c, sb_ones_c), (cmask_in, sb_cmask), (cvals_in, sb_cvals)]:
            hc.inc(s.dma_start(out=cdst, in_=csrc), 16)
        M["consts"] = hc.n
    SY(phA_sync)

    def phA_g(g):
        for kj in range(8):
            dwr.inc(g.dma_start(out=sb_wr[:, kj, :], in_=wr_in[kj * 128:(kj + 1) * 128, :]), 16)
        M["wrdone"] = dwr.n
        po.inc(g.memset(sb_bpad, 0.0))
        g.wait_ge(po.h, po.n)
        dw.inc(g.dma_start(out=id_disp.ap().rearrange("(p t) -> p t", p=128)[:, 0:128],
                           in_=sb_bpad), 16)
        dw.inc(g.dma_start(out=id_disp.ap().rearrange("(p t) -> p t", p=128)[:, 128:160],
                           in_=sb_bpad[:, 0:32]), 16)
        M["idz"] = dw.n
        for kj in range(8):
            dw.inc(g.dma_start(out=wk_sb[:, kj, :], in_=wkT_in[kj * 128:(kj + 1) * 128, :]), 16)
        M["wkv"] = dw.n
        for kj in range(8):
            dwv.inc(g.dma_start(out=wv_sb[:, kj, :], in_=wvT_in[kj * 128:(kj + 1) * 128, :]), 16)
        M["wv"] = dwv.n
        for kj in range(8):
            dw.inc(g.dma_start(out=wq_sb[:, kj, :], in_=wqT_in[kj * 128:(kj + 1) * 128, :]), 16)
        M["wqkv"] = dw.n
    G(phA_g)

    wwtmp = [q_sb[:, i, 0:1024] for i in range(8)]

    def phA_wo2(g):
        for kj in range(8):
            dww2[0].inc(g.dma_start(out=wwtmp[kj],
                                    in_=wo2_in[kj * 128:(kj + 1) * 128, :]), 16)
    G(phA_wo2)

    def emit_ww():
        ctxA = ExitStack()
        psWW = ctxA.enter_context(nc.psum_tensor("psWW", [128, 128], F32))

        def phA_ww(t):
            t.wait_ge(dww2[0].h, 128)
            t.wait_ge(dv.h, M[("halfVdone", 0)])
            for mc in range(8):
                for kj in range(8):
                    t.matmul(psWW.ap()[:, mc * 8:(mc + 1) * 8],
                             wwtmp[kj][:, mc * 128:(mc + 1) * 128],
                             sb_wr[:, kj, :], start=(kj == 0), stop=(kj == 7))
            last = None
            for kj in range(8):
                last = t.matmul(psWW.ap()[0:1, 64:72], sb_ones_c, sb_wr[:, kj, :],
                                start=(kj == 0), stop=(kj == 7))
            pe.inc(last)
            M["WWmm"] = pe.n
        T(phA_ww)

        def phA_wwev(v):
            v.wait_ge(pe.h, M["WWmm"])
            dv.inc(v.tensor_copy(sb_WW.rearrange("p a b -> p (a b)"), psWW.ap()[:, 0:64]))
            dv.inc(v.tensor_copy(sb_brow[0:1, 0:8], psWW.ap()[0:1, 64:72]))
            M["WWev"] = dv.n
        V(phA_wwev)

        def phA_sbc(g):
            g.wait_ge(dv.h, M["WWev"])
            dbr.inc(g.dma_start(out=brow_d.ap()[0:1, 0:8], in_=sb_brow[0:1, 0:8]), 16)
            g.wait_ge(dbr.h, dbr.n)
            M["Sbc"] = dbr.inc(g.dma_start(
                out=sb_Sbc, in_=brow_d.ap()[0:1, 0:8].partition_broadcast(128)), 16)
        G(phA_sbc)
        ctxA.close()

    x_loaded = [0] * NT
    ln_applied = [0] * NT
    eluk = [y1_rot[0][:, 0:512], y1_rot[0][:, 512:1024]]

    for half in range(2):
        tiles = list(range(half * 8, half * 8 + 8))

        ctxB = ExitStack()
        psB = [ctxB.enter_context(nc.psum_tensor(f"psB{half}_{i}", [128, 128], BF16))
               for i in range(2)]
        psLgX = ctxB.enter_context(nc.psum_tensor(f"psLgX{half}", [128, 8], F32))
        psK2 = [ctxB.enter_context(nc.psum_tensor(f"psK{half}_{i}", [128, 512], F32))
                for i in range(2)]
        psV2 = [ctxB.enter_context(nc.psum_tensor(f"psV{half}_{i}", [128, 512], F32))
                for i in range(2)]

        def emit_ldx(g_):
            def ld_x(s, g_=g_):
                if g_ >= 2:
                    s.wait_ge(dv.h, ln_applied[g_ - 2])
                x_loaded[g_] = dx2[g_ % 2].inc(s.dma_start(out=x_rot[g_ % 2],
                                                  in_=x_in[g_ * 128:(g_ + 1) * 128, :]), 16)
            SY(ld_x)

        def emit_stats(g_):
            def ln_stats(v, g_=g_):
                xa = x_rot[g_ % 2]
                v.wait_ge(dx2[g_ % 2].h, x_loaded[g_])
                dv.inc(v.reduce_sum(ln1["sum"][:, g_:g_ + 1], xa, axis=AX.X))
                n1 = dv.inc(v.scalar_tensor_tensor(sb_junk, xa, 0.0, xa, ALU.add, ALU.mult,
                                                   accum_out=ln1["ssq"][:, g_:g_ + 1]))
                v.wait_ge(dv.h, n1)
                n2 = dv.inc(v.tensor_scalar(ln1["m"][:, g_:g_ + 1], ln1["sum"][:, g_:g_ + 1],
                                            1.0 / D, None, ALU.mult))
                v.wait_ge(dv.h, n2)
                n3 = dv.inc(v.tensor_mul(ln1["msq"][:, g_:g_ + 1], ln1["m"][:, g_:g_ + 1],
                                         ln1["m"][:, g_:g_ + 1]))
                v.wait_ge(dv.h, n3)
                M[("var1", g_)] = dv.inc(v.scalar_tensor_tensor(
                    ln1["var"][:, g_:g_ + 1], ln1["ssq"][:, g_:g_ + 1], 1.0 / D,
                    ln1["msq"][:, g_:g_ + 1], ALU.mult, ALU.subtract))
            V(ln_stats)

        def emit_rstd(g_):
            def sc_tile(sc, g_=g_):
                sc.wait_ge(dv.h, M[("var1", g_)])
                a1 = ac.inc(sc.activation(ln1["lnl"][:, g_:g_ + 1], ln1["var"][:, g_:g_ + 1],
                                          AF.Ln, bias=sb_eps[:, 0:1], scale=1.0))
                sc.wait_ge(ac.h, a1)
                ac.inc(sc.activation(ln1["std"][:, g_:g_ + 1], ln1["lnl"][:, g_:g_ + 1],
                                     AF.Exp, bias=0.0, scale=0.5))
                M[("rstd1", g_)] = ac.inc(sc.activation(ln1["rstd"][:, g_:g_ + 1],
                                                        ln1["lnl"][:, g_:g_ + 1],
                                                        AF.Exp, bias=0.0, scale=-0.5))
            SC(sc_tile)

        def emit_apply(g_):
            def ln_apply(v, g_=g_):
                v.wait_ge(ac.h, M[("rstd1", g_)])
                if ("trlast", g_ - 2) in M:
                    v.wait_ge(pe.h, M[("trlast", g_ - 2)])
                ln_applied[g_] = dv.inc(v.tensor_scalar(
                    xn_rot[g_ % 2], x_rot[g_ % 2], ln1["m"][:, g_:g_ + 1],
                    ln1["rstd"][:, g_:g_ + 1], ALU.subtract, ALU.mult))
                M[("xn1", g_)] = ln_applied[g_]
            V(ln_apply)

        emit_ldx(tiles[0])
        emit_stats(tiles[0])
        emit_rstd(tiles[0])
        emit_apply(tiles[0])
        for g_ in tiles:
            gl = g_ - half * 8
            if g_ + 1 <= tiles[-1]:
                emit_ldx(g_ + 1)

            for j in range(8):
                def tr_xn(t, g_=g_, j=j):
                    if g_ == 0 and j == 0:
                        t.wait_ge(hc.h, M["consts"])
                    if g_ == 8 and j == 0:
                        t.wait_ge(dv.h, M["half0_alldv"])
                    if j == 0:
                        t.wait_ge(dv.h, M[("xn1", g_)])
                    prev = ("trev", g_, j - 2) if j >= 2 else ("trev", g_ - 1, 6 + j)
                    if prev in M:
                        t.wait_ge(dv.h, M[prev])
                    M[("tr", g_, j)] = pe.inc(t.transpose(
                        psB[j % 2].ap(), xn_rot[g_ % 2][:, j * 128:(j + 1) * 128],
                        sb_ident))
                T(tr_xn)

                def tr_ev(v, g_=g_, gl=gl, j=j):
                    v.wait_ge(pe.h, M[("tr", g_, j)])
                    M[("trev", g_, j)] = dv.inc(v.tensor_copy(
                        xnT[:, j, gl * 128:(gl + 1) * 128], psB[j % 2].ap()))
                V(tr_ev)
            M[("trlast", g_)] = M[("tr", g_, 7)]

            def b_lga(t, g_=g_, gl=gl):
                if g_ == 0:
                    t.wait_ge(dwr.h, M["wrdone"])
                t.wait_ge(dv.h, M[("trev", g_, 7)])
                if ("lgaev", g_ - 1) in M:
                    t.wait_ge(dv.h, M[("lgaev", g_ - 1)])
                last = None
                for kj in range(8):
                    last = t.matmul(psLgX.ap(), xnT[:, kj, gl * 128:(gl + 1) * 128],
                                    sb_wr[:, kj, :], start=(kj == 0), stop=(kj == 7))
                M[("lgamm", g_)] = pe.inc(last)
            T(b_lga)

            def b_lgaev(v, g_=g_):
                v.wait_ge(pe.h, M[("lgamm", g_)])
                M[("lgaev", g_)] = dv.inc(v.tensor_copy(sb_lgt[:, g_ * 8:(g_ + 1) * 8],
                                                        psLgX.ap()))
            V(b_lgaev)

            for hh in range(2):
                def kmm(t, g_=g_, hh=hh, gl=gl):
                    if g_ == 0 and hh == 0:
                        t.wait_ge(dw.h, M["wkv"])
                    if ("kstt", g_ - 1, hh) in M:
                        t.wait_ge(dv.h, M[("kstt", g_ - 1, hh)])
                    last = None
                    for kj in range(8):
                        last = t.matmul(psK2[hh].ap(), xnT[:, kj, gl * 128:(gl + 1) * 128],
                                        wk_sb[:, kj, hh * 512:(hh + 1) * 512],
                                        start=(kj == 0), stop=(kj == 7))
                    M[("kmm", g_, hh)] = pe.inc(last)
                T(kmm)
            for hh in range(2):
                def vmm(t, g_=g_, hh=hh, gl=gl):
                    if g_ == 0 and hh == 0:
                        t.wait_ge(dwv.h, M["wv"])
                    if ("vev", g_ - 1, hh) in M:
                        t.wait_ge(dv.h, M[("vev", g_ - 1, hh)])
                    last = None
                    for kj in range(8):
                        last = t.matmul(psV2[hh].ap(), xnT[:, kj, gl * 128:(gl + 1) * 128],
                                        wv_sb[:, kj, hh * 512:(hh + 1) * 512],
                                        start=(kj == 0), stop=(kj == 7))
                    M[("vmm", g_, hh)] = pe.inc(last)
                T(vmm)

            if g_ + 1 <= tiles[-1]:
                emit_stats(g_ + 1)
                emit_rstd(g_ + 1)

            for hh in range(2):
                def kmin(v, g_=g_, hh=hh):
                    v.wait_ge(pe.h, M[("kmm", g_, hh)])
                    if g_ == 0 and hh == 0:
                        v.wait_ge(dw.h, M["idz"])
                    if ("kstt", g_ - 1, hh) in M:
                        v.wait_ge(dv.h, M[("kstt", g_ - 1, hh)])
                    M[("kmin", g_, hh)] = dv.inc(v.tensor_single_scalar(
                        eluk[hh], psK2[hh].ap(), 0.0, ALU.min))
                V(kmin)

            if g_ + 1 <= tiles[-1]:
                emit_apply(g_ + 1)

            for hh in range(2):
                def kexp(sc, g_=g_, hh=hh):
                    sc.wait_ge(dv.h, M[("kmin", g_, hh)])
                    M[("kexp", g_, hh)] = ac.inc(sc.activation(
                        eluk[hh], eluk[hh], AF.Exp, bias=0.0, scale=1.0))
                SC(kexp)
            for hh in range(2):
                def vev(v, g_=g_, hh=hh):
                    v.wait_ge(pe.h, M[("vmm", g_, hh)])
                    M[("vev", g_, hh)] = dv.inc(v.tensor_copy(
                        v_sb[:, g_, hh * 512:(hh + 1) * 512], psV2[hh].ap()))
                V(vev)
            for hh in range(2):
                def kstt(v, g_=g_, hh=hh):
                    v.wait_ge(ac.h, M[("kexp", g_, hh)])
                    M[("kstt", g_, hh)] = dv.inc(v.scalar_tensor_tensor(
                        k_sb[:, g_, hh * 512:(hh + 1) * 512], psK2[hh].ap(), 0.0,
                        eluk[hh], ALU.max, ALU.add))
                V(kstt)

        M[("halfVdone", half)] = dv.n
        ctxB.close()

        if half == 0:
            emit_ww()

        ctxQ = ExitStack()
        psQ = [ctxQ.enter_context(nc.psum_tensor(f"psQ{half}_{i}", [128, 512], F32))
               for i in range(2)]

        for s_ in range(2):
            for oj in range(8):
                def q_mm(t, s_=s_, oj=oj, half=half):
                    if s_ == 0 and oj == 0:
                        t.wait_ge(dw.h, M["wqkv"])
                        t.wait_ge(dv.h, M[("halfVdone", half)])
                        if half == 0:
                            t.wait_ge(dv.h, M["WWev"])
                    key = ("qev", half, s_, oj - 2) if oj >= 2 else \
                          (("qev", half, s_ - 1, 6 + oj) if s_ > 0 else ("qev", half - 1, 1, 6 + oj))
                    if key in M:
                        t.wait_ge(dv.h, M[key])
                    last = None
                    for kj in range(8):
                        last = t.matmul(psQ[oj % 2].ap(),
                                        wq_sb[:, kj, oj * 128:(oj + 1) * 128],
                                        xnT[:, kj, s_ * 512:(s_ + 1) * 512],
                                        start=(kj == 0), stop=(kj == 7))
                    M[("qmm", half, s_, oj)] = pe.inc(last)
                T(q_mm)

                def q_min(v, s_=s_, oj=oj, half=half):
                    i = (s_ * 8 + oj) % 2
                    v.wait_ge(pe.h, M[("qmm", half, s_, oj)])
                    key = ("qev", half, s_, oj - 2) if oj >= 2 else \
                          (("qev", half, s_ - 1, 6 + oj) if s_ > 0 else ("qev", half - 1, 1, 6 + oj))
                    if key in M:
                        v.wait_ge(dv.h, M[key])
                    M[("qmin", half, s_, oj)] = dv.inc(v.tensor_single_scalar(
                        sb_elu[i], psQ[oj % 2].ap(), 0.0, ALU.min))
                V(q_min)

                def q_exp(sc, s_=s_, oj=oj, half=half):
                    i = (s_ * 8 + oj) % 2
                    sc.wait_ge(dv.h, M[("qmin", half, s_, oj)])
                    M[("qexp", half, s_, oj)] = ac.inc(sc.activation(
                        sb_elu[i], sb_elu[i], AF.Exp, bias=0.0, scale=1.0))
                SC(q_exp)

                def q_ev(v, s_=s_, oj=oj, half=half):
                    i = (s_ * 8 + oj) % 2
                    v.wait_ge(ac.h, M[("qexp", half, s_, oj)])
                    if half == 0 and s_ == 0 and oj == 0:
                        v.wait_ge(pe.h, M["WWmm"])
                    c0 = half * 1024 + s_ * 512
                    M[("qev", half, s_, oj)] = dv.inc(v.scalar_tensor_tensor(
                        q_sb[:, oj, c0:c0 + 512], psQ[oj % 2].ap(), 0.0, sb_elu[i],
                        ALU.max, ALU.add))
                V(q_ev)
        ctxQ.close()

        if half == 0:
            M["half0_alldv"] = dv.n
            M["half0_dv"] = dv.n

    M["attn_proj_done_dv"] = dv.n
    M["attn_proj_done_pe"] = pe.n

    if upto <= 1:
        return finish_early()

    def w1_load(g):
        g.wait_ge(pe.h, M["attn_proj_done_pe"])
        for kj in range(6):
            dw1.inc(g.dma_start(out=w1_sb[:, kj, :], in_=w1_in[kj * 128:(kj + 1) * 128, :]), 16)
    G(w1_load)

    ctxD = ExitStack()
    psKVp = [ctxD.enter_context(nc.psum_tensor(f"psKVp{i}", [128, 512], F32)) for i in range(8)]

    def d_kv(t):
        t.wait_ge(dv.h, M["attn_proj_done_dv"])
        for j in range(8):
            last = None
            for g_ in range(NT):
                last = t.matmul(psKVp[j].ap()[:, 0:128],
                                k_sb[:, g_, j * 128:(j + 1) * 128],
                                v_sb[:, g_, j * 128:(j + 1) * 128],
                                start=(g_ == 0), stop=(g_ == NT - 1))
            M[("kvp", j)] = pe.inc(last)
    T(d_kv)

    def d_kv_ev(v):
        for j in range(8):
            v.wait_ge(pe.h, M[("kvp", j)])
            M[("kvpev", j)] = dv.inc(v.tensor_copy(kv_sb[:, j * 128:(j + 1) * 128],
                                                   psKVp[j].ap()[:, 0:128]))
        M["kvall_ev"] = dv.n
    V(d_kv_ev)
    ctxD.close()

    ctxD2 = ExitStack()
    psKS = ctxD2.enter_context(nc.psum_tensor("psKS", [1, 1024], F32))

    def d_ks(t):
        t.wait_ge(dv.h, M["kvall_ev"])
        for hh in range(2):
            last = None
            for g_ in range(NT):
                last = t.matmul(psKS.ap()[0:1, hh * 512:(hh + 1) * 512],
                                sb_ones_c, k_sb[:, g_, hh * 512:(hh + 1) * 512],
                                start=(g_ == 0), stop=(g_ == NT - 1))
            M[("ksum", hh)] = pe.inc(last)
    T(d_ks)

    def d_ks_ev(v):
        v.wait_ge(pe.h, M[("ksum", 1)])
        M["ksumev"] = dv.inc(v.tensor_copy(sb_ksrow, psKS.ap()[0:1, :]))
    V(d_ks_ev)
    ctxD2.close()

    def d_ar(g):
        g.wait_ge(dv.h, M["kvall_ev"])
        dkv.inc(g.dma_start(out=kv_red_i.ap()[0:128, :], in_=kv_sb), 16)
        g.wait_ge(dv.h, M["ksumev"])
        dkv.inc(g.dma_start(out=kv_red_i.ap()[128:129, :], in_=sb_ksrow), 16)
        g.wait_ge(dkv.h, dkv.n)
        c_kv.inc(g.collective_compute(
            "AllReduce", ALU.add, replica_groups=[[0, 1, 2, 3], [4, 5, 6, 7]],
            ins=[kv_red_i.ap().opt()], outs=[kv_red_o.ap().opt()]))
        g.wait_ge(c_kv.h, 1)
        dkv.inc(g.dma_start(out=kv_sb, in_=kv_red_o.ap()[0:128, :]), 16)
        M["kvback"] = dkv.inc(g.dma_start(out=sb_ksrow, in_=kv_red_o.ap()[128:129, :]), 16)
        for kj in range(2):
            dw1.inc(g.dma_start(out=w1t_sb[:, kj, :],
                                in_=w1_in[(6 + kj) * 128:(7 + kj) * 128, :]), 16)
        M["w1full"] = dw1.n
        for kj in range(8):
            dwo.inc(g.dma_start(out=wo_sb[:, kj, :], in_=woT_in[kj * 128:(kj + 1) * 128, :]), 16)
        M["wofull"] = dwo.n
        for fj in range(8):
            dw2.inc(g.dma_start(out=w2c_sb[:, fj, :],
                                in_=w2_in[(24 + fj) * 128:(25 + fj) * 128, :]), 16)
    G(d_ar)

    ctxD3 = ExitStack()
    psT1 = ctxD3.enter_context(nc.psum_tensor("psT1", [128, 512], F32))

    def d_kv2(v):
        v.wait_ge(dkv.h, M["kvback"])
        n0 = dv.inc(v.memset(sb_kv2[:, :, :], 0.0))
        v.wait_ge(dv.h, n0)
        for j in range(8):
            dv.inc(v.tensor_copy(sb_kv2[0:64, j, 0:64], kv_sb[0:64, j * 128:j * 128 + 64]))
            dv.inc(v.tensor_copy(sb_kv2[64:128, j, 64:128],
                                 kv_sb[64:128, j * 128 + 64:(j + 1) * 128]))
        M["kv2done"] = dv.n
        n1 = dv.inc(v.memset(sb_KS[:, :, :], 0.0))
        n2 = dv.inc(v.memset(sb_kspad, 0.0))
        v.wait_ge(dv.h, n2)
        M["kspadz"] = dv.n
    V(d_kv2)

    for j in range(8):
        def ks_row(v, j=j):
            if j == 0:
                v.wait_ge(dv.h, M["kspadz"])
            else:
                v.wait_ge(pe.h, M[("kstr", j - 1)])
            M[("kspadw", j)] = dv.inc(v.tensor_copy(sb_kspad[0:1, :],
                                                    sb_ksrow[0:1, j * 128:(j + 1) * 128]))
        V(ks_row)

        def ks_tr(t, j=j):
            t.wait_ge(dv.h, M[("kspadw", j)])
            if j > 0:
                t.wait_ge(dv.h, M[("kscopy", j - 1)])
            M[("kstr", j)] = pe.inc(t.transpose(psT1.ap()[:, 0:128], sb_kspad, sb_identf))
        T(ks_tr)

        def ks_cp(v, j=j):
            v.wait_ge(pe.h, M[("kstr", j)])
            dv.inc(v.tensor_copy(sb_KS[0:64, j, 2 * j:2 * j + 1], psT1.ap()[0:64, 0:1]))
            M[("kscopy", j)] = dv.inc(v.tensor_copy(sb_KS[64:128, j, 2 * j + 1:2 * j + 2],
                                                    psT1.ap()[64:128, 0:1]))
        V(ks_cp)
    M["KSdone"] = dv.n
    ctxD3.close()

    if upto <= 2:
        return finish_early()

    ctxE = ExitStack()
    psDen = ctxE.enter_context(nc.psum_tensor("psDen", [128, 512], F32))
    psNum = [ctxE.enter_context(nc.psum_tensor(f"psNum{i}", [128, 512], F32)) for i in range(2)]
    psBc = [ctxE.enter_context(nc.psum_tensor(f"psBc{i}", [128, 512], F32)) for i in range(2)]
    psWo = [ctxE.enter_context(nc.psum_tensor(f"psWo{i}", [128, 512], F32)) for i in range(2)]
    psLg = ctxE.enter_context(nc.psum_tensor("psLg", [128, 512], F32))

    for s_ in range(4):
        def e_den(t, s_=s_):
            if s_ == 0:
                t.wait_ge(dv.h, M["KSdone"])
                t.wait_ge(dv.h, M["kv2done"])
            else:
                t.wait_ge(dv.h, M[("denev", s_ - 1)])
            last = None
            for j in range(8):
                last = t.matmul(psDen.ap()[0:16, :], sb_KS[:, j, :],
                                q_sb[:, j, s_ * 512:(s_ + 1) * 512],
                                start=(j == 0), stop=(j == 7))
            M[("denmm", s_)] = pe.inc(last)
        T(e_den)

        def e_denev(v, s_=s_):
            v.wait_ge(pe.h, M[("denmm", s_)])
            n1 = dv.inc(v.tensor_scalar(sb_den, psDen.ap()[0:16, :], EPS_ATTN, None, ALU.add))
            v.wait_ge(dv.h, n1)
            n2 = dv.inc(v.reciprocal(sb_den, sb_den))
            v.wait_ge(dv.h, n2)
            M[("denev", s_)] = dv.inc(v.tensor_copy(sb_recb, sb_den))
        V(e_denev)

        for j in range(8):
            def e_num(t, s_=s_, j=j):
                if j == 0:
                    t.wait_ge(dv.h, M[("denev", s_)])
                keyn = ("ainum", s_, j - 2) if j >= 2 else ("ainum", s_ - 1, 6 + j)
                if keyn in M:
                    t.wait_ge(dv.h, M[keyn])
                M[("nummm", s_, j)] = pe.inc(t.matmul(psNum[j % 2].ap(), sb_kv2[:, j, :],
                                                      q_sb[:, j, s_ * 512:(s_ + 1) * 512],
                                                      start=True, stop=True))
                keyb = ("bcev", s_, j - 2) if j >= 2 else ("bcev", s_ - 1, 6 + j)
                if keyb in M:
                    t.wait_ge(dv.h, M[keyb])
                M[("bcmm", s_, j)] = pe.inc(t.matmul(psBc[j % 2].ap(),
                                                     sb_sel[:, j * 128:(j + 1) * 128],
                                                     sb_recb, start=True, stop=True))
            T(e_num)

            def e_bcev(v, s_=s_, j=j):
                v.wait_ge(pe.h, M[("bcmm", s_, j)])
                keym = ("ainum", s_, j - 1) if j >= 1 else ("ainum", s_ - 1, 7)
                if keym in M:
                    v.wait_ge(dv.h, M[keym])
                M[("bcev", s_, j)] = dv.inc(v.tensor_copy(sb_bcs, psBc[j % 2].ap()))
            V(e_bcev)

            def e_ai(v, s_=s_, j=j):
                v.wait_ge(pe.h, M[("nummm", s_, j)])
                v.wait_ge(dv.h, M[("bcev", s_, j)])
                M[("ainum", s_, j)] = dv.inc(v.tensor_mul(
                    ai_sb[:, j, s_ * 512:(s_ + 1) * 512], psNum[j % 2].ap(), sb_bcs))
            V(e_ai)
        M[("aislice", s_)] = dv.n

        for gg in range(4):
            g_ = s_ * 4 + gg

            def e_ldx(s, g_=g_, s_=s_):
                if g_ < 2:
                    s.wait_ge(dv.h, M["attn_proj_done_dv"])
                if ("y1add", g_ - 2) in M:
                    s.wait_ge(dv.h, M[("y1add", g_ - 2)])
                x_loaded[g_] = dx2[g_ % 2].inc(s.dma_start(out=x_rot[g_ % 2],
                                                  in_=x_in[g_ * 128:(g_ + 1) * 128, :]), 16)
            SY(e_ldx)

            def e_wo(t, g_=g_, s_=s_):
                if g_ == 0:
                    t.wait_ge(dwo.h, M["wofull"])
                t.wait_ge(dv.h, M[("aislice", s_)])
                if ("y1add", g_ - 1) in M:
                    t.wait_ge(dv.h, M[("y1add", g_ - 1)])
                last = None
                for hh in range(2):
                    for j in range(8):
                        last = t.matmul(psWo[hh].ap(), ai_sb[:, j, g_ * 128:(g_ + 1) * 128],
                                        wo_sb[:, j, hh * 512:(hh + 1) * 512],
                                        start=(j == 0), stop=(j == 7))
                M[("womm", g_)] = pe.inc(last)
            T(e_wo)

            def e_res(v, g_=g_):
                v.wait_ge(pe.h, M[("womm", g_)])
                v.wait_ge(dx2[g_ % 2].h, x_loaded[g_])
                if g_ == 0:
                    v.wait_ge(dw.h, M["wqkv"])
                if ("y1store", g_ - 2) in M:
                    v.wait_ge(dy2[g_ % 2].h, M[("y1store", g_ - 2)])
                dv.inc(v.tensor_add(y1_rot[g_ % 2][:, 0:512], psWo[0].ap(),
                                    x_rot[g_ % 2][:, 0:512]))
                n0 = dv.inc(v.tensor_add(y1_rot[g_ % 2][:, 512:1024], psWo[1].ap(),
                                         x_rot[g_ % 2][:, 512:1024]))
                M[("y1add", g_)] = n0
            V(e_res)

            def e_rstd(sc, g_=g_):
                ya = y1_rot[g_ % 2]
                gcol = ln2["m"][:, g_:g_ + 1]
                sc.wait_ge(dv.h, M[("y1add", g_)])
                a0 = ac.inc(sc.activation(sb_junk, ya, AF.Copy, bias=0.0, scale=1.0,
                                          accum_out=ln2["sum"][:, g_:g_ + 1]))
                sc.wait_ge(ac.h, a0)
                a1 = ac.inc(sc.activation(ln2["msq"][:, g_:g_ + 1], ln2["sum"][:, g_:g_ + 1],
                                          AF.Copy, bias=0.0, scale=-1.0 / D))
                a1b = ac.inc(sc.activation(gcol, ln2["sum"][:, g_:g_ + 1], AF.Copy,
                                           bias=0.0, scale=1.0 / D))
                sc.wait_ge(ac.h, a1b)
                a2 = ac.inc(sc.activation(sb_junk, ya, AF.Square,
                                          bias=ln2["msq"][:, g_:g_ + 1], scale=1.0,
                                          accum_out=ln2["ssq"][:, g_:g_ + 1]))
                sc.wait_ge(ac.h, a2)
                a3 = ac.inc(sc.activation(ln2["lnl"][:, g_:g_ + 1], ln2["ssq"][:, g_:g_ + 1],
                                          AF.Ln, bias=sb_eps[:, 0:1], scale=1.0 / D))
                sc.wait_ge(ac.h, a3)
                M[("rstd2", g_)] = ac.inc(sc.activation(ln2["rstd"][:, g_:g_ + 1],
                                                        ln2["lnl"][:, g_:g_ + 1],
                                                        AF.Exp, bias=0.0, scale=-0.5))
            SC(e_rstd)

            def e_t(v, g_=g_):
                v.wait_ge(ac.h, M[("rstd2", g_)])
                if ("tstore", g_ - 2) in M:
                    v.wait_ge(dt2[g_ % 2].h, M[("tstore", g_ - 2)])
                M[("t2", g_)] = dv.inc(v.tensor_scalar(
                    t_rot[g_ % 2], y1_rot[g_ % 2], ln2["m"][:, g_:g_ + 1],
                    ln2["rstd"][:, g_:g_ + 1], ALU.subtract, ALU.mult))
            V(e_t)

            def e_store(s, g_=g_):
                s.wait_ge(dv.h, M[("y1add", g_)])
                M[("y1store", g_)] = dy2[g_ % 2].inc(s.dma_start(
                    out=y1_dram.ap()[g_ * 128:(g_ + 1) * 128, :], in_=y1_rot[g_ % 2]), 16)
                s.wait_ge(dv.h, M[("t2", g_)])
                M[("tstore", g_)] = dt2[g_ % 2].inc(s.dma_start(
                    out=t_loc.ap()[g_ * 128:(g_ + 1) * 128, :], in_=t_rot[g_ % 2]), 16)
            SY(e_store)

        for gg in range(4):
            g_ = s_ * 4 + gg

            def e_lg(t, g_=g_, s_=s_):
                if g_ == 0:
                    t.wait_ge(dbr.h, M["Sbc"])
                t.wait_ge(dv.h, M[("aislice", s_)])
                if ("lgtev", g_ - 1) in M:
                    t.wait_ge(dv.h, M[("lgtev", g_ - 1)])
                last = None
                for j in range(8):
                    last = t.matmul(psLg.ap()[:, 0:8], ai_sb[:, j, g_ * 128:(g_ + 1) * 128],
                                    sb_WW[:, j, :], start=(j == 0), stop=(j == 7))
                M[("lgmm2", g_)] = pe.inc(last)
            T(e_lg)

            def e_lgc(v, g_=g_):
                v.wait_ge(pe.h, M[("lgmm2", g_)])
                v.wait_ge(ac.h, M[("rstd2", g_)])
                n0 = dv.inc(v.tensor_mul(sb_c1[:, g_:g_ + 1], ln1["std"][:, g_:g_ + 1],
                                         ln2["rstd"][:, g_:g_ + 1]))
                n1 = dv.inc(v.tensor_sub(sb_c2[:, g_:g_ + 1], ln1["m"][:, g_:g_ + 1],
                                         ln2["m"][:, g_:g_ + 1]))
                v.wait_ge(dv.h, n1)
                n2 = dv.inc(v.tensor_mul(sb_c2[:, g_:g_ + 1], sb_c2[:, g_:g_ + 1],
                                         ln2["rstd"][:, g_:g_ + 1]))
                v.wait_ge(dv.h, max(n0, n2))
                n3 = dv.inc(v.tensor_scalar(sb_lgt[:, g_ * 8:(g_ + 1) * 8],
                                            sb_lgt[:, g_ * 8:(g_ + 1) * 8],
                                            sb_c1[:, g_:g_ + 1], None, ALU.mult))
                v.wait_ge(dv.h, n3)
                n4 = dv.inc(v.scalar_tensor_tensor(
                    sb_lgt[:, g_ * 8:(g_ + 1) * 8], psLg.ap()[:, 0:8],
                    ln2["rstd"][:, g_:g_ + 1], sb_lgt[:, g_ * 8:(g_ + 1) * 8],
                    ALU.mult, ALU.add))
                v.wait_ge(dv.h, n4)
                M[("lgtev", g_)] = dv.inc(v.scalar_tensor_tensor(
                    sb_lgt[:, g_ * 8:(g_ + 1) * 8], sb_Sbc, sb_c2[:, g_:g_ + 1],
                    sb_lgt[:, g_ * 8:(g_ + 1) * 8], ALU.mult, ALU.add))
            V(e_lgc)

            def r_nmx(v, g_=g_):
                v.wait_ge(dv.h, M[("lgtev", g_)])
                n0 = dv.inc(v.reduce_max(sb_nmx[:, g_:g_ + 1], sb_lgt[:, g_ * 8:(g_ + 1) * 8],
                                         axis=AX.X))
                v.wait_ge(dv.h, n0)
                M[("nmx", g_)] = dv.inc(v.tensor_scalar(sb_nmx[:, g_:g_ + 1],
                                                        sb_nmx[:, g_:g_ + 1],
                                                        -1.0, None, ALU.mult))
            V(r_nmx)

            def r_exp(sc, g_=g_):
                sc.wait_ge(dv.h, M[("nmx", g_)])
                if ("probs", g_ - 1) in M:
                    sc.wait_ge(dv.h, M[("probs", g_ - 1)])
                M[("eexp", g_)] = ac.inc(sc.activation(sb_eexp, sb_lgt[:, g_ * 8:(g_ + 1) * 8],
                                                       AF.Exp, bias=sb_nmx[:, g_:g_ + 1],
                                                       scale=1.0))
            SC(r_exp)

            def r_rest(v, g_=g_):
                v.wait_ge(ac.h, M[("eexp", g_)])
                n1 = dv.inc(v.reduce_sum(sb_esum[:, g_:g_ + 1], sb_eexp, axis=AX.X))
                v.wait_ge(dv.h, n1)
                n2 = dv.inc(v.reciprocal(sb_ers[:, g_:g_ + 1], sb_esum[:, g_:g_ + 1]))
                v.wait_ge(dv.h, n2)
                M[("probs", g_)] = dv.inc(v.tensor_scalar(sb_probs, sb_eexp,
                                                          sb_ers[:, g_:g_ + 1], None, ALU.mult))
                v.wait_ge(dv.h, M[("probs", g_)])
                n3 = dv.inc(v.max(sb_top8, sb_probs))
                v.wait_ge(dv.h, n3)
                n4 = dv.inc(v.max_index(sb_idx8, sb_top8, sb_probs))
                v.wait_ge(dv.h, n4)
                dv.inc(v.tensor_copy(sb_pm[:, g_:g_ + 1], sb_top8[:, 0:1]))
                n5 = dv.inc(v.tensor_copy(sb_idxf[:, g_:g_ + 1], sb_idx8[:, 0:1]))
                v.wait_ge(dv.h, n5)
                n6 = dv.inc(v.tensor_scalar(sb_ohb[:, g_ * 8:(g_ + 1) * 8], sb_iota8,
                                            sb_idxf[:, g_:g_ + 1], None, ALU.is_equal))
                v.wait_ge(dv.h, n6)
                M[("oh", g_)] = dv.inc(v.tensor_copy(sb_ohf[:, g_ * 8:(g_ + 1) * 8],
                                                     sb_ohb[:, g_ * 8:(g_ + 1) * 8]))
            V(r_rest)

        def t_ag_chunk(g, s_=s_):
            g.wait_ge(dt2[0].h, M[("tstore", s_ * 4 + 2)])
            g.wait_ge(dt2[1].h, M[("tstore", s_ * 4 + 3)])
            c_tc[s_].inc(g.collective_compute(
                "AllGather", ALU.bypass, replica_groups=[list(range(NCORE))],
                ins=[t_loc.ap()[s_ * 512:(s_ + 1) * 512, :].opt()],
                outs=[t_full.ap()[s_ * 4096:(s_ + 1) * 4096, :].opt()]))
        G(t_ag_chunk)

    M["E_done_pe"] = pe.n
    M["E_done_dv"] = dv.n
    M["t_stored0"] = dt2[0].n
    M["t_stored1"] = dt2[1].n
    ctxE.close()

    if upto <= 3:
        return finish_early()

    def w2_load(g):
        g.wait_ge(pe.h, M["E_done_pe"])
        g.wait_ge(dv.h, M["E_done_dv"])
        for fj in range(16):
            dw2.inc(g.dma_start(out=w2a_sb[:, fj, :], in_=w2_in[fj * 128:(fj + 1) * 128, :]), 16)
        for fj in range(8):
            dw2.inc(g.dma_start(out=w2b_sb[:, fj, :],
                                in_=w2_in[(16 + fj) * 128:(17 + fj) * 128, :]), 16)
        M["w2full"] = dw2.n
    G(w2_load)

    ctxF = ExitStack()
    psPos = ctxF.enter_context(nc.psum_tensor("psPos", [128, 512], F32))
    psSm = [ctxF.enter_context(nc.psum_tensor(f"psSm{i}", [128, 512], F32)) for i in range(3)]

    M["oh_all"] = M[("oh", NT - 1)]

    def f_pos(t):
        t.wait_ge(dv.h, M["oh_all"])
        M["posmm"] = pe.inc(t.matmul(psPos.ap()[:, 0:128], sb_L, sb_ohb, start=True, stop=True))
        M["totmm"] = pe.inc(t.matmul(psSm[0].ap()[0:1, 0:128], sb_ones_c, sb_ohb,
                                     start=True, stop=True))
    T(f_pos)

    def f_tot(v):
        v.wait_ge(pe.h, M["totmm"])
        n0 = dv.inc(v.tensor_copy(sb_totr, psSm[0].ap()[0:1, 0:128]))
        n1 = dv.inc(v.memset(sb_totp, 0.0))
        v.wait_ge(dv.h, max(n0, n1))
        M["totp"] = dv.inc(v.tensor_copy(sb_totp[0:1, :], sb_totr))
    V(f_tot)

    def f_tottr(t):
        t.wait_ge(dv.h, M["totp"])
        M["tottr"] = pe.inc(t.transpose(psSm[1].ap()[:, 0:128], sb_totp, sb_identf))
    T(f_tottr)

    def f_totc(v):
        v.wait_ge(pe.h, M["tottr"])
        M["totc"] = dv.inc(v.tensor_copy(sb_totc, psSm[1].ap()[:, 0:1]))
    V(f_totc)

    def f_bt(t):
        t.wait_ge(dv.h, M["totc"])
        M["btmm"] = pe.inc(t.matmul(psSm[2].ap()[:, 0:1], sb_pref, sb_totc,
                                    start=True, stop=True))
        M["cntmm"] = pe.inc(t.matmul(psSm[0].ap()[0:8, 0:1], sb_sumg, sb_totc,
                                     start=True, stop=True))
    T(f_bt)

    def f_cnt(v):
        v.wait_ge(pe.h, M["cntmm"])
        M["cnt8"] = dv.inc(v.tensor_copy(sb_cnt8, psSm[0].ap()[0:8, 0:1]))
        v.wait_ge(pe.h, M["btmm"])
        M["btev"] = dv.inc(v.tensor_copy(sb_bt, psSm[2].ap()[:, 0:1]))
    V(f_cnt)

    def f_cntx(g):
        g.wait_ge(dv.h, M["cnt8"])
        dcn.inc(g.dma_start(out=cnt_i.ap(), in_=sb_cnt8), 16)
        g.wait_ge(dcn.h, dcn.n)
        c_cnt.inc(g.collective_compute(
            "AllGather", ALU.bypass, replica_groups=[list(range(NCORE))],
            ins=[cnt_i.ap().opt()], outs=[cnt_o.ap().opt()]))
        g.wait_ge(c_cnt.h, 1)
        M["cnts"] = dcn.inc(g.dma_start(out=sb_cnts,
                                       in_=cnt_o.ap().rearrange("(c e) o -> c (e o)", e=8)), 16)
    G(f_cntx)

    def f_bcore(t):
        t.wait_ge(dcn.h, M["cnts"])
        M["bcoremm"] = pe.inc(t.matmul(psSm[0].ap()[0:8, 0:1], sb_cnts, sb_cmask,
                                       start=True, stop=True))
    T(f_bcore)

    def f_bcore8(v):
        v.wait_ge(pe.h, M["bcoremm"])
        M["bcore8"] = dv.inc(v.tensor_copy(sb_cnt8, psSm[0].ap()[0:8, 0:1]))
    V(f_bcore8)

    def f_bc2(t):
        t.wait_ge(dv.h, M["bcore8"])
        M["bc2mm"] = pe.inc(t.matmul(psSm[1].ap()[:, 0:1], sb_tileE, sb_cnt8,
                                     start=True, stop=True))
    T(f_bc2)

    def f_bcol(v):
        v.wait_ge(pe.h, M["bc2mm"])
        n0 = dv.inc(v.tensor_copy(sb_bc2, psSm[1].ap()[:, 0:1]))
        v.wait_ge(dv.h, n0)
        n1 = dv.inc(v.tensor_add(sb_bcol, sb_bt, sb_bc2))
        n2 = dv.inc(v.memset(sb_bpad, 0.0))
        v.wait_ge(dv.h, max(n1, n2))
        M["bpad"] = dv.inc(v.tensor_copy(sb_bpad[:, 0:1], sb_bcol))
    V(f_bcol)

    def f_btr(t):
        t.wait_ge(dv.h, M["bpad"])
        M["btr"] = pe.inc(t.transpose(psSm[2].ap()[:, 0:128], sb_bpad, sb_identf))
    T(f_btr)

    def f_brow(v):
        v.wait_ge(pe.h, M["btr"])
        M["brow"] = dv.inc(v.tensor_copy(sb_brow, psSm[2].ap()[0:1, 0:128]))
    V(f_brow)

    def f_bbc(g):
        g.wait_ge(dv.h, M["brow"])
        dbr.inc(g.dma_start(out=brow_d.ap(), in_=sb_brow), 16)
        g.wait_ge(dbr.h, dbr.n)
        M["bbc"] = dbr.inc(g.dma_start(out=sb_bbc, in_=brow_d.ap().partition_broadcast(128)), 16)
    G(f_bbc)

    def f_flat(v):
        v.wait_ge(dbr.h, M["bbc"])
        v.wait_ge(pe.h, M["posmm"])
        n0 = dv.inc(v.tensor_add(sb_posf, psPos.ap()[:, 0:128], sb_bbc))
        v.wait_ge(dv.h, n0)
        n1 = dv.inc(v.tensor_mul(sb_prod, sb_posf, sb_ohf))
        v.wait_ge(dv.h, n1)
        n2 = dv.inc(v.reduce_sum(sb_psel, sb_prod.rearrange("p (g e) -> p g e", e=8), axis=AX.X))
        v.wait_ge(dv.h, n2)
        dv.inc(v.tensor_scalar(sb_kept, sb_psel, float(CAP), None, ALU.is_lt))
        n3 = dv.inc(v.tensor_scalar(sb_posc, sb_psel, float(CAP - 1), None, ALU.min))
        v.wait_ge(dv.h, n3)
        n4 = dv.inc(v.scalar_tensor_tensor(sb_flat, sb_idxf, float(CAP), sb_posc,
                                           ALU.mult, ALU.add))
        na = dv.inc(v.tensor_scalar(sb_plo, sb_posc, 512.0, None, ALU.is_ge))
        for thr in (1024.0, 1536.0, 2048.0):
            nb = dv.inc(v.tensor_scalar(sb_phi, sb_posc, thr, None, ALU.is_ge))
            v.wait_ge(dv.h, nb)
            na = dv.inc(v.tensor_add(sb_plo, sb_plo, sb_phi))
            v.wait_ge(dv.h, na)
        nc_ = dv.inc(v.scalar_tensor_tensor(sb_phi, sb_plo, 3584.0, sb_posc,
                                            ALU.mult, ALU.add))
        v.wait_ge(dv.h, nc_)
        nd = dv.inc(v.scalar_tensor_tensor(sb_phi, sb_idxf, 512.0, sb_phi,
                                           ALU.mult, ALU.add))
        v.wait_ge(dv.h, max(n4, nd))
        dv.inc(v.tensor_copy(sb_flatu, sb_phi))
        dv.inc(v.tensor_mul(sb_afac, sb_kept, sb_pm))
        dv.inc(v.tensor_scalar(sb_bfac, sb_kept, -1.0, 1.0, ALU.mult, ALU.add))
        dv.inc(v.tensor_scalar(sb_gid1, sb_tokid1, sb_cvals[:, 0:1], None, ALU.add))
        n7 = dv.inc(v.tensor_scalar(sb_scf, sb_flat, 1.0e6, None, ALU.add))
        v.wait_ge(dv.h, n7)
        n8 = dv.inc(v.scalar_tensor_tensor(sb_scf2, sb_kept, -1.0e6, sb_scf,
                                           ALU.mult, ALU.add))
        v.wait_ge(dv.h, n8)
        M["scu"] = dv.inc(v.tensor_copy(sb_scu, sb_scf2))
        M["routing_done"] = dv.n
    V(f_flat)
    ctxF.close()

    def f_scatter(g):
        g.wait_ge(dv.h, M["routing_done"])
        g.wait_ge(dw.h, M["wqkv"])
        for g_ in range(NT):
            dsid.inc(g.indirect_dma_start(
                out=id_disp.ap().rearrange("(a b) -> a b", b=1),
                out_offset=bass.IndirectOffsetOnAxis(sb_scu[:, g_:g_ + 1], 0),
                in_=sb_gid1[:, g_:g_ + 1], in_offset=None,
                bounds_check=E * CAP - 1, oob_is_err=False), 16)
        g.wait_ge(dsid.h, dsid.n)
        c_id.inc(g.collective_compute(
            "AllReduce", ALU.add, replica_groups=[list(range(NCORE))],
            ins=[id_disp.ap().rearrange("(a b) -> a b", b=128).opt()],
            outs=[id_full.ap().rearrange("(a b) -> a b", b=128).opt()]))
        po.inc(g.iota(sb_idx20, pattern=[[0, 1]], base=0, channel_multiplier=128,
                      allow_small_or_imprecise_dtypes=True))
        M["po_iota"] = po.n
    G(f_scatter)

    def f_idx20(v):
        v.wait_ge(po.h, M["po_iota"])
        n0 = dv.inc(v.tensor_scalar(sb_idx20, sb_idx20, sb_cvals[0:20, 1:2], None, ALU.add))
        v.wait_ge(dv.h, n0)
        M["idx20u"] = dv.inc(v.tensor_copy(sb_idx20u, sb_idx20))
    V(f_idx20)

    def f_ids(g):
        g.wait_ge(dv.h, M["idx20u"])
        g.wait_ge(c_id.h, 1)
        M["ids20"] = did.inc(g.indirect_dma_start(
            out=sb_ids20[:, :], out_offset=None,
            in_=id_full.ap().rearrange("(a b) -> a b", b=1),
            in_offset=bass.IndirectOffsetOnAxis(sb_idx20u[:, 0:1], 0)), 16)
    G(f_ids)

    ctxG0 = ExitStack()
    psG0 = ctxG0.enter_context(nc.psum_tensor("psG0", [128, 512], F32))

    def g_idpad(v):
        v.wait_ge(did.h, M["ids20"])
        n0 = dv.inc(v.memset(sb_bpad, 0.0))
        v.wait_ge(dv.h, n0)
        M["idpad"] = dv.inc(v.tensor_copy(sb_bpad[0:20, :], sb_ids20))
    V(g_idpad)

    def g_idtr(t):
        t.wait_ge(dv.h, M["idpad"])
        M["idtr"] = pe.inc(t.transpose(psG0.ap()[:, 0:128], sb_bpad, sb_identf))
    T(g_idtr)

    def g_gidx(v):
        v.wait_ge(pe.h, M["idtr"])
        n0 = dv.inc(v.tensor_scalar(sb_idm1, psG0.ap()[:, 0:20], 1.0, None, ALU.subtract))
        v.wait_ge(dv.h, n0)
        M["gidx"] = dv.inc(v.tensor_copy(sb_gidx, sb_idm1))
    V(g_gidx)
    ctxG0.close()

    if upto <= 4:
        return finish_early()

    ctxG = ExitStack()
    psTr = [ctxG.enter_context(nc.psum_tensor(f"psTrG{i}", [128, 1024], BF16)) for i in range(2)]
    psH = [ctxG.enter_context(nc.psum_tensor(f"psHG{i}", [128, 512], F32)) for i in range(2)]
    psEo = [ctxG.enter_context(nc.psum_tensor(f"psEG{i}", [128, 512], F32)) for i in range(4)]

    NST = CAP // 256

    for st in range(NST):
        for qq in range(2):
            r0 = st * 2 + qq

            def g_gath(s, st=st, qq=qq, r0=r0):
                if st == 0 and qq == 0:
                    s.wait_ge(dv.h, M["gidx"])
                    for cs in c_tc:
                        s.wait_ge(cs.h, 1)
                if ("tgathtr", st - 1, qq) in M:
                    s.wait_ge(pe.h, M[("tgathtr", st - 1, qq)])
                M[("tgath", st, qq)] = dtg2[qq].inc(s.indirect_dma_start(
                    out=gath_sb[qq], out_offset=None,
                    in_=t_full.ap(),
                    in_offset=bass.IndirectOffsetOnAxis(sb_gidx[:, r0:r0 + 1], 0),
                    bounds_check=NCORE * TPC - 1, oob_is_err=False), 16)
            G(g_gath)

            for j in range(8):
                def g_tr(t, st=st, qq=qq, j=j):
                    if j == 0:
                        t.wait_ge(dtg2[qq].h, M[("tgath", st, qq)])
                    prev = ("tgev", st, qq, j - 2) if j >= 2 else \
                           (("tgev", st, qq - 1, 6 + j) if qq > 0 else ("tgev", st - 1, 1, 6 + j))
                    if prev in M:
                        t.wait_ge(dv.h, M[prev])
                    M[("tgtr", st, qq, j)] = pe.inc(t.transpose(
                        psTr[j % 2].ap()[:, 0:128], gath_sb[qq][:, j * 128:(j + 1) * 128],
                        sb_ident))
                T(g_tr)

                def g_trev(v, st=st, qq=qq, j=j):
                    v.wait_ge(pe.h, M[("tgtr", st, qq, j)])
                    M[("tgev", st, qq, j)] = dv.inc(v.tensor_copy(
                        tTe_sb[:, j, qq * 128:(qq + 1) * 128], psTr[j % 2].ap()[:, 0:128]))
                V(g_trev)
            M[("tgathtr", st, qq)] = pe.n
            M[("tTe", st, qq)] = dv.n

        for fj in range(32):
            def g_h(t, st=st, fj=fj):
                if fj == 0:
                    t.wait_ge(dv.h, M[("tTe", st, 1)])
                    if st == 0:
                        t.wait_ge(dw1.h, M["w1full"])
                key = ("hev", st, fj - 2) if fj >= 2 else ("hev", st - 1, 30 + fj)
                if key in M:
                    t.wait_ge(dv.h, M[key])
                last = None
                for kj in range(8):
                    last = t.matmul(psH[fj % 2].ap()[:, 0:256], w1_ap(kj, fj * 128, (fj + 1) * 128),
                                    tTe_sb[:, kj, :], start=(kj == 0), stop=(kj == 7))
                M[("hmm", st, fj)] = pe.inc(last)
            T(g_h)

            def g_hev(v, st=st, fj=fj):
                v.wait_ge(pe.h, M[("hmm", st, fj)])
                if fj == 0:
                    if st == 0:
                        v.wait_ge(pe.h, M[("lgmm2", 15)])
                    if ("w2mm", st - 1) in M:
                        v.wait_ge(pe.h, M[("w2mm", st - 1)])
                M[("hev", st, fj)] = dv.inc(v.tensor_scalar(
                    h_sb[:, fj, :], psH[fj % 2].ap()[:, 0:256], 0.0, None, ALU.max))
            V(g_hev)
        M[("hall", st)] = dv.n

        def g_w2(t, st=st):
            t.wait_ge(dv.h, M[("hall", st)])
            if st == 0:
                t.wait_ge(dw2.h, M["w2full"])
            if ("eoev", st - 1) in M:
                t.wait_ge(dv.h, M[("eoev", st - 1)])
            last = None
            for rs in range(2):
                for hh in range(2):
                    for fj in range(32):
                        last = t.matmul(psEo[rs * 2 + hh].ap(),
                                        h_sb[:, fj, rs * 128:(rs + 1) * 128],
                                        w2_ap(fj)[:, hh * 512:(hh + 1) * 512],
                                        start=(fj == 0), stop=(fj == 31))
            M[("w2mm", st)] = pe.inc(last)
        T(g_w2)

        def g_eoev(v, st=st):
            v.wait_ge(pe.h, M[("w2mm", st)])
            if ("eostore", st - 1) in M:
                v.wait_ge(deo.h, M[("eostore", st - 1)])
            for rs in range(2):
                dv.inc(v.tensor_copy(eo_sb[rs][:, 0:512], psEo[rs * 2].ap()))
                dv.inc(v.tensor_copy(eo_sb[rs][:, 512:1024], psEo[rs * 2 + 1].ap()))
            M[("eoev", st)] = dv.n
        V(g_eoev)

        def g_eost(s, st=st):
            s.wait_ge(dv.h, M[("eoev", st)])
            for rs in range(2):
                deo.inc(s.dma_start(
                    out=eout_loc.ap()[st * 256 + rs * 128: st * 256 + (rs + 1) * 128, :],
                    in_=eo_sb[rs]), 16)
            M[("eostore", st)] = deo.n
        SY(g_eost)

        if st % 2 == 1:
            def eo_ag_chunk(g, st=st):
                k = st // 2
                g.wait_ge(deo.h, M[("eostore", st)])
                c_ec[k].inc(g.collective_compute(
                    "AllGather", ALU.bypass, replica_groups=[list(range(NCORE))],
                    ins=[eout_loc.ap()[k * 512:(k + 1) * 512, :].opt()],
                    outs=[eout_full.ap()[k * 4096:(k + 1) * 4096, :].opt()]))
            G(eo_ag_chunk)
    ctxG.close()

    if upto <= 5:
        return finish_early()

    def emit_eg(g_):
        def h_eg(g, g_=g_):
            if g_ == 0:
                for cs in c_ec:
                    g.wait_ge(cs.h, 1)
            if ("comb1", g_ - 2) in M:
                g.wait_ge(dv.h, M[("comb1", g_ - 2)])
            M[("eg", g_)] = deg2[g_ % 2].inc(g.indirect_dma_start(
                out=eg_sb[g_ % 2], out_offset=None,
                in_=eout_full.ap(),
                in_offset=bass.IndirectOffsetOnAxis(sb_flatu[:, g_:g_ + 1], 0)), 16)
        G(h_eg)

    emit_eg(0)
    for g_ in range(NT):
        if g_ + 1 < NT:
            emit_eg(g_ + 1)

        def h_ld(s, g_=g_):
            if g_ == 0:
                s.wait_ge(deo.h, M[("eostore", NST - 1)])
                s.wait_ge(dv.h, M["E_done_dv"])
                s.wait_ge(dy2[0].h, dy2[0].n)
                s.wait_ge(dy2[1].h, dy2[1].n)
            if ("comb2", g_ - 2) in M:
                s.wait_ge(pcb.h, M[("comb2", g_ - 2)])
            M[("y1l", g_)] = dhl2[g_ % 2].inc(s.dma_start(out=y1_rot[g_ % 2],
                                                 in_=y1_dram.ap()[g_ * 128:(g_ + 1) * 128, :]), 16)
            M[("tl", g_)] = dhl2[g_ % 2].inc(s.dma_start(out=t_rot[g_ % 2],
                                                in_=t_loc.ap()[g_ * 128:(g_ + 1) * 128, :]), 16)
        SY(h_ld)

        def h_comb(v, g_=g_):
            v.wait_ge(deg2[g_ % 2].h, M[("eg", g_)])
            v.wait_ge(dhl2[g_ % 2].h, M[("tl", g_)])
            if g_ < 2:
                v.wait_ge(dv.h, M["E_done_dv"])
            if ("yst", g_ - 2) in M:
                v.wait_ge(osy2[g_ % 2].h, M[("yst", g_ - 2)])
            M[("comb1", g_)] = dv.inc(v.scalar_tensor_tensor(
                x_rot[g_ % 2], eg_sb[g_ % 2], sb_afac[:, g_:g_ + 1], y1_rot[g_ % 2],
                ALU.mult, ALU.add))
        V(h_comb)

        def h_comb2(v, g_=g_):
            v.wait_ge(dv.h, M[("comb1", g_)])
            M[("comb2", g_)] = pcb.inc(v.scalar_tensor_tensor(
                x_rot[g_ % 2], t_rot[g_ % 2], sb_bfac[:, g_:g_ + 1], x_rot[g_ % 2],
                ALU.mult, ALU.add))
        V(h_comb2)

        def h_st(sc, g_=g_):
            sc.wait_ge(pcb.h, M[("comb2", g_)])
            M[("yst", g_)] = osy2[g_ % 2].inc(sc.dma_start(out=y_out[g_ * 128:(g_ + 1) * 128, :],
                                                   in_=x_rot[g_ % 2]), 16)
        SC(h_st)

    def fin(g):
        g.wait_ge(osy2[0].h, osy2[0].n)
        g.wait_ge(osy2[1].h, osy2[1].n)
    G(fin)

    return nc


_NC_CACHE = None


def kernel(L0_attn_x, L0_attn_Wq, L0_attn_Wk, L0_attn_Wv, L0_attn_Wo,
           L0_ffn_Wr, L0_ffn_W1, L0_ffn_W2):
    global _NC_CACHE
    x = np.ascontiguousarray(np.asarray(L0_attn_x, dtype=np.float32).reshape(NCORE * TPC, D))
    wqT = np.ascontiguousarray(np.asarray(L0_attn_Wq, dtype=np.float32).T)
    wkT = np.ascontiguousarray(np.asarray(L0_attn_Wk, dtype=np.float32).T)
    wvT = np.ascontiguousarray(np.asarray(L0_attn_Wv, dtype=np.float32).T)
    woT = np.ascontiguousarray(np.asarray(L0_attn_Wo, dtype=np.float32).T)
    wo2 = np.ascontiguousarray(np.asarray(L0_attn_Wo, dtype=np.float32))
    wr = np.ascontiguousarray(np.asarray(L0_ffn_Wr, dtype=np.float32))
    w1 = np.ascontiguousarray(np.asarray(L0_ffn_W1, dtype=np.float32))
    w2 = np.ascontiguousarray(np.asarray(L0_ffn_W2, dtype=np.float32))

    if _NC_CACHE is None:
        _NC_CACHE = build()
    nc = _NC_CACHE

    in_maps = []
    for c in range(NCORE):
        cmask = (np.arange(8) < c).astype(np.float32).reshape(8, 1)
        cvals = np.zeros((128, 2), np.float32)
        cvals[:, 0] = c * 512
        cvals[:, 1] = c * CAP
        in_maps.append({
            "x": x[c * TPC:(c + 1) * TPC],
            "wqT": wqT, "wkT": wkT, "wvT": wvT, "woT": woT, "wo2": wo2,
            "wr": wr,
            "w1c": np.ascontiguousarray(w1[c]),
            "w2c": np.ascontiguousarray(w2[c]),
            "cmask": cmask, "cvals": cvals,
        })
    res = run_bass_kernel_spmd(nc, in_maps, core_ids=list(range(NCORE)))
    out = np.concatenate([r["y"] for r in res.results], axis=0)
    return out.reshape(2, 8192, 1024)


# version 15 (final)
# speedup vs baseline: 1.1131x; 1.0825x over previous
import numpy as np
import ml_dtypes
from contextlib import ExitStack

import concourse.bass as bass
import concourse.mybir as mybir
from concourse.bass_utils import run_bass_kernel_spmd

F32 = mybir.dt.float32
BF16 = mybir.dt.bfloat16
U32 = mybir.dt.uint32
AF = mybir.ActivationFunctionType
ALU = mybir.AluOpType
AX = mybir.AxisListType
bf16 = ml_dtypes.bfloat16

NCORE = 8
D = 1024
E, DFF = 8, 4096
CAP = 2560
TPC = 2048
NT = 16
KB = 1024
EPS_ATTN = 1e-6
EPS_LN = 1e-5


class S:
    def __init__(self, ctx, nc, name):
        self.h = ctx.enter_context(nc.semaphore(name))
        self.n = 0

    def inc(self, inst, k=1):
        inst.then_inc(self.h, k)
        self.n += k
        return self.n


def build(upto=6):
    nc = bass.Bass("TRN2", target_bir_lowering=False)

    x_in = nc.dram_tensor("x", [TPC, D], F32, kind="ExternalInput").ap()
    wqT_in = nc.dram_tensor("wqT", [D, D], F32, kind="ExternalInput").ap()
    wkT_in = nc.dram_tensor("wkT", [D, D], F32, kind="ExternalInput").ap()
    wvT_in = nc.dram_tensor("wvT", [D, D], F32, kind="ExternalInput").ap()
    woT_in = nc.dram_tensor("woT", [D, D], F32, kind="ExternalInput").ap()
    wo2_in = nc.dram_tensor("wo2", [D, D], F32, kind="ExternalInput").ap()
    wr_in = nc.dram_tensor("wr", [D, E], F32, kind="ExternalInput").ap()
    w1_in = nc.dram_tensor("w1c", [D, DFF], F32, kind="ExternalInput").ap()
    w2_in = nc.dram_tensor("w2c", [DFF, D], F32, kind="ExternalInput").ap()
    cmask_in = nc.dram_tensor("cmask", [8, 1], F32, kind="ExternalInput").ap()
    cvals_in = nc.dram_tensor("cvals", [128, 2], F32, kind="ExternalInput").ap()

    y_out = nc.dram_tensor("y", [TPC, D], F32, kind="ExternalOutput").ap()

    ident_bf = nc.inline_tensor(np.eye(128, dtype=bf16), name="ident_bf").ap()
    ident_f = nc.inline_tensor(np.eye(128, dtype=np.float32), name="ident_f").ap()
    jj = np.arange(128)
    Lstrict_c = nc.inline_tensor((jj[:, None] < jj[None, :]).astype(bf16), name="Lstrict").ap()
    gidx_, eidx_ = np.arange(128) // 8, np.arange(128) % 8
    Pref_c = nc.inline_tensor(((eidx_[:, None] == eidx_[None, :]) &
                               (gidx_[:, None] < gidx_[None, :])).astype(bf16), name="Pref").ap()
    SumG_c = nc.inline_tensor((eidx_[:, None] == np.arange(8)[None, :]).astype(bf16), name="SumG").ap()
    TileE_c = nc.inline_tensor((np.arange(8)[:, None] == eidx_[None, :]).astype(np.float32),
                               name="TileE").ap()
    selp = np.zeros((16, 8 * 128), dtype=bf16)
    for j in range(8):
        selp[2 * j, j * 128:j * 128 + 64] = 1
        selp[2 * j + 1, j * 128 + 64:(j + 1) * 128] = 1
    SEL_c = nc.inline_tensor(selp, name="SELpack").ap()
    iota8_c = nc.inline_tensor(np.tile(np.arange(8, dtype=np.float32), (128, 1)), name="iota8").ap()
    _g = np.arange(16)[None, :]
    _p = np.arange(128)[:, None]
    tokid1_c = nc.inline_tensor(((_g // 4) * 4096 + (_g % 4) * 128 + _p + 1
                                 ).astype(np.float32), name="tokid1").ap()
    onescol_c = nc.inline_tensor(np.ones((128, 1), dtype=bf16), name="onescol").ap()
    onesrow_f = nc.inline_tensor(np.ones((1, 128), dtype=np.float32), name="onesrowf").ap()
    _p128 = np.arange(128)[:, None]
    _m128 = np.arange(128)[None, :]
    _j16 = (np.arange(128)[None, :] // 16)
    _m16 = (np.arange(128)[None, :] % 16)
    ksmask_c = nc.inline_tensor((_m16 == 2 * _j16 + (_p128 >= 64)).astype(np.float32),
                                name="ksmask").ap()

    y1_dram = nc.dram_tensor("y1_dram", [TPC, D], F32)
    t_loc = nc.dram_tensor("t_loc", [TPC, D], BF16)
    t_full = nc.dram_tensor("t_full", [NCORE * TPC, D], BF16, addr_space="Shared")
    eout_loc = nc.dram_tensor("eout_loc", [CAP, D], BF16)
    eout_full = nc.dram_tensor("eout_full", [NCORE * CAP, D], BF16, addr_space="Shared")
    kv_red_i = nc.dram_tensor("kv_red_i", [128, 1032], F32)
    kv_red_o = nc.dram_tensor("kv_red_o", [128, 1032], F32)
    cnt_i = nc.dram_tensor("cnt_i", [8, 1], F32)
    cnt_o = nc.dram_tensor("cnt_o", [64, 1], F32)
    id_disp = nc.dram_tensor("id_disp", [E * CAP], F32)
    id_full = nc.dram_tensor("id_full", [E * CAP], F32, addr_space="Shared")
    brow_d = nc.dram_tensor("brow_d", [1, 128], F32)

    ctx = ExitStack()
    EC = ctx.enter_context

    def sb(name, shape, dtype=F32):
        return EC(nc.sbuf_tensor(name, shape, dtype)).ap()

    sb_ident = sb("c_ident", [128, 128], BF16)
    sb_identf = sb("c_identf", [128, 128], F32)
    sb_L = sb("c_L", [128, 128], BF16)
    sb_pref = sb("c_pref", [128, 128], BF16)
    sb_sumg = sb("c_sumg", [128, 8], BF16)
    sb_tileE = sb("c_tileE", [8, 128], F32)
    sb_sel = sb("c_sel", [16, 1024], BF16)
    sb_iota8 = sb("c_iota8", [128, 8], F32)
    sb_tokid1 = sb("c_tokid1", [128, 16], F32)
    sb_ones_c = sb("c_onesc", [128, 1], BF16)
    sb_cmask = sb("c_cmask", [8, 1], F32)
    sb_cvals = sb("c_cvals", [128, 2], F32)
    sb_wr = sb("wr_sb", [128, 8, 8], BF16)

    x_rot = [sb(f"x{i}", [128, D], F32) for i in range(2)]
    xn_rot = [sb(f"xn{i}", [128, D], BF16) for i in range(2)]
    y1_rot = [sb(f"y1r{i}", [128, D], F32) for i in range(2)]
    t_rot = [sb(f"trt{i}", [128, D], BF16) for i in range(2)]
    sb_elu = [sb(f"eluscr{i}", [128, 512], F32) for i in range(2)]

    def cols16(name):
        return sb(name, [128, 16], F32)
    ln1 = {k: cols16("ln1" + k) for k in ("sum", "ssq", "m", "msq", "var", "lnl", "rstd", "std")}
    ln2 = {k: cols16("ln2" + k) for k in ("sum", "ssq", "m", "msq", "var", "lnl", "rstd")}
    sb_WW = sb("WW", [128, 8, 8], BF16)
    sb_Sbc = sb("Sbc", [128, 8], F32)
    sb_c1 = cols16("lgc1")
    sb_c2 = cols16("lgc2")

    sb_KS = sb("KS", [128, 8, 16], BF16)
    sb_onesrf = sb("c_onesrf", [1, 128], F32)
    sb_ksmask = sb("c_ksmask", [128, 8, 16], F32)
    sb_kscol = sb("kscol", [128, 8], F32)

    sb_lgt = sb("lgt", [128, 128], F32)
    sb_top8 = sb("top8", [128, 8], F32)
    sb_idx8 = sb("idx8", [128, 8], U32)
    sb_nmx = cols16("nmx")
    sb_esum = cols16("esum")
    sb_ers = cols16("ers")
    sb_pm = cols16("pm")
    sb_idxf = cols16("idxf")
    sb_ohb = sb("ohb", [128, 128], BF16)
    sb_ohf = sb("ohf", [128, 128], F32)
    sb_totr = sb("totr", [1, 128], F32)
    sb_totp = sb("totp", [128, 128], F32)
    sb_totc = sb("totc", [128, 1], BF16)
    sb_cnt8 = sb("cnt8", [8, 1], F32)
    sb_cnts = sb("cnts", [8, 8], F32)
    sb_bt = sb("bt", [128, 1], F32)
    sb_bc2 = sb("bc2", [128, 1], F32)
    sb_bcol = sb("bcol", [128, 1], F32)
    sb_bpad = sb("bpad", [128, 128], F32)
    sb_brow = sb("brow_s", [1, 128], F32)
    sb_bbc = sb("bbc", [128, 128], F32)
    sb_posf = sb("posf", [128, 128], F32)
    sb_prod = sb("prod", [128, 128], F32)
    sb_psel = cols16("psel")
    sb_kept = cols16("kept")
    sb_plo = cols16("plo")
    sb_phi = cols16("phi")
    sb_posc = cols16("posc")
    sb_flat = cols16("flat")
    sb_flatu = sb("flatu", [128, 16], U32)
    sb_afac = cols16("afac")
    sb_bfac = cols16("bfac")
    sb_gid1 = cols16("gid1")
    sb_scf = cols16("scf")
    sb_scf2 = cols16("scf2")
    sb_scu = sb("scu", [128, 16], U32)
    sb_eps = sb("epsc", [128, 1], F32)

    sb_ids20 = sb("ids20", [20, 128], F32)
    sb_idx20 = sb("idx20", [20, 1], F32)
    sb_idx20u = sb("idx20u", [20, 1], U32)
    sb_idm1 = sb("idm1", [128, 20], F32)
    sb_gidx = sb("gidx_sb", [128, 20], U32)


    MTOT = 166 * KB
    MBASE = nc.sbuf_top - MTOT
    rem = nc.sbuf_bytes_remaining
    assert rem >= MTOT + 128, f"left allocations too big: remaining={rem}, need {MTOT}"

    def at(name, shape, dtype, rel):
        return nc.alloc_sbuf_tensor_at(name, shape, dtype, offset=MBASE + rel).ap()

    xnT = at("xnT", [128, 8, 1024], BF16, 0)
    wq_sb = at("wq_sb", [128, 8, 1024], BF16, 16 * KB)
    wk_sb = at("wk_sb", [128, 8, 1024], BF16, 32 * KB)
    wv_sb = at("wv_sb", [128, 8, 1024], BF16, 48 * KB)
    q_sb = at("q_sb", [128, 8, 2048], BF16, 64 * KB)
    ai_sb = at("ai_sb", [128, 8, 2048], BF16, 64 * KB)
    k_sb = at("k_sb", [128, 16, 1024], BF16, 96 * KB)
    v_sb = at("v_sb", [128, 16, 1024], BF16, 128 * KB)
    kv_sb = at("kv_sb", [128, 1024], F32, 160 * KB)
    h_sb = at("h_sb", [128, 32, 256], BF16, 0)
    w1_sb = at("w1_sb", [128, 6, 4096], BF16, 16 * KB)
    w1t_sb = at("w1t_sb", [128, 2, 4096], BF16, 96 * KB)
    wo_sb = at("wo_sb", [128, 8, 1024], BF16, 112 * KB)
    w2a_sb = at("w2a_sb", [128, 16, 1024], BF16, 64 * KB)
    w2b_sb = at("w2b_sb", [128, 8, 1024], BF16, 112 * KB)
    w2c_sb = at("w2c_sb", [128, 8, 1024], BF16, 128 * KB)
    tTe_sb = at("tTe_sb", [128, 8, 256], BF16, 144 * KB)
    gath_sb = [at(f"gathA{i}", [128, 1024], BF16, 148 * KB + i * 2 * KB) for i in range(2)]
    eo_sb = [at(f"eoA{i}", [128, 1024], BF16, 152 * KB + i * 2 * KB) for i in range(2)]
    eg_sb = [at(f"egA{i}", [128, 1024], BF16, 156 * KB + i * 2 * KB) for i in range(2)]
    sb_junk = at("junkA", [128, 1024], F32, 156 * KB)
    sb_lgb = at("lgbA", [128, 128], F32, 12 * KB)
    sb_eexpB = at("eexpBA", [128, 128], F32, 12 * KB + 512)
    sb_den4 = at("den4A", [16, 4, 512], F32, 0)
    sb_recb4 = at("recb4A", [16, 4, 512], BF16, 160 * KB)
    sb_bcs = at("bcsA", [128, 512], F32, 10 * KB)
    sb_lgev = at("lgevA", [8, 512], F32, 148 * KB)
    sb_kv2 = at("kv2A", [128, 8, 128], BF16, 8 * KB)
    sb_tts = at("ttsA", [128, 2, 512], BF16, 11 * KB)

    def w2_ap(fj):
        if fj < 16:
            return w2a_sb[:, fj, :]
        if fj < 24:
            return w2b_sb[:, fj - 16, :]
        return w2c_sb[:, fj - 24, :]

    def w1_ap(kj, f0, f1):
        if kj < 6:
            return w1_sb[:, kj, f0:f1]
        return w1t_sb[:, kj - 6, f0:f1]

    hc = S(ctx, nc, "hc")
    dw = S(ctx, nc, "dw")
    dw1 = S(ctx, nc, "dw1")
    dwo = S(ctx, nc, "dwo")
    dw2 = S(ctx, nc, "dw2")
    dkv = S(ctx, nc, "dkv")
    dcn = S(ctx, nc, "dcn")
    dsid = S(ctx, nc, "dsid")
    did = S(ctx, nc, "did")
    dbr = S(ctx, nc, "dbr")
    deo = S(ctx, nc, "deo")
    dts2 = [S(ctx, nc, f"dts{i}") for i in range(2)]
    dww2 = [S(ctx, nc, f"dww{i}") for i in range(2)]
    dwr = S(ctx, nc, "dwr")
    dwv = S(ctx, nc, "dwv")
    pcb = S(ctx, nc, "pcb")
    pst = S(ctx, nc, "pst")
    dtg2 = [S(ctx, nc, f"dtg{i}") for i in range(2)]
    deg2 = [S(ctx, nc, f"deg{i}") for i in range(2)]
    dx2 = [S(ctx, nc, f"dx{i}") for i in range(2)]
    dy2 = [S(ctx, nc, f"dy{i}") for i in range(2)]
    dt2 = [S(ctx, nc, f"dt{i}") for i in range(2)]
    dhl2 = [S(ctx, nc, f"dhl{i}") for i in range(2)]
    osy2 = [S(ctx, nc, f"osy{i}") for i in range(2)]
    pe = S(ctx, nc, "pe")
    dv = S(ctx, nc, "dv")
    ac = S(ctx, nc, "ac")
    po = S(ctx, nc, "po")
    c_kv = S(ctx, nc, "c_kv")
    c_cnt = S(ctx, nc, "c_cnt")
    c_id = S(ctx, nc, "c_id")
    c_tc = [S(ctx, nc, f"c_tc{i}") for i in range(4)]
    c_ec = [S(ctx, nc, f"c_ec{i}") for i in range(5)]

    ALLSEMS = [pe, dv, ac, po, hc, dw, dw1, dwo, dw2, dkv, dcn, dsid, did, dbr,
               deo, dwr, dwv, pcb, pst] + dww2 + dts2 + dtg2 + deg2 + dx2 + dy2 + dt2 + dhl2 + osy2
    ALLCOLL = [c_kv, c_cnt, c_id] + c_tc + c_ec

    block = EC(nc.Block())
    M = {}

    def finish_early():
        def flush(g):
            for s_ in ALLSEMS:
                if s_.n:
                    g.wait_ge(s_.h, s_.n)
            for s_ in ALLCOLL:
                if s_.n:
                    g.wait_ge(s_.h, 1)
        G(flush)
        return nc

    def T(f):
        block.tensor(f)

    def V(f):
        block.vector(f)

    def SC(f):
        block.scalar(f)

    def SY(f):
        block.sync(f)

    def G(f):
        block.gpsimd(f)

    def phA_eps(v):
        dv.inc(v.memset(sb_eps, EPS_LN))
    V(phA_eps)

    def phA_sync(s):
        for csrc, cdst in [(ident_bf, sb_ident), (ident_f, sb_identf), (Lstrict_c, sb_L),
                           (Pref_c, sb_pref), (SumG_c, sb_sumg), (TileE_c, sb_tileE),
                           (SEL_c, sb_sel), (iota8_c, sb_iota8), (tokid1_c, sb_tokid1),
                           (onescol_c, sb_ones_c), (cmask_in, sb_cmask), (cvals_in, sb_cvals),
                           (onesrow_f, sb_onesrf), (ksmask_c, sb_ksmask)]:
            hc.inc(s.dma_start(out=cdst, in_=csrc), 16)
        M["consts"] = hc.n
    SY(phA_sync)

    def phA_g(g):
        for kj in range(8):
            dwr.inc(g.dma_start(out=sb_wr[:, kj, :], in_=wr_in[kj * 128:(kj + 1) * 128, :]), 16)
        M["wrdone"] = dwr.n
        po.inc(g.memset(sb_bpad, 0.0))
        g.wait_ge(po.h, po.n)
        dw.inc(g.dma_start(out=id_disp.ap().rearrange("(p t) -> p t", p=128)[:, 0:128],
                           in_=sb_bpad), 16)
        dw.inc(g.dma_start(out=id_disp.ap().rearrange("(p t) -> p t", p=128)[:, 128:160],
                           in_=sb_bpad[:, 0:32]), 16)
        M["idz"] = dw.n
        for kj in range(8):
            dw.inc(g.dma_start(out=wk_sb[:, kj, :], in_=wkT_in[kj * 128:(kj + 1) * 128, :]), 16)
        M["wkv"] = dw.n
        for kj in range(8):
            dwv.inc(g.dma_start(out=wv_sb[:, kj, :], in_=wvT_in[kj * 128:(kj + 1) * 128, :]), 16)
        M["wv"] = dwv.n
        for kj in range(8):
            dw.inc(g.dma_start(out=wq_sb[:, kj, :], in_=wqT_in[kj * 128:(kj + 1) * 128, :]), 16)
        M["wqkv"] = dw.n
    G(phA_g)

    wwtmp = [q_sb[:, i, 0:1024] for i in range(8)]

    def phA_wo2(g):
        for kj in range(8):
            dww2[0].inc(g.dma_start(out=wwtmp[kj],
                                    in_=wo2_in[kj * 128:(kj + 1) * 128, :]), 16)
    G(phA_wo2)

    def emit_ww():
        ctxA = ExitStack()
        psWW = ctxA.enter_context(nc.psum_tensor("psWW", [128, 128], F32))

        def phA_ww(t):
            t.wait_ge(dww2[0].h, 128)
            t.wait_ge(dv.h, M[("halfVdone", 0)])
            for mc in range(8):
                for kj in range(8):
                    t.matmul(psWW.ap()[:, mc * 8:(mc + 1) * 8],
                             wwtmp[kj][:, mc * 128:(mc + 1) * 128],
                             sb_wr[:, kj, :], start=(kj == 0), stop=(kj == 7))
            last = None
            for kj in range(8):
                last = t.matmul(psWW.ap()[0:1, 64:72], sb_ones_c, sb_wr[:, kj, :],
                                start=(kj == 0), stop=(kj == 7))
            pe.inc(last)
            M["WWmm"] = pe.n
        T(phA_ww)

        def phA_wwev(v):
            v.wait_ge(pe.h, M["WWmm"])
            dv.inc(v.tensor_copy(sb_WW.rearrange("p a b -> p (a b)"), psWW.ap()[:, 0:64]))
            dv.inc(v.tensor_copy(sb_brow[0:1, 0:8], psWW.ap()[0:1, 64:72]))
            M["WWev"] = dv.n
        V(phA_wwev)

        def phA_sbc(g):
            g.wait_ge(dv.h, M["WWev"])
            dbr.inc(g.dma_start(out=brow_d.ap()[0:1, 0:8], in_=sb_brow[0:1, 0:8]), 16)
            g.wait_ge(dbr.h, dbr.n)
            M["Sbc"] = dbr.inc(g.dma_start(
                out=sb_Sbc, in_=brow_d.ap()[0:1, 0:8].partition_broadcast(128)), 16)
        G(phA_sbc)
        ctxA.close()

    def emit_dkv():
        ctxD = ExitStack()
        psKV4 = [ctxD.enter_context(nc.psum_tensor(f"psKV4_{i}", [128, 512], F32))
                 for i in range(4)]
        psKC4 = [ctxD.enter_context(nc.psum_tensor(f"psKC4_{i}", [128, 8], F32))
                 for i in range(4)]

        for w in range(2):
            for j in range(4 * w, 4 * w + 4):
                def dkv_mm(t, j=j, w=w):
                    if j == 0:
                        t.wait_ge(dv.h, M[("halfVdone", 1)])
                    if w == 1:
                        t.wait_ge(dv.h, M[("kvpev", j - 4)])
                    last = None
                    for g_ in range(NT):
                        last = t.matmul(psKV4[j % 4].ap()[:, 0:128],
                                        k_sb[:, g_, j * 128:(j + 1) * 128],
                                        v_sb[:, g_, j * 128:(j + 1) * 128],
                                        start=(g_ == 0), stop=(g_ == NT - 1))
                    for g_ in range(NT):
                        last = t.matmul(psKC4[j % 4].ap()[:, 0:1],
                                        k_sb[:, g_, j * 128:(j + 1) * 128],
                                        sb_ones_c,
                                        start=(g_ == 0), stop=(g_ == NT - 1))
                    M[("kvp", j)] = pe.inc(last)
                T(dkv_mm)

                def dkv_ev(v, j=j):
                    v.wait_ge(pe.h, M[("kvp", j)])
                    dv.inc(v.tensor_copy(kv_sb[:, j * 128:(j + 1) * 128],
                                         psKV4[j % 4].ap()[:, 0:128]))
                    M[("kvpev", j)] = dv.inc(v.tensor_copy(sb_kscol[:, j:j + 1],
                                                           psKC4[j % 4].ap()[:, 0:1]))
                V(dkv_ev)
        M["kvall_ev"] = dv.n
        ctxD.close()

        def d_ar(g):
            g.wait_ge(dv.h, M["kvall_ev"])
            dkv.inc(g.dma_start(out=kv_red_i.ap()[:, 0:1024], in_=kv_sb), 16)
            dkv.inc(g.dma_start(out=kv_red_i.ap()[:, 1024:1032], in_=sb_kscol), 16)
            g.wait_ge(dkv.h, dkv.n)
            c_kv.inc(g.collective_compute(
                "AllReduce", ALU.add, replica_groups=[[0, 1, 2, 3], [4, 5, 6, 7]],
                ins=[kv_red_i.ap().opt()], outs=[kv_red_o.ap().opt()]))
            g.wait_ge(c_kv.h, 1)
            dkv.inc(g.dma_start(out=kv_sb, in_=kv_red_o.ap()[:, 0:1024]), 16)
            M["kvback"] = dkv.inc(g.dma_start(out=sb_kscol, in_=kv_red_o.ap()[:, 1024:1032]), 16)
            for kj in range(2):
                dw1.inc(g.dma_start(out=w1t_sb[:, kj, :],
                                    in_=w1_in[(6 + kj) * 128:(7 + kj) * 128, :]), 16)
            M["w1full"] = dw1.n
            for kj in range(8):
                dwo.inc(g.dma_start(out=wo_sb[:, kj, :], in_=woT_in[kj * 128:(kj + 1) * 128, :]), 16)
            M["wofull"] = dwo.n
            for fj in range(8):
                dw2.inc(g.dma_start(out=w2c_sb[:, fj, :],
                                    in_=w2_in[(24 + fj) * 128:(25 + fj) * 128, :]), 16)
        G(d_ar)


    x_loaded = [0] * NT
    ln_applied = [0] * NT
    eluk = [y1_rot[0][:, 0:512], y1_rot[0][:, 512:1024]]

    for half in range(2):
        tiles = list(range(half * 8, half * 8 + 8))

        ctxB = ExitStack()
        psB = [ctxB.enter_context(nc.psum_tensor(f"psB{half}_{i}", [128, 128], BF16))
               for i in range(2)]
        psLgX = ctxB.enter_context(nc.psum_tensor(f"psLgX{half}", [128, 8], F32))
        psK2 = [ctxB.enter_context(nc.psum_tensor(f"psK{half}_{i}", [128, 512], F32))
                for i in range(2)]
        psV2 = [ctxB.enter_context(nc.psum_tensor(f"psV{half}_{i}", [128, 512], F32))
                for i in range(2)]

        def emit_ldx(g_):
            def ld_x(s, g_=g_):
                if g_ >= 2:
                    s.wait_ge(dv.h, ln_applied[g_ - 2])
                x_loaded[g_] = dx2[g_ % 2].inc(s.dma_start(out=x_rot[g_ % 2],
                                                  in_=x_in[g_ * 128:(g_ + 1) * 128, :]), 16)
            SY(ld_x)

        def emit_stats(g_):
            def ln_stats(v, g_=g_):
                xa = x_rot[g_ % 2]
                v.wait_ge(dx2[g_ % 2].h, x_loaded[g_])
                dv.inc(v.reduce_sum(ln1["sum"][:, g_:g_ + 1], xa, axis=AX.X))
                n1 = dv.inc(v.scalar_tensor_tensor(sb_junk, xa, 0.0, xa, ALU.add, ALU.mult,
                                                   accum_out=ln1["ssq"][:, g_:g_ + 1]))
                v.wait_ge(dv.h, n1)
                n2 = dv.inc(v.tensor_scalar(ln1["m"][:, g_:g_ + 1], ln1["sum"][:, g_:g_ + 1],
                                            1.0 / D, None, ALU.mult))
                v.wait_ge(dv.h, n2)
                n3 = dv.inc(v.tensor_mul(ln1["msq"][:, g_:g_ + 1], ln1["m"][:, g_:g_ + 1],
                                         ln1["m"][:, g_:g_ + 1]))
                v.wait_ge(dv.h, n3)
                M[("var1", g_)] = dv.inc(v.scalar_tensor_tensor(
                    ln1["var"][:, g_:g_ + 1], ln1["ssq"][:, g_:g_ + 1], 1.0 / D,
                    ln1["msq"][:, g_:g_ + 1], ALU.mult, ALU.subtract))
            V(ln_stats)

        def emit_rstd(g_):
            def sc_tile(sc, g_=g_):
                sc.wait_ge(dv.h, M[("var1", g_)])
                a1 = ac.inc(sc.activation(ln1["lnl"][:, g_:g_ + 1], ln1["var"][:, g_:g_ + 1],
                                          AF.Ln, bias=sb_eps[:, 0:1], scale=1.0))
                sc.wait_ge(ac.h, a1)
                ac.inc(sc.activation(ln1["std"][:, g_:g_ + 1], ln1["lnl"][:, g_:g_ + 1],
                                     AF.Exp, bias=0.0, scale=0.5))
                M[("rstd1", g_)] = ac.inc(sc.activation(ln1["rstd"][:, g_:g_ + 1],
                                                        ln1["lnl"][:, g_:g_ + 1],
                                                        AF.Exp, bias=0.0, scale=-0.5))
            SC(sc_tile)

        def emit_apply(g_):
            def ln_apply(v, g_=g_):
                v.wait_ge(ac.h, M[("rstd1", g_)])
                if ("trlast", g_ - 2) in M:
                    v.wait_ge(pe.h, M[("trlast", g_ - 2)])
                ln_applied[g_] = dv.inc(v.tensor_scalar(
                    xn_rot[g_ % 2], x_rot[g_ % 2], ln1["m"][:, g_:g_ + 1],
                    ln1["rstd"][:, g_:g_ + 1], ALU.subtract, ALU.mult))
                M[("xn1", g_)] = ln_applied[g_]
            V(ln_apply)

        emit_ldx(tiles[0])
        emit_stats(tiles[0])
        emit_rstd(tiles[0])
        emit_apply(tiles[0])
        for g_ in tiles:
            gl = g_ - half * 8
            if g_ + 1 <= tiles[-1]:
                emit_ldx(g_ + 1)

            for j in range(8):
                def tr_xn(t, g_=g_, j=j):
                    if g_ == 0 and j == 0:
                        t.wait_ge(hc.h, M["consts"])
                    if g_ == 8 and j == 0:
                        t.wait_ge(dv.h, M["half0_alldv"])
                    if j == 0:
                        t.wait_ge(dv.h, M[("xn1", g_)])
                    prev = ("trev", g_, j - 2) if j >= 2 else ("trev", g_ - 1, 6 + j)
                    if prev in M:
                        t.wait_ge(dv.h, M[prev])
                    M[("tr", g_, j)] = pe.inc(t.transpose(
                        psB[j % 2].ap(), xn_rot[g_ % 2][:, j * 128:(j + 1) * 128],
                        sb_ident))
                T(tr_xn)

                def tr_ev(v, g_=g_, gl=gl, j=j):
                    v.wait_ge(pe.h, M[("tr", g_, j)])
                    M[("trev", g_, j)] = dv.inc(v.tensor_copy(
                        xnT[:, j, gl * 128:(gl + 1) * 128], psB[j % 2].ap()))
                V(tr_ev)
            M[("trlast", g_)] = M[("tr", g_, 7)]

            def b_lga(t, g_=g_, gl=gl):
                if g_ == 0:
                    t.wait_ge(dwr.h, M["wrdone"])
                t.wait_ge(dv.h, M[("trev", g_, 7)])
                if ("lgaev", g_ - 1) in M:
                    t.wait_ge(dv.h, M[("lgaev", g_ - 1)])
                last = None
                for kj in range(8):
                    last = t.matmul(psLgX.ap(), xnT[:, kj, gl * 128:(gl + 1) * 128],
                                    sb_wr[:, kj, :], start=(kj == 0), stop=(kj == 7))
                M[("lgamm", g_)] = pe.inc(last)
            T(b_lga)

            def b_lgaev(v, g_=g_):
                v.wait_ge(pe.h, M[("lgamm", g_)])
                M[("lgaev", g_)] = dv.inc(v.tensor_copy(sb_lgt[:, g_ * 8:(g_ + 1) * 8],
                                                        psLgX.ap()))
            V(b_lgaev)

            for hh in range(2):
                def kmm(t, g_=g_, hh=hh, gl=gl):
                    if g_ == 0 and hh == 0:
                        t.wait_ge(dw.h, M["wkv"])
                    if ("kstt", g_ - 1, hh) in M:
                        t.wait_ge(dv.h, M[("kstt", g_ - 1, hh)])
                    last = None
                    for kj in range(8):
                        last = t.matmul(psK2[hh].ap(), xnT[:, kj, gl * 128:(gl + 1) * 128],
                                        wk_sb[:, kj, hh * 512:(hh + 1) * 512],
                                        start=(kj == 0), stop=(kj == 7))
                    M[("kmm", g_, hh)] = pe.inc(last)
                T(kmm)
            for hh in range(2):
                def vmm(t, g_=g_, hh=hh, gl=gl):
                    if g_ == 0 and hh == 0:
                        t.wait_ge(dwv.h, M["wv"])
                    if ("vev", g_ - 1, hh) in M:
                        t.wait_ge(dv.h, M[("vev", g_ - 1, hh)])
                    last = None
                    for kj in range(8):
                        last = t.matmul(psV2[hh].ap(), xnT[:, kj, gl * 128:(gl + 1) * 128],
                                        wv_sb[:, kj, hh * 512:(hh + 1) * 512],
                                        start=(kj == 0), stop=(kj == 7))
                    M[("vmm", g_, hh)] = pe.inc(last)
                T(vmm)

            if g_ + 1 <= tiles[-1]:
                emit_stats(g_ + 1)
                emit_rstd(g_ + 1)

            for hh in range(2):
                def kmin(v, g_=g_, hh=hh):
                    v.wait_ge(pe.h, M[("kmm", g_, hh)])
                    if g_ == 0 and hh == 0:
                        v.wait_ge(dw.h, M["idz"])
                    if ("kstt", g_ - 1, hh) in M:
                        v.wait_ge(dv.h, M[("kstt", g_ - 1, hh)])
                    M[("kmin", g_, hh)] = dv.inc(v.tensor_single_scalar(
                        eluk[hh], psK2[hh].ap(), 0.0, ALU.min))
                V(kmin)

            if g_ + 1 <= tiles[-1]:
                emit_apply(g_ + 1)

            for hh in range(2):
                def kexp(sc, g_=g_, hh=hh):
                    sc.wait_ge(dv.h, M[("kmin", g_, hh)])
                    M[("kexp", g_, hh)] = ac.inc(sc.activation(
                        eluk[hh], eluk[hh], AF.Exp, bias=0.0, scale=1.0))
                SC(kexp)
            for hh in range(2):
                def vev(v, g_=g_, hh=hh):
                    v.wait_ge(pe.h, M[("vmm", g_, hh)])
                    M[("vev", g_, hh)] = dv.inc(v.tensor_copy(
                        v_sb[:, g_, hh * 512:(hh + 1) * 512], psV2[hh].ap()))
                V(vev)
            for hh in range(2):
                def kstt(v, g_=g_, hh=hh):
                    v.wait_ge(ac.h, M[("kexp", g_, hh)])
                    M[("kstt", g_, hh)] = dv.inc(v.scalar_tensor_tensor(
                        k_sb[:, g_, hh * 512:(hh + 1) * 512], psK2[hh].ap(), 0.0,
                        eluk[hh], ALU.max, ALU.add))
                V(kstt)

        M[("halfVdone", half)] = dv.n
        ctxB.close()

        if half == 0:
            emit_ww()
        if half == 1:
            emit_dkv()

        ctxQ = ExitStack()
        psQ = [ctxQ.enter_context(nc.psum_tensor(f"psQ{half}_{i}", [128, 512], F32))
               for i in range(2)]

        for s_ in range(2):
            for oj in range(8):
                def q_mm(t, s_=s_, oj=oj, half=half):
                    if s_ == 0 and oj == 0:
                        t.wait_ge(dw.h, M["wqkv"])
                        t.wait_ge(dv.h, M[("halfVdone", half)])
                        if half == 0:
                            t.wait_ge(dv.h, M["WWev"])
                        else:
                            t.wait_ge(dv.h, M["kvall_ev"])
                    key = ("qev", half, s_, oj - 2) if oj >= 2 else \
                          (("qev", half, s_ - 1, 6 + oj) if s_ > 0 else ("qev", half - 1, 1, 6 + oj))
                    if key in M:
                        t.wait_ge(dv.h, M[key])
                    last = None
                    for kj in range(8):
                        last = t.matmul(psQ[oj % 2].ap(),
                                        wq_sb[:, kj, oj * 128:(oj + 1) * 128],
                                        xnT[:, kj, s_ * 512:(s_ + 1) * 512],
                                        start=(kj == 0), stop=(kj == 7))
                    M[("qmm", half, s_, oj)] = pe.inc(last)
                T(q_mm)

                def q_min(v, s_=s_, oj=oj, half=half):
                    i = (s_ * 8 + oj) % 2
                    v.wait_ge(pe.h, M[("qmm", half, s_, oj)])
                    key = ("qev", half, s_, oj - 2) if oj >= 2 else \
                          (("qev", half, s_ - 1, 6 + oj) if s_ > 0 else ("qev", half - 1, 1, 6 + oj))
                    if key in M:
                        v.wait_ge(dv.h, M[key])
                    M[("qmin", half, s_, oj)] = dv.inc(v.tensor_single_scalar(
                        sb_elu[i], psQ[oj % 2].ap(), 0.0, ALU.min))
                V(q_min)

                def q_exp(sc, s_=s_, oj=oj, half=half):
                    i = (s_ * 8 + oj) % 2
                    sc.wait_ge(dv.h, M[("qmin", half, s_, oj)])
                    M[("qexp", half, s_, oj)] = ac.inc(sc.activation(
                        sb_elu[i], sb_elu[i], AF.Exp, bias=0.0, scale=1.0))
                SC(q_exp)

                def q_ev(v, s_=s_, oj=oj, half=half):
                    i = (s_ * 8 + oj) % 2
                    v.wait_ge(ac.h, M[("qexp", half, s_, oj)])
                    if half == 0 and s_ == 0 and oj == 0:
                        v.wait_ge(pe.h, M["WWmm"])
                    c0 = half * 1024 + s_ * 512
                    M[("qev", half, s_, oj)] = dv.inc(v.scalar_tensor_tensor(
                        q_sb[:, oj, c0:c0 + 512], psQ[oj % 2].ap(), 0.0, sb_elu[i],
                        ALU.max, ALU.add))
                V(q_ev)
        ctxQ.close()

        if half == 0:
            M["half0_alldv"] = dv.n
            M["half0_dv"] = dv.n

    M["attn_proj_done_dv"] = dv.n
    M["attn_proj_done_pe"] = pe.n

    if upto <= 1:
        return finish_early()

    def w1_load(g):
        g.wait_ge(pe.h, M["attn_proj_done_pe"])
        for kj in range(6):
            dw1.inc(g.dma_start(out=w1_sb[:, kj, :], in_=w1_in[kj * 128:(kj + 1) * 128, :]), 16)
    G(w1_load)

    def d_kv2(v):
        v.wait_ge(dkv.h, M["kvback"])
        n0 = dv.inc(v.memset(sb_kv2[:, :, :], 0.0))
        v.wait_ge(dv.h, n0)
        for j in range(8):
            dv.inc(v.tensor_copy(sb_kv2[0:64, j, 0:64], kv_sb[0:64, j * 128:j * 128 + 64]))
            dv.inc(v.tensor_copy(sb_kv2[64:128, j, 64:128],
                                 kv_sb[64:128, j * 128 + 64:(j + 1) * 128]))
        M["kv2done"] = dv.n
        for j in range(8):
            dv.inc(v.tensor_scalar(sb_KS[:, j, :], sb_ksmask[:, j, :],
                                   sb_kscol[:, j:j + 1], None, ALU.mult))
        M["KSdone"] = dv.n
    V(d_kv2)

    if upto <= 2:
        return finish_early()

    ctxEpre = ExitStack()
    psDenP = [ctxEpre.enter_context(nc.psum_tensor(f"psDenP{i}", [128, 512], F32))
              for i in range(2)]

    for s_ in range(4):
        def e_den(t, s_=s_):
            if s_ == 0:
                t.wait_ge(dv.h, M["KSdone"])
            if ("denev", s_ - 2) in M:
                t.wait_ge(dv.h, M[("denev", s_ - 2)])
            last = None
            for j in range(8):
                last = t.matmul(psDenP[s_ % 2].ap()[0:16, :], sb_KS[:, j, :],
                                q_sb[:, j, s_ * 512:(s_ + 1) * 512],
                                start=(j == 0), stop=(j == 7))
            M[("denmm", s_)] = pe.inc(last)
        T(e_den)

        def e_denev(v, s_=s_):
            v.wait_ge(pe.h, M[("denmm", s_)])
            n1 = dv.inc(v.tensor_scalar(sb_den4[:, s_, :], psDenP[s_ % 2].ap()[0:16, :],
                                        EPS_ATTN, None, ALU.add))
            v.wait_ge(dv.h, n1)
            n2 = dv.inc(v.reciprocal(sb_den4[:, s_, :], sb_den4[:, s_, :]))
            v.wait_ge(dv.h, n2)
            M[("denev", s_)] = dv.inc(v.tensor_copy(sb_recb4[:, s_, :], sb_den4[:, s_, :]))
        V(e_denev)
    M["denall"] = dv.n
    ctxEpre.close()

    ctxE = ExitStack()
    psNum = [ctxE.enter_context(nc.psum_tensor(f"psNum{i}", [128, 512], F32)) for i in range(2)]
    psBc = [ctxE.enter_context(nc.psum_tensor(f"psBc{i}", [128, 512], F32)) for i in range(2)]
    psWo = [ctxE.enter_context(nc.psum_tensor(f"psWo{i}", [128, 512], F32)) for i in range(2)]

    for s_ in range(4):
        for j in range(8):
            def e_num(t, s_=s_, j=j):
                if s_ == 0 and j == 0:
                    t.wait_ge(dv.h, M["denall"])
                    t.wait_ge(dv.h, M["kv2done"])
                keyn = ("ainum", s_, j - 2) if j >= 2 else ("ainum", s_ - 1, 6 + j)
                if keyn in M:
                    t.wait_ge(dv.h, M[keyn])
                M[("nummm", s_, j)] = pe.inc(t.matmul(psNum[j % 2].ap(), sb_kv2[:, j, :],
                                                      q_sb[:, j, s_ * 512:(s_ + 1) * 512],
                                                      start=True, stop=True))
                keyb = ("bcev", s_, j - 2) if j >= 2 else ("bcev", s_ - 1, 6 + j)
                if keyb in M:
                    t.wait_ge(dv.h, M[keyb])
                M[("bcmm", s_, j)] = pe.inc(t.matmul(psBc[j % 2].ap(),
                                                     sb_sel[:, j * 128:(j + 1) * 128],
                                                     sb_recb4[:, s_, :], start=True, stop=True))
            T(e_num)

            def e_bcev(v, s_=s_, j=j):
                v.wait_ge(pe.h, M[("bcmm", s_, j)])
                keym = ("ainum", s_, j - 1) if j >= 1 else ("ainum", s_ - 1, 7)
                if keym in M:
                    v.wait_ge(dv.h, M[keym])
                M[("bcev", s_, j)] = dv.inc(v.tensor_copy(sb_bcs, psBc[j % 2].ap()))
            V(e_bcev)

            def e_ai(v, s_=s_, j=j):
                v.wait_ge(pe.h, M[("nummm", s_, j)])
                v.wait_ge(dv.h, M[("bcev", s_, j)])
                M[("ainum", s_, j)] = dv.inc(v.tensor_mul(
                    ai_sb[:, j, s_ * 512:(s_ + 1) * 512], psNum[j % 2].ap(), sb_bcs))
            V(e_ai)
        M[("aislice", s_)] = dv.n

        for gg in range(4):
            g_ = s_ * 4 + gg

            def e_ldx(s, g_=g_, s_=s_):
                if g_ < 2:
                    s.wait_ge(dv.h, M["attn_proj_done_dv"])
                if ("y1add", g_ - 2) in M:
                    s.wait_ge(dv.h, M[("y1add", g_ - 2)])
                x_loaded[g_] = dx2[g_ % 2].inc(s.dma_start(out=x_rot[g_ % 2],
                                                  in_=x_in[g_ * 128:(g_ + 1) * 128, :]), 16)
            SY(e_ldx)

            def e_wo(t, g_=g_, s_=s_):
                if g_ == 0:
                    t.wait_ge(dwo.h, M["wofull"])
                t.wait_ge(dv.h, M[("aislice", s_)])
                if ("y1add", g_ - 1) in M:
                    t.wait_ge(dv.h, M[("y1add", g_ - 1)])
                last = None
                for hh in range(2):
                    for j in range(8):
                        last = t.matmul(psWo[hh].ap(), ai_sb[:, j, g_ * 128:(g_ + 1) * 128],
                                        wo_sb[:, j, hh * 512:(hh + 1) * 512],
                                        start=(j == 0), stop=(j == 7))
                M[("womm", g_)] = pe.inc(last)
            T(e_wo)

            def e_res(v, g_=g_):
                v.wait_ge(pe.h, M[("womm", g_)])
                v.wait_ge(dx2[g_ % 2].h, x_loaded[g_])
                if g_ == 0:
                    v.wait_ge(dw.h, M["wqkv"])
                if ("y1store", g_ - 2) in M:
                    v.wait_ge(dy2[g_ % 2].h, M[("y1store", g_ - 2)])
                dv.inc(v.tensor_add(y1_rot[g_ % 2][:, 0:512], psWo[0].ap(),
                                    x_rot[g_ % 2][:, 0:512]))
                n0 = dv.inc(v.tensor_add(y1_rot[g_ % 2][:, 512:1024], psWo[1].ap(),
                                         x_rot[g_ % 2][:, 512:1024]))
                M[("y1add", g_)] = n0
            V(e_res)

            def e_rstd(sc, g_=g_):
                ya = y1_rot[g_ % 2]
                gcol = ln2["m"][:, g_:g_ + 1]
                sc.wait_ge(dv.h, M[("y1add", g_)])
                a0 = ac.inc(sc.activation(sb_junk, ya, AF.Copy, bias=0.0, scale=1.0,
                                          accum_out=ln2["sum"][:, g_:g_ + 1]))
                sc.wait_ge(ac.h, a0)
                a1 = ac.inc(sc.activation(ln2["msq"][:, g_:g_ + 1], ln2["sum"][:, g_:g_ + 1],
                                          AF.Copy, bias=0.0, scale=-1.0 / D))
                a1b = ac.inc(sc.activation(gcol, ln2["sum"][:, g_:g_ + 1], AF.Copy,
                                           bias=0.0, scale=1.0 / D))
                sc.wait_ge(ac.h, a1b)
                a2 = ac.inc(sc.activation(sb_junk, ya, AF.Square,
                                          bias=ln2["msq"][:, g_:g_ + 1], scale=1.0,
                                          accum_out=ln2["ssq"][:, g_:g_ + 1]))
                sc.wait_ge(ac.h, a2)
                a3 = ac.inc(sc.activation(ln2["lnl"][:, g_:g_ + 1], ln2["ssq"][:, g_:g_ + 1],
                                          AF.Ln, bias=sb_eps[:, 0:1], scale=1.0 / D))
                sc.wait_ge(ac.h, a3)
                M[("rstd2", g_)] = ac.inc(sc.activation(ln2["rstd"][:, g_:g_ + 1],
                                                        ln2["lnl"][:, g_:g_ + 1],
                                                        AF.Exp, bias=0.0, scale=-0.5))
            SC(e_rstd)

            def e_t(v, g_=g_):
                v.wait_ge(ac.h, M[("rstd2", g_)])
                if ("tstore", g_ - 2) in M:
                    v.wait_ge(dt2[g_ % 2].h, M[("tstore", g_ - 2)])
                M[("t2", g_)] = dv.inc(v.tensor_scalar(
                    t_rot[g_ % 2], y1_rot[g_ % 2], ln2["m"][:, g_:g_ + 1],
                    ln2["rstd"][:, g_:g_ + 1], ALU.subtract, ALU.mult))
            V(e_t)

            def e_store(s, g_=g_):
                s.wait_ge(dv.h, M[("y1add", g_)])
                M[("y1store", g_)] = dy2[g_ % 2].inc(s.dma_start(
                    out=y1_dram.ap()[g_ * 128:(g_ + 1) * 128, :], in_=y1_rot[g_ % 2]), 16)
                s.wait_ge(dv.h, M[("t2", g_)])
                M[("tstore", g_)] = dt2[g_ % 2].inc(s.dma_start(
                    out=t_loc.ap()[g_ * 128:(g_ + 1) * 128, :], in_=t_rot[g_ % 2]), 16)
            SY(e_store)

        def t_ag_chunk(g, s_=s_):
            g.wait_ge(dt2[0].h, M[("tstore", s_ * 4 + 2)])
            g.wait_ge(dt2[1].h, M[("tstore", s_ * 4 + 3)])
            c_tc[s_].inc(g.collective_compute(
                "AllGather", ALU.bypass, replica_groups=[list(range(NCORE))],
                ins=[t_loc.ap()[s_ * 512:(s_ + 1) * 512, :].opt()],
                outs=[t_full.ap()[s_ * 4096:(s_ + 1) * 4096, :].opt()]))
        G(t_ag_chunk)

    M["E_dv_mark"] = dv.n
    ctxE.close()

    ctxR = ExitStack()
    psLgR = [ctxR.enter_context(nc.psum_tensor(f"psLgR{i}", [128, 8], F32)) for i in range(2)]

    for g_ in range(NT):
        def r_lg(t, g_=g_):
            if g_ == 0:
                t.wait_ge(dv.h, M["E_dv_mark"])
            if ("lgbev", g_ - 2) in M:
                t.wait_ge(dv.h, M[("lgbev", g_ - 2)])
            last = None
            for j in range(8):
                last = t.matmul(psLgR[g_ % 2].ap(), ai_sb[:, j, g_ * 128:(g_ + 1) * 128],
                                sb_WW[:, j, :], start=(j == 0), stop=(j == 7))
            M[("lgmm2", g_)] = pe.inc(last)
        T(r_lg)

        def r_lgbev(v, g_=g_):
            v.wait_ge(pe.h, M[("lgmm2", g_)])
            M[("lgbev", g_)] = dv.inc(v.tensor_copy(sb_lgb[:, g_ * 8:(g_ + 1) * 8],
                                                    psLgR[g_ % 2].ap()))
        V(r_lgbev)

    lgt3 = sb_lgt.rearrange("p (g e) -> p g e", e=8)
    lgb3 = sb_lgb.rearrange("p (g e) -> p g e", e=8)
    ee3 = sb_eexpB.rearrange("p (g e) -> p g e", e=8)

    def bc16(ap2):
        return ap2.rearrange("p (g o) -> p g o", o=1).broadcast_to([128, 16, 8])

    def r_comb(v):
        v.wait_ge(ac.h, M[("rstd2", 15)])
        v.wait_ge(dv.h, M[("lgbev", NT - 1)])
        v.wait_ge(dbr.h, M["Sbc"])
        n0 = dv.inc(v.tensor_mul(sb_c1[:, 0:16], ln1["std"][:, 0:16], ln2["rstd"][:, 0:16]))
        n1 = dv.inc(v.tensor_sub(sb_c2[:, 0:16], ln1["m"][:, 0:16], ln2["m"][:, 0:16]))
        v.wait_ge(dv.h, n1)
        n2 = dv.inc(v.tensor_mul(sb_c2[:, 0:16], sb_c2[:, 0:16], ln2["rstd"][:, 0:16]))
        v.wait_ge(dv.h, max(n0, n2))
        n3 = dv.inc(v.tensor_tensor(lgt3, lgt3, bc16(sb_c1[:, 0:16]), ALU.mult))
        n4 = dv.inc(v.tensor_tensor(lgb3, lgb3, bc16(ln2["rstd"][:, 0:16]), ALU.mult))
        v.wait_ge(dv.h, max(n3, n4))
        n5 = dv.inc(v.tensor_add(sb_lgt, sb_lgt, sb_lgb))
        v.wait_ge(dv.h, n5)
        for g_ in range(NT):
            dv.inc(v.scalar_tensor_tensor(
                sb_lgt[:, g_ * 8:(g_ + 1) * 8], sb_Sbc, sb_c2[:, g_:g_ + 1],
                sb_lgt[:, g_ * 8:(g_ + 1) * 8], ALU.mult, ALU.add))
        n6 = dv.n
        v.wait_ge(dv.h, n6)
        n7 = dv.inc(v.reduce_max(sb_nmx[:, 0:16], lgt3, axis=AX.X))
        v.wait_ge(dv.h, n7)
        M["lgshift"] = dv.inc(v.tensor_tensor(lgt3, lgt3, bc16(sb_nmx[:, 0:16]),
                                              ALU.subtract))
    V(r_comb)

    def r_expB(sc):
        sc.wait_ge(dv.h, M["lgshift"])
        M["expB"] = ac.inc(sc.activation(sb_eexpB, sb_lgt, AF.Exp, bias=0.0, scale=1.0))
    SC(r_expB)

    def r_probs(v):
        v.wait_ge(ac.h, M["expB"])
        n0 = dv.inc(v.reduce_sum(sb_esum[:, 0:16], ee3, axis=AX.X))
        v.wait_ge(dv.h, n0)
        n1 = dv.inc(v.reciprocal(sb_ers[:, 0:16], sb_esum[:, 0:16]))
        v.wait_ge(dv.h, n1)
        M["probsB"] = dv.inc(v.tensor_tensor(ee3, ee3, bc16(sb_ers[:, 0:16]), ALU.mult))
    V(r_probs)

    for g_ in range(NT):
        def r_top(v, g_=g_):
            if g_ == 0:
                v.wait_ge(dv.h, M["probsB"])
            n3 = dv.inc(v.max(sb_top8, sb_eexpB[:, g_ * 8:(g_ + 1) * 8]))
            v.wait_ge(dv.h, n3)
            n4 = dv.inc(v.max_index(sb_idx8, sb_top8, sb_eexpB[:, g_ * 8:(g_ + 1) * 8]))
            v.wait_ge(dv.h, n4)
            dv.inc(v.tensor_copy(sb_pm[:, g_:g_ + 1], sb_top8[:, 0:1]))
            M[("idxf", g_)] = dv.inc(v.tensor_copy(sb_idxf[:, g_:g_ + 1], sb_idx8[:, 0:1]))
        V(r_top)

    def r_oh(v):
        v.wait_ge(dv.h, M[("idxf", NT - 1)])
        oh3 = sb_ohb.rearrange("p (g e) -> p g e", e=8)
        iota3 = sb_iota8.rearrange("p (o e) -> p o e", o=1).broadcast_to([128, 16, 8])
        n6 = dv.inc(v.tensor_tensor(oh3, iota3, bc16(sb_idxf[:, 0:16]), ALU.is_equal))
        v.wait_ge(dv.h, n6)
        M["oh_all"] = dv.inc(v.tensor_copy(sb_ohf, sb_ohb))
    V(r_oh)
    ctxR.close()

    M["E_done_pe"] = pe.n
    M["E_done_dv"] = dv.n
    M["t_stored0"] = dt2[0].n
    M["t_stored1"] = dt2[1].n

    if upto <= 3:
        return finish_early()

    def w2_load(g):
        g.wait_ge(pe.h, M["E_done_pe"])
        g.wait_ge(dv.h, M["E_done_dv"])
        for fj in range(16):
            dw2.inc(g.dma_start(out=w2a_sb[:, fj, :], in_=w2_in[fj * 128:(fj + 1) * 128, :]), 16)
        for fj in range(8):
            dw2.inc(g.dma_start(out=w2b_sb[:, fj, :],
                                in_=w2_in[(16 + fj) * 128:(17 + fj) * 128, :]), 16)
        M["w2full"] = dw2.n
    G(w2_load)

    ctxF = ExitStack()
    psPos = ctxF.enter_context(nc.psum_tensor("psPos", [128, 512], F32))
    psSm = [ctxF.enter_context(nc.psum_tensor(f"psSm{i}", [128, 512], F32)) for i in range(3)]


    def f_pos(t):
        t.wait_ge(dv.h, M["oh_all"])
        M["posmm"] = pe.inc(t.matmul(psPos.ap()[:, 0:128], sb_L, sb_ohb, start=True, stop=True))
        M["totmm"] = pe.inc(t.matmul(psSm[0].ap()[0:1, 0:128], sb_ones_c, sb_ohb,
                                     start=True, stop=True))
    T(f_pos)

    def f_tot(v):
        v.wait_ge(pe.h, M["totmm"])
        n0 = dv.inc(v.tensor_copy(sb_totr, psSm[0].ap()[0:1, 0:128]))
        n1 = dv.inc(v.memset(sb_totp, 0.0))
        v.wait_ge(dv.h, max(n0, n1))
        M["totp"] = dv.inc(v.tensor_copy(sb_totp[0:1, :], sb_totr))
    V(f_tot)

    def f_tottr(t):
        t.wait_ge(dv.h, M["totp"])
        M["tottr"] = pe.inc(t.transpose(psSm[1].ap()[:, 0:128], sb_totp, sb_identf))
    T(f_tottr)

    def f_totc(v):
        v.wait_ge(pe.h, M["tottr"])
        M["totc"] = dv.inc(v.tensor_copy(sb_totc, psSm[1].ap()[:, 0:1]))
    V(f_totc)

    def f_bt(t):
        t.wait_ge(dv.h, M["totc"])
        M["btmm"] = pe.inc(t.matmul(psSm[2].ap()[:, 0:1], sb_pref, sb_totc,
                                    start=True, stop=True))
        M["cntmm"] = pe.inc(t.matmul(psSm[0].ap()[0:8, 0:1], sb_sumg, sb_totc,
                                     start=True, stop=True))
    T(f_bt)

    def f_cnt(v):
        v.wait_ge(pe.h, M["cntmm"])
        M["cnt8"] = dv.inc(v.tensor_copy(sb_cnt8, psSm[0].ap()[0:8, 0:1]))
        v.wait_ge(pe.h, M["btmm"])
        M["btev"] = dv.inc(v.tensor_copy(sb_bt, psSm[2].ap()[:, 0:1]))
    V(f_cnt)

    def f_cntx(g):
        g.wait_ge(dv.h, M["cnt8"])
        dcn.inc(g.dma_start(out=cnt_i.ap(), in_=sb_cnt8), 16)
        g.wait_ge(dcn.h, dcn.n)
        c_cnt.inc(g.collective_compute(
            "AllGather", ALU.bypass, replica_groups=[list(range(NCORE))],
            ins=[cnt_i.ap().opt()], outs=[cnt_o.ap().opt()]))
        g.wait_ge(c_cnt.h, 1)
        M["cnts"] = dcn.inc(g.dma_start(out=sb_cnts,
                                       in_=cnt_o.ap().rearrange("(c e) o -> c (e o)", e=8)), 16)
    G(f_cntx)

    def f_bcore(t):
        t.wait_ge(dcn.h, M["cnts"])
        M["bcoremm"] = pe.inc(t.matmul(psSm[0].ap()[0:8, 0:1], sb_cnts, sb_cmask,
                                       start=True, stop=True))
    T(f_bcore)

    def f_bcore8(v):
        v.wait_ge(pe.h, M["bcoremm"])
        M["bcore8"] = dv.inc(v.tensor_copy(sb_cnt8, psSm[0].ap()[0:8, 0:1]))
    V(f_bcore8)

    def f_bc2(t):
        t.wait_ge(dv.h, M["bcore8"])
        M["bc2mm"] = pe.inc(t.matmul(psSm[1].ap()[:, 0:1], sb_tileE, sb_cnt8,
                                     start=True, stop=True))
    T(f_bc2)

    def f_bcol(v):
        v.wait_ge(pe.h, M["bc2mm"])
        n0 = dv.inc(v.tensor_copy(sb_bc2, psSm[1].ap()[:, 0:1]))
        v.wait_ge(dv.h, n0)
        M["bcol"] = dv.inc(v.tensor_add(sb_bcol, sb_bt, sb_bc2))
    V(f_bcol)

    def f_btr(t):
        t.wait_ge(dv.h, M["bcol"])
        M["btr"] = pe.inc(t.matmul(psSm[2].ap()[0:1, 0:128], sb_bcol, sb_identf,
                                   start=True, stop=True))
    T(f_btr)

    def f_brow(v):
        v.wait_ge(pe.h, M["btr"])
        M["brow"] = dv.inc(v.tensor_copy(sb_brow, psSm[2].ap()[0:1, 0:128]))
    V(f_brow)

    def f_bbc(t):
        t.wait_ge(dv.h, M["brow"])
        M["bbcmm"] = pe.inc(t.matmul(psSm[1].ap()[:, 0:128], sb_onesrf, sb_brow,
                                     start=True, stop=True))
    T(f_bbc)

    def f_bbcev(v):
        v.wait_ge(pe.h, M["bbcmm"])
        M["bbc"] = dv.inc(v.tensor_copy(sb_bbc, psSm[1].ap()[:, 0:128]))
    V(f_bbcev)

    def f_flat(v):
        v.wait_ge(dv.h, M["bbc"])
        v.wait_ge(pe.h, M["posmm"])
        n0 = dv.inc(v.tensor_add(sb_posf, psPos.ap()[:, 0:128], sb_bbc))
        v.wait_ge(dv.h, n0)
        n1 = dv.inc(v.tensor_mul(sb_prod, sb_posf, sb_ohf))
        v.wait_ge(dv.h, n1)
        n2 = dv.inc(v.reduce_sum(sb_psel, sb_prod.rearrange("p (g e) -> p g e", e=8), axis=AX.X))
        v.wait_ge(dv.h, n2)
        dv.inc(v.tensor_scalar(sb_kept, sb_psel, float(CAP), None, ALU.is_lt))
        n3 = dv.inc(v.tensor_scalar(sb_posc, sb_psel, float(CAP - 1), None, ALU.min))
        v.wait_ge(dv.h, n3)
        n4 = dv.inc(v.scalar_tensor_tensor(sb_flat, sb_idxf, float(CAP), sb_posc,
                                           ALU.mult, ALU.add))
        na = dv.inc(v.tensor_scalar(sb_plo, sb_posc, 512.0, None, ALU.is_ge))
        for thr in (1024.0, 1536.0, 2048.0):
            nb = dv.inc(v.tensor_scalar(sb_phi, sb_posc, thr, None, ALU.is_ge))
            v.wait_ge(dv.h, nb)
            na = dv.inc(v.tensor_add(sb_plo, sb_plo, sb_phi))
            v.wait_ge(dv.h, na)
        nc_ = dv.inc(v.scalar_tensor_tensor(sb_phi, sb_plo, 3584.0, sb_posc,
                                            ALU.mult, ALU.add))
        v.wait_ge(dv.h, nc_)
        nd = dv.inc(v.scalar_tensor_tensor(sb_phi, sb_idxf, 512.0, sb_phi,
                                           ALU.mult, ALU.add))
        v.wait_ge(dv.h, max(n4, nd))
        dv.inc(v.tensor_copy(sb_flatu, sb_phi))
        dv.inc(v.tensor_mul(sb_afac, sb_kept, sb_pm))
        dv.inc(v.tensor_scalar(sb_bfac, sb_kept, -1.0, 1.0, ALU.mult, ALU.add))
        dv.inc(v.tensor_scalar(sb_gid1, sb_tokid1, sb_cvals[:, 0:1], None, ALU.add))
        n7 = dv.inc(v.tensor_scalar(sb_scf, sb_flat, 1.0e6, None, ALU.add))
        v.wait_ge(dv.h, n7)
        n8 = dv.inc(v.scalar_tensor_tensor(sb_scf2, sb_kept, -1.0e6, sb_scf,
                                           ALU.mult, ALU.add))
        v.wait_ge(dv.h, n8)
        M["scu"] = dv.inc(v.tensor_copy(sb_scu, sb_scf2))
        M["routing_done"] = dv.n
    V(f_flat)
    ctxF.close()

    def f_scatter(g):
        g.wait_ge(dv.h, M["routing_done"])
        g.wait_ge(dw.h, M["wqkv"])
        for g_ in range(NT):
            dsid.inc(g.indirect_dma_start(
                out=id_disp.ap().rearrange("(a b) -> a b", b=1),
                out_offset=bass.IndirectOffsetOnAxis(sb_scu[:, g_:g_ + 1], 0),
                in_=sb_gid1[:, g_:g_ + 1], in_offset=None,
                bounds_check=E * CAP - 1, oob_is_err=False), 16)
        g.wait_ge(dsid.h, dsid.n)
        c_id.inc(g.collective_compute(
            "AllReduce", ALU.add, replica_groups=[list(range(NCORE))],
            ins=[id_disp.ap().rearrange("(a b) -> a b", b=128).opt()],
            outs=[id_full.ap().rearrange("(a b) -> a b", b=128).opt()]))
        po.inc(g.iota(sb_idx20, pattern=[[0, 1]], base=0, channel_multiplier=128,
                      allow_small_or_imprecise_dtypes=True))
        M["po_iota"] = po.n
    G(f_scatter)

    def f_idx20(v):
        v.wait_ge(po.h, M["po_iota"])
        n0 = dv.inc(v.tensor_scalar(sb_idx20, sb_idx20, sb_cvals[0:20, 1:2], None, ALU.add))
        v.wait_ge(dv.h, n0)
        M["idx20u"] = dv.inc(v.tensor_copy(sb_idx20u, sb_idx20))
    V(f_idx20)

    def f_ids(g):
        g.wait_ge(dv.h, M["idx20u"])
        g.wait_ge(c_id.h, 1)
        M["ids20"] = did.inc(g.indirect_dma_start(
            out=sb_ids20[:, :], out_offset=None,
            in_=id_full.ap().rearrange("(a b) -> a b", b=1),
            in_offset=bass.IndirectOffsetOnAxis(sb_idx20u[:, 0:1], 0)), 16)
    G(f_ids)

    ctxG0 = ExitStack()
    psG0 = ctxG0.enter_context(nc.psum_tensor("psG0", [128, 512], F32))

    def g_idpad(v):
        v.wait_ge(did.h, M["ids20"])
        n0 = dv.inc(v.memset(sb_bpad, 0.0))
        v.wait_ge(dv.h, n0)
        M["idpad"] = dv.inc(v.tensor_copy(sb_bpad[0:20, :], sb_ids20))
    V(g_idpad)

    def g_idtr(t):
        t.wait_ge(dv.h, M["idpad"])
        M["idtr"] = pe.inc(t.transpose(psG0.ap()[:, 0:128], sb_bpad, sb_identf))
    T(g_idtr)

    def g_gidx(v):
        v.wait_ge(pe.h, M["idtr"])
        n0 = dv.inc(v.tensor_scalar(sb_idm1, psG0.ap()[:, 0:20], 1.0, None, ALU.subtract))
        v.wait_ge(dv.h, n0)
        M["gidx"] = dv.inc(v.tensor_copy(sb_gidx, sb_idm1))
    V(g_gidx)
    ctxG0.close()

    if upto <= 4:
        return finish_early()

    ctxG = ExitStack()
    psTr = [ctxG.enter_context(nc.psum_tensor(f"psTrG{i}", [128, 1024], BF16)) for i in range(2)]
    psH = [ctxG.enter_context(nc.psum_tensor(f"psHG{i}", [128, 512], F32)) for i in range(2)]
    psEo = [ctxG.enter_context(nc.psum_tensor(f"psEG{i}", [128, 512], F32)) for i in range(4)]

    NST = CAP // 256

    for st in range(NST):
        for qq in range(2):
            r0 = st * 2 + qq

            def g_gath(s, st=st, qq=qq, r0=r0):
                if st == 0 and qq == 0:
                    s.wait_ge(dv.h, M["gidx"])
                    for cs in c_tc:
                        s.wait_ge(cs.h, 1)
                if ("tgathtr", st - 1, qq) in M:
                    s.wait_ge(pe.h, M[("tgathtr", st - 1, qq)])
                M[("tgath", st, qq)] = dtg2[qq].inc(s.indirect_dma_start(
                    out=gath_sb[qq], out_offset=None,
                    in_=t_full.ap(),
                    in_offset=bass.IndirectOffsetOnAxis(sb_gidx[:, r0:r0 + 1], 0),
                    bounds_check=NCORE * TPC - 1, oob_is_err=False), 16)
            G(g_gath)

            for j in range(8):
                def g_tr(t, st=st, qq=qq, j=j):
                    if j == 0:
                        t.wait_ge(dtg2[qq].h, M[("tgath", st, qq)])
                    prev = ("tgev", st, qq, j - 2) if j >= 2 else \
                           (("tgev", st, qq - 1, 6 + j) if qq > 0 else ("tgev", st - 1, 1, 6 + j))
                    if prev in M:
                        t.wait_ge(dv.h, M[prev])
                    M[("tgtr", st, qq, j)] = pe.inc(t.transpose(
                        psTr[j % 2].ap()[:, 0:128], gath_sb[qq][:, j * 128:(j + 1) * 128],
                        sb_ident))
                T(g_tr)

                def g_trev(v, st=st, qq=qq, j=j):
                    v.wait_ge(pe.h, M[("tgtr", st, qq, j)])
                    M[("tgev", st, qq, j)] = dv.inc(v.tensor_copy(
                        tTe_sb[:, j, qq * 128:(qq + 1) * 128], psTr[j % 2].ap()[:, 0:128]))
                V(g_trev)
            M[("tgathtr", st, qq)] = pe.n
            M[("tTe", st, qq)] = dv.n

        for fj in range(32):
            def g_h(t, st=st, fj=fj):
                if fj == 0:
                    t.wait_ge(dv.h, M[("tTe", st, 1)])
                    if st == 0:
                        t.wait_ge(dw1.h, M["w1full"])
                key = ("hev", st, fj - 2) if fj >= 2 else ("hev", st - 1, 30 + fj)
                if key in M:
                    t.wait_ge(dv.h, M[key])
                last = None
                for kj in range(8):
                    last = t.matmul(psH[fj % 2].ap()[:, 0:256], w1_ap(kj, fj * 128, (fj + 1) * 128),
                                    tTe_sb[:, kj, :], start=(kj == 0), stop=(kj == 7))
                M[("hmm", st, fj)] = pe.inc(last)
            T(g_h)

            def g_hev(v, st=st, fj=fj):
                v.wait_ge(pe.h, M[("hmm", st, fj)])
                if fj == 0:
                    if st == 0:
                        v.wait_ge(pe.h, M[("lgmm2", 15)])
                    if ("w2mm", st - 1) in M:
                        v.wait_ge(pe.h, M[("w2mm", st - 1)])
                M[("hev", st, fj)] = dv.inc(v.tensor_scalar(
                    h_sb[:, fj, :], psH[fj % 2].ap()[:, 0:256], 0.0, None, ALU.max))
            V(g_hev)
        M[("hall", st)] = dv.n

        def g_w2(t, st=st):
            t.wait_ge(dv.h, M[("hall", st)])
            if st == 0:
                t.wait_ge(dw2.h, M["w2full"])
            if ("eoev", st - 1) in M:
                t.wait_ge(dv.h, M[("eoev", st - 1)])
            last = None
            for rs in range(2):
                for hh in range(2):
                    for fj in range(32):
                        last = t.matmul(psEo[rs * 2 + hh].ap(),
                                        h_sb[:, fj, rs * 128:(rs + 1) * 128],
                                        w2_ap(fj)[:, hh * 512:(hh + 1) * 512],
                                        start=(fj == 0), stop=(fj == 31))
            M[("w2mm", st)] = pe.inc(last)
        T(g_w2)

        def g_eoev(v, st=st):
            v.wait_ge(pe.h, M[("w2mm", st)])
            if ("eostore", st - 1) in M:
                v.wait_ge(deo.h, M[("eostore", st - 1)])
            for rs in range(2):
                dv.inc(v.tensor_copy(eo_sb[rs][:, 0:512], psEo[rs * 2].ap()))
                dv.inc(v.tensor_copy(eo_sb[rs][:, 512:1024], psEo[rs * 2 + 1].ap()))
            M[("eoev", st)] = dv.n
        V(g_eoev)

        def g_eost(s, st=st):
            s.wait_ge(dv.h, M[("eoev", st)])
            for rs in range(2):
                deo.inc(s.dma_start(
                    out=eout_loc.ap()[st * 256 + rs * 128: st * 256 + (rs + 1) * 128, :],
                    in_=eo_sb[rs]), 16)
            M[("eostore", st)] = deo.n
        SY(g_eost)

        if st % 2 == 1:
            def eo_ag_chunk(g, st=st):
                k = st // 2
                g.wait_ge(deo.h, M[("eostore", st)])
                c_ec[k].inc(g.collective_compute(
                    "AllGather", ALU.bypass, replica_groups=[list(range(NCORE))],
                    ins=[eout_loc.ap()[k * 512:(k + 1) * 512, :].opt()],
                    outs=[eout_full.ap()[k * 4096:(k + 1) * 4096, :].opt()]))
            G(eo_ag_chunk)
    ctxG.close()

    if upto <= 5:
        return finish_early()

    def emit_eg(g_):
        def h_eg(g, g_=g_):
            if g_ == 0:
                for cs in c_ec:
                    g.wait_ge(cs.h, 1)
            if ("comb1", g_ - 2) in M:
                g.wait_ge(dv.h, M[("comb1", g_ - 2)])
            M[("eg", g_)] = deg2[g_ % 2].inc(g.indirect_dma_start(
                out=eg_sb[g_ % 2], out_offset=None,
                in_=eout_full.ap(),
                in_offset=bass.IndirectOffsetOnAxis(sb_flatu[:, g_:g_ + 1], 0)), 16)
        G(h_eg)

    emit_eg(0)
    for g_ in range(NT):
        if g_ + 1 < NT:
            emit_eg(g_ + 1)

        def h_ld(s, g_=g_):
            if g_ == 0:
                s.wait_ge(deo.h, M[("eostore", NST - 1)])
                s.wait_ge(dv.h, M["E_done_dv"])
                s.wait_ge(dy2[0].h, dy2[0].n)
                s.wait_ge(dy2[1].h, dy2[1].n)
            if ("comb2", g_ - 2) in M:
                s.wait_ge(pcb.h, M[("comb2", g_ - 2)])
            M[("y1l", g_)] = dhl2[g_ % 2].inc(s.dma_start(out=y1_rot[g_ % 2],
                                                 in_=y1_dram.ap()[g_ * 128:(g_ + 1) * 128, :]), 16)
            M[("tl", g_)] = dhl2[g_ % 2].inc(s.dma_start(out=t_rot[g_ % 2],
                                                in_=t_loc.ap()[g_ * 128:(g_ + 1) * 128, :]), 16)
        SY(h_ld)

        def h_comb(v, g_=g_):
            v.wait_ge(deg2[g_ % 2].h, M[("eg", g_)])
            v.wait_ge(dhl2[g_ % 2].h, M[("tl", g_)])
            if g_ < 2:
                v.wait_ge(dv.h, M["E_done_dv"])
            if ("yst", g_ - 2) in M:
                v.wait_ge(osy2[g_ % 2].h, M[("yst", g_ - 2)])
            M[("comb1", g_)] = dv.inc(v.scalar_tensor_tensor(
                x_rot[g_ % 2], eg_sb[g_ % 2], sb_afac[:, g_:g_ + 1], y1_rot[g_ % 2],
                ALU.mult, ALU.add))
        V(h_comb)

        def h_comb2(v, g_=g_):
            v.wait_ge(dv.h, M[("comb1", g_)])
            M[("comb2", g_)] = pcb.inc(v.scalar_tensor_tensor(
                x_rot[g_ % 2], t_rot[g_ % 2], sb_bfac[:, g_:g_ + 1], x_rot[g_ % 2],
                ALU.mult, ALU.add))
        V(h_comb2)

        def h_st(sc, g_=g_):
            sc.wait_ge(pcb.h, M[("comb2", g_)])
            M[("yst", g_)] = osy2[g_ % 2].inc(sc.dma_start(out=y_out[g_ * 128:(g_ + 1) * 128, :],
                                                   in_=x_rot[g_ % 2]), 16)
        SC(h_st)

    def fin(g):
        g.wait_ge(osy2[0].h, osy2[0].n)
        g.wait_ge(osy2[1].h, osy2[1].n)
    G(fin)

    return nc


_NC_CACHE = None


def kernel(L0_attn_x, L0_attn_Wq, L0_attn_Wk, L0_attn_Wv, L0_attn_Wo,
           L0_ffn_Wr, L0_ffn_W1, L0_ffn_W2):
    global _NC_CACHE
    x = np.ascontiguousarray(np.asarray(L0_attn_x, dtype=np.float32).reshape(NCORE * TPC, D))
    wqT = np.ascontiguousarray(np.asarray(L0_attn_Wq, dtype=np.float32).T)
    wkT = np.ascontiguousarray(np.asarray(L0_attn_Wk, dtype=np.float32).T)
    wvT = np.ascontiguousarray(np.asarray(L0_attn_Wv, dtype=np.float32).T)
    woT = np.ascontiguousarray(np.asarray(L0_attn_Wo, dtype=np.float32).T)
    wo2 = np.ascontiguousarray(np.asarray(L0_attn_Wo, dtype=np.float32))
    wr = np.ascontiguousarray(np.asarray(L0_ffn_Wr, dtype=np.float32))
    w1 = np.ascontiguousarray(np.asarray(L0_ffn_W1, dtype=np.float32))
    w2 = np.ascontiguousarray(np.asarray(L0_ffn_W2, dtype=np.float32))

    if _NC_CACHE is None:
        _NC_CACHE = build()
    nc = _NC_CACHE

    in_maps = []
    for c in range(NCORE):
        cmask = (np.arange(8) < c).astype(np.float32).reshape(8, 1)
        cvals = np.zeros((128, 2), np.float32)
        cvals[:, 0] = c * 512
        cvals[:, 1] = c * CAP
        in_maps.append({
            "x": x[c * TPC:(c + 1) * TPC],
            "wqT": wqT, "wkT": wkT, "wvT": wvT, "woT": woT, "wo2": wo2,
            "wr": wr,
            "w1c": np.ascontiguousarray(w1[c]),
            "w2c": np.ascontiguousarray(w2[c]),
            "cmask": cmask, "cvals": cvals,
        })
    res = run_bass_kernel_spmd(nc, in_maps, core_ids=list(range(NCORE)))
    out = np.concatenate([r["y"] for r in res.results], axis=0)
    return out.reshape(2, 8192, 1024)
